# version 22
# baseline (speedup 1.0000x reference)
import numpy as np
import ml_dtypes
from contextlib import ExitStack
import concourse.bass as bass
import concourse.mybir as mybir
from concourse.bass_utils import run_bass_kernel_spmd

F32 = mybir.dt.float32
F32R = mybir.dt.float32r
BF16 = mybir.dt.bfloat16
U8 = mybir.dt.uint8
AF = mybir.ActivationFunctionType
ALU = mybir.AluOpType
AX = mybir.AxisListType
BF = ml_dtypes.bfloat16

NCORES = 8
NSEQ = 2
D = 1024
T = 2048
NCH = 8
NH = 16
HD = 64
QC = 512
NQC = T // QC
NKT = T // 128
VS = NH * (HD + 1)

MODE = "fused"
PHASES = 99
QKV_SUB = 9
Q_SUB = 9
DEBUG = False
SERIALIZE = False
SER_TAGS = set()
SER_ENGS = {"tensor", "vector", "scalar", "gpsimd", "sync"}
EVAC_FULLWAIT = False
NO_AUG = False
NO_VEVAC = False
DMA_QUEUE = "gpsimd"
DRAINS = False

ENG_SEM = {"tensor": "sP", "vector": "sV", "scalar": "sA", "gpsimd": "sO", "sync": "sL"}


class _Mock:
    def __getattr__(self, name):
        return lambda *a, **kw: self
_MOCK = _Mock()


class Bld:

    def __init__(self, nc, block, sems):
        self.nc, self.block, self.sems = nc, block, sems
        self.cnt = {k: 0 for k in sems}
        self.tag = ""
        self.pend = {e: [] for e in ("tensor", "vector", "scalar", "gpsimd", "sync")}

    def inc(self, instr, s, k):
        if instr is _MOCK:
            self.cnt[s] += k
        else:
            instr.then_inc(self.sems[s], k)

    def mark(self, s):
        return (s, self.cnt[s])

    def sec(self, eng, waits, emit, sem=None):
        if eng == "sync2":
            eng = "sync"
            sem = sem or "sS"
        elif eng == "sync":
            eng = DMA_QUEUE
        """One section on `eng`: waits (markers) then emit(handle). emit returns
        the last instruction (gets +1 on the engine sem) or None (emit did its
        own incs, e.g. DMA +16s). Returns a completion marker."""
        s = sem or ENG_SEM[eng]
        me = self
        ser = []
        if SERIALIZE or (self.tag in SER_TAGS and eng in SER_ENGS):
            ser = [(ss, self.cnt[ss]) for ss in ("sL", "sG", "sP", "sV", "sA", "sO")
                   if self.cnt[ss] > 0 and ss != s]
        if emit(_MOCK) is not None:
            self.cnt[s] += 1
        wl = ser + list(waits)

        def body(h):
            seen = {}
            for (ws, wv) in wl:
                if wv > 0 and seen.get(ws, -1) < wv:
                    h.wait_ge(me.sems[ws], wv)
                    seen[ws] = wv
            last = emit(h)
            if last is not None:
                last.then_inc(me.sems[s], 1)
        self.pend[eng].append(body)
        return self.mark(s)

    def flush(self):
        for e in ("sync", "gpsimd", "tensor", "vector", "scalar"):
            fns = self.pend[e]
            if not fns:
                continue
            self.pend[e] = []

            def body(h, fns=fns):
                for f in fns:
                    f(h)
            getattr(self.block, e)(body)

    def barrier(self):
        self.flush()
        tgt = self.cnt["bar"] + 5
        nG, nL, nS = self.cnt["sG"], self.cnt["sL"], self.cnt["sS"]

        def _(h):
            h.wait_ge(self.sems["sG"], nG)
        self.block.gpsimd(_)

        def _(h):
            h.wait_ge(self.sems["sL"], nL)
            h.wait_ge(self.sems["sS"], nS)
        self.block.sync(_)
        for e in ENG_SEM:
            def _(h):
                h.sem_inc(self.sems["bar"], 1)
            getattr(self.block, e)(_)
        self.cnt["bar"] = tgt
        for e in ENG_SEM:
            def _(h, tgt=tgt):
                h.wait_ge(self.sems["bar"], tgt)
            getattr(self.block, e)(_)


def build_program(layers=(0, 1)):
    first, last = (0 in layers), (1 in layers)
    nc = bass.Bass("TRN2", target_bir_lowering=False)
    dm = {}

    def din(name, shape, dt=F32, kind="ExternalInput"):
        dm[name] = nc.dram_tensor(name, shape, dt, kind=kind).ap()

    if first:
        din("x", [NSEQ, D, T])
    if first and last:
        din("xmid", [NSEQ, D, T], kind="Internal")
    elif first:
        din("xmid", [NSEQ, D, T], kind="ExternalOutput")
    else:
        din("xmid", [NSEQ, D, T], kind="ExternalInput")
    if last:
        din("out", [NSEQ, D, T], kind="ExternalOutput")
    din("ybuf", [D, T], BF16, kind="Internal")

    if first:
        din("wqkvT", [D, 3 * D])
        din("bqkv_fm", [128, 16])
        din("bproj_fm", [128, 8])
        din("wprojT", [D, D])
        din("g1", [32, 256])
        din("g2", [64, 32])
        din("fbias_fm", [128, 8])
        din("repD", [32, 4, 128])
        din("repG", [16, 128])
    if last:
        for p in "qkvo":
            din("w%sT" % p, [D, D])
        din("bq_fm", [128, 8]); din("bk_fm", [128, 8]); din("bo_fm", [128, 8])
        din("a1T", [D, 64]); din("avT", [D, 8]); din("aoT", [D, 8])
        din("bTall", [8, 4 * D])
        din("rzrep", [128, 1])
        din("wfT", [D, D]); din("fscale_fm", [128, 8]); din("fbias1_fm", [128, 8])
    din("bvrep", [2, 128, D])
    din("idn", [128, 128])
    din("fvrep", [2, 128, 256], BF16)
    din("fexp", [NKT, 128, D])
    din("masks", [128, 8 * QC], U8)
    din("eps_t", [1, 1])
    din("onescol", [128, 1])
    din("onesrr", [1, 128])
    din("selr", [128, 8 * 128])
    dm["lscr"] = nc.dram_tensor("lscr", [4, 16, QC], F32, kind="Internal").ap()
    if DEBUG:
        din("dbg_xn", [128, NCH * T], BF16, kind="ExternalOutput")
        din("dbg_q", [128, NCH * T], BF16, kind="ExternalOutput")
        din("dbg_k", [128, NCH * T], BF16, kind="ExternalOutput")
        din("dbg_v", [128, NKT * VS], BF16, kind="ExternalOutput")
        din("dbg_lp", [64, QC], kind="ExternalOutput")
        din("dbg_t1", [128, NCH * T], BF16, kind="ExternalOutput")
        din("dbg_yblk", [128, NCH * QC], BF16, kind="ExternalOutput")
        din("dbg_rp", [64, QC], kind="ExternalOutput")
        din("dbg_y", [D, T], BF16, kind="ExternalOutput")
        din("dbg_fv", [128, 256], BF16, kind="ExternalOutput")
        din("dbg_fv1", [128, 256], BF16, kind="ExternalOutput")
        din("dbg_fv2", [128, 256], BF16, kind="ExternalOutput")
        din("dbg_p1", [128, 256], BF16, kind="ExternalOutput")
        din("dbg_p2", [128, 256], BF16, kind="ExternalOutput")
        din("dbg_p3", [128, 256], BF16, kind="ExternalOutput")
        din("dbg_fvdram", [2, 128, 256], BF16, kind="ExternalOutput")

    xsrc = {0: dm.get("x"), 1: dm["xmid"]}
    xdst = {0: dm["xmid"], 1: dm.get("out")}

    es = ExitStack()
    with es:
        def _al(f, *a):
            return es.enter_context(f(*a))
        arenaA = _al(nc.sbuf_tensor, "arenaA", [128, 65536], U8)
        arenaB = _al(nc.sbuf_tensor, "arenaB", [128, 33280], U8)
        arenaC = _al(nc.sbuf_tensor, "arenaC", [128, 32768], U8)
        xn_sb = _al(nc.sbuf_tensor, "xn_sb", [128, NCH * T], BF16)
        if DEBUG:
            fv_sb = _al(nc.sbuf_tensor, "fv_sb", [128, 256], BF16)
        masks_sb = _al(nc.sbuf_tensor, "masks_sb", [128, 8 * QC], U8)
        eps_sb = _al(nc.sbuf_tensor, "eps_sb", [1, 1], F32)
        ones_r = _al(nc.sbuf_tensor, "ones_r", [128, 1], F32)
        st_tmp = _al(nc.sbuf_tensor, "st_tmp", [1, 2 * QC], F32)
        r_r = _al(nc.sbuf_tensor, "r_r", [1, QC], F32R)
        mr_r = _al(nc.sbuf_tensor, "mr_r", [1, QC], F32R)
        ones_rr = _al(nc.sbuf_tensor, "ones_rr", [1, 128], F32R)
        sel_r = _al(nc.sbuf_tensor, "sel_r", [128, 8 * 128], F32R)
        rpAr = _al(nc.sbuf_tensor, "rpAr", [128, QC], F32R)
        ltmp = _al(nc.sbuf_tensor, "ltmp", [1, 2 * QC], F32)
        fbb_sb = _al(nc.sbuf_tensor, "fbb_sb", [128, 2 * D], F32)
        stg = _al(nc.sbuf_tensor, "stg", [128, 2048], U8)
        ystgA = stg[:, 0:2048].bitcast(BF16)[0:64, :]
        ystgB = stg[:, 0:2048].bitcast(BF16)[64:128, :]
        x2stg = stg[:, 0:2048].bitcast(F32)
        bias24 = _al(nc.sbuf_tensor, "bias24", [128, 24], F32)
        fsc_sb = _al(nc.sbuf_tensor, "fsc_sb", [128, 8], F32)
        fbi_sb = _al(nc.sbuf_tensor, "fbi_sb", [128, 8], F32)
        ffb_sb = _al(nc.sbuf_tensor, "ffb_sb", [128, 8], F32)
        rz_sb = _al(nc.sbuf_tensor, "rz_sb", [128, 1], F32)
        u_v = _al(nc.sbuf_tensor, "u_v", [8, T], BF16)
        u_o = u_v
        a1_bf = _al(nc.sbuf_tensor, "a1_bf", [128, NCH * 64], BF16)
        av_bf = _al(nc.sbuf_tensor, "av_bf", [128, NCH * 8], BF16)
        ao_bf = _al(nc.sbuf_tensor, "ao_bf", [128, NCH * 8], BF16)
        bT_cur = _al(nc.sbuf_tensor, "bT_cur", [8, D], BF16)
        scr1 = _al(nc.sbuf_tensor, "scr1", [128, 512], F32)
        scr2 = _al(nc.sbuf_tensor, "scr2", [128, 512], F32)
        scr3 = _al(nc.sbuf_tensor, "scr3", [128, 512], F32)
        ps = _al(nc.psum_tensor, "ps", [128, 4096], F32)
        _sL = _al(nc.semaphore, "sL")
        _sG = _al(nc.semaphore, "sG")
        _sP = _al(nc.semaphore, "sP")
        _sV = _al(nc.semaphore, "sV")
        _sA = _al(nc.semaphore, "sA")
        _sO = _al(nc.semaphore, "sO")
        _sS = _al(nc.semaphore, "sS")
        _bar = _al(nc.semaphore, "bar")
        block = _al(nc.Block)
        b = Bld(nc, block, dict(sL=_sL, sG=_sG, sP=_sP, sV=_sV, sA=_sA, sO=_sO, sS=_sS, bar=_bar))
        bank = [ps[:, i * 512:(i + 1) * 512] for i in range(8)]

        q_sb = arenaA[:, 0:32768].bitcast(BF16)
        k_sb = arenaA[:, 32768:65536].bitcast(BF16)
        t1_sb = arenaA[:, 0:32768].bitcast(BF16)
        sw_bf = arenaA[:, 32768:65536].bitcast(BF16)
        wr_bf = arenaA[:, 32768:36864].bitcast(BF16)
        wi_bf = arenaA[:, 36864:40960].bitcast(BF16)
        win_bf = arenaA[:, 40960:45056].bitcast(BF16)
        v_sb = arenaB[:, 0:2 * NKT * VS].bitcast(BF16)
        x_blk = arenaB[:, 0:16384].bitcast(F32)
        x_blk1 = arenaB[:, 16384:32768].bitcast(F32)
        xsq_c = arenaC[:, 28672:32768].bitcast(F32)
        p1_sb = arenaC[:, 0:32768].bitcast(BF16)
        wbf = arenaC[:, 0:16384].bitcast(BF16)
        wstage = arenaC[:, 16384:20480].bitcast(F32)
        wstage2 = arenaC[:, 28672:32768].bitcast(F32)
        ffA = arenaC[:, 4096:8192].bitcast(F32)
        ffB = arenaC[:, 8192:12288].bitcast(F32)
        ffC = arenaC[:, 12288:16384].bitcast(F32)
        ffC2 = arenaC[:, 0:4096].bitcast(F32)
        yblk = arenaC[:, 20480:28672].bitcast(BF16)
        yblk2 = arenaB[:, 16384:24576].bitcast(BF16)
        u_q = arenaC[:, 20480:24576].bitcast(BF16)
        u_k = arenaC[:, 24576:28672].bitcast(BF16)
        idn_sb = arenaB[:, 29824:30336].bitcast(F32)
        gld = arenaB[:, 24576:25728].bitcast(F32)[0:64, :]
        g1t = arenaB[:, 25728:26240].bitcast(F32)[0:32, :]
        g2t = arenaB[:, 26240:26496].bitcast(F32)[0:16, :]
        repD_sb = arenaB[:, 26496:28544].bitcast(F32)[0:32, :]
        repG_sb = arenaB[:, 28544:29056].bitcast(F32)[0:16, :]
        g1rep = arenaB[:, 29056:29568].bitcast(F32)
        g2rep = arenaB[:, 29568:29824].bitcast(F32)

        def _(sync):
            for (dst, src) in [
                (masks_sb[:], dm["masks"]), (eps_sb[:], dm["eps_t"]),
            ]:
                b.inc(sync.dma_start(out=dst, in_=src), "sL", 16)
            b.inc(sync.dma_start(out=ones_r[:], in_=dm["onescol"]), "sL", 16)
            b.inc(sync.dma_start(out=ones_rr[:], in_=dm["onesrr"].bitcast(F32R)), "sL", 16)
            b.inc(sync.dma_start(out=sel_r[:], in_=dm["selr"].bitcast(F32R)), "sL", 16)
            return None
        b.sec("sync", [], _)

        def _(v):
            v.memset(scr3[:], 1.0)
            return v.tensor_copy(rpAr[:], scr3[:])
        b.sec("vector", [], _)

        def _(sync):
            if first:
                b.inc(sync.dma_start(out=ffb_sb[:], in_=dm["fbias_fm"]), "sL", 16)
            if last:
                b.inc(sync.dma_start(out=rz_sb[:], in_=dm["rzrep"]), "sL", 16)
                b.inc(sync.dma_start(out=fsc_sb[:], in_=dm["fscale_fm"]), "sL", 16)
                b.inc(sync.dma_start(out=fbi_sb[:], in_=dm["fbias1_fm"]), "sL", 16)
                pass
            return None
        b.sec("sync", [], _)
        mSetup = b.mark("sL")

        if last:
            mprev = mSetup
            for (dst, srcn, w_) in (
                (a1_bf, "a1T", 64), (av_bf, "avT", 8), (ao_bf, "aoT", 8),
            ):
                def _(sync, srcn=srcn, w_=w_):
                    for ic in range(NCH):
                        b.inc(sync.dma_start(out=wstage[:, ic * w_:(ic + 1) * w_],
                                             in_=dm[srcn][ic * 128:(ic + 1) * 128, :]), "sL", 16)
                    return None
                b.sec("sync", [mprev], _)
                mw = b.mark("sL")

                def _(v, dst=dst, w_=w_):
                    return v.tensor_copy(dst[:], wstage[:, 0:NCH * w_])
                mprev = b.sec("vector", [mw], _)

        def unfold_start(waits):
            def _(sync):
                b.inc(sync.dma_start(out=gld[0:32, 0:256], in_=dm["g1"]), "sL", 16)
                b.inc(sync.dma_start(out=gld[0:64, 256:288], in_=dm["g2"]), "sL", 16)
                b.inc(sync.dma_start(out=repD_sb[:].rearrange("d (c p) -> d c p", c=4),
                                     in_=dm["repD"]), "sL", 16)
                b.inc(sync.dma_start(out=repG_sb[:], in_=dm["repG"]), "sL", 16)
                b.inc(sync.dma_start(out=idn_sb[:], in_=dm["idn"]), "sL", 16)
                return None
            b.sec("sync", waits, _)
            return {"mld": b.mark("sL"), "mprev": None}

        def unfold_tr(st, mi):
            def _(t, mi=mi):
                for c in range(4):
                    t.transpose(ps[0:32, 3584 + c * 32:3584 + (c + 1) * 32],
                                gld[0:32, mi * 128 + c * 32:mi * 128 + (c + 1) * 32],
                                idn_sb[0:32, 0:32])
                return t.transpose(ps[0:16, 3712:3776],
                                   gld[0:64, 256 + mi * 16:256 + (mi + 1) * 16],
                                   idn_sb[0:64, 0:64])
            mT = b.sec("tensor", [st["mld"]] + ([st["mprev"]] if st["mprev"] else []), _)

            def _(v):
                v.tensor_copy(g1t[0:32, :], ps[0:32, 3584:3712])
                return v.tensor_copy(g2t[0:16, :], ps[0:16, 3712:3776])
            st["mprev"] = b.sec("vector", [mT], _)

        def unfold_cc(st, mi, cc):
            wdst, wneg = ((wr_bf, None) if mi == 0 else (wi_bf, win_bf))

            def _(t, cc=cc):
                i = None
                for c in range(4):
                    t.matmul(ps[:, 3776 + c * 32:3776 + (c + 1) * 32],
                             repD_sb[:, cc * 128:(cc + 1) * 128],
                             g1t[0:32, c * 32:(c + 1) * 32], start=True, stop=True)
                    i = t.matmul(ps[:, 3904 + c * 16:3904 + (c + 1) * 16],
                                 repG_sb[:], g2t[0:16, c * 16:(c + 1) * 16],
                                 start=True, stop=True)
                return i
            mrep = b.sec("tensor", [st["mprev"]], _)

            def _(v, cc=cc, wdst=wdst, wneg=wneg):
                v.tensor_copy(g1rep[:], ps[:, 3776:3904])
                v.tensor_copy(g2rep[:], ps[:, 3904:3968])
                for c in range(4):
                    av = g1rep[:, c * 32:(c + 1) * 32].unsqueeze(2).broadcast_to([128, 32, 16])
                    bv = g2rep[:, c * 16:(c + 1) * 16].unsqueeze(1).broadcast_to([128, 32, 16])
                    dst = (scr1 if c == 0 else scr2)[:, 0:512].rearrange(
                        "p (i j) -> p i j", i=32)
                    v.tensor_tensor(dst, av, bv, ALU.mult)
                    if c > 0:
                        v.tensor_tensor(scr1[:], scr1[:], scr2[:], ALU.add)
                i = v.tensor_copy(wdst[:, cc * 512:(cc + 1) * 512], scr1[:])
                if wneg is not None:
                    i = v.tensor_scalar(wneg[:, cc * 512:(cc + 1) * 512],
                                        scr1[:], -1.0, None, ALU.mult)
                return i
            st["mprev"] = b.sec("vector", [mrep], _)

        if DEBUG:
            def _(v):
                return v.memset(fv_sb[:], 7.0)
            msent = b.sec("vector", [], _)

            def _(g):
                b.inc(g.dma_start(out=dm["dbg_p1"], in_=fv_sb[:]), "sG", 16)
                return None
            b.sec("gpsimd", [msent], _)
        b.barrier()

        dbg_done = [False]
        pre = {}

        if first:
            def _(sync):
                src = dm["x"][0].rearrange("(c p) t -> p c t", p=128)[:, :, 0:QC]
                b.inc(sync.dma_start(
                    out=x_blk[:].rearrange("p (c t) -> p c t", c=NCH), in_=src), "sS", 16)
                return None
            b.sec("sync2", [], _)
            pre["mx0"] = b.mark("sS")

        for seq in range(NSEQ):
            for lyr in layers:
                L1 = (lyr == 1)
                xin = xsrc[lyr]
                xout = xdst[lyr]

                def _(sync):
                    if not L1:
                        b.inc(sync.dma_start(out=bias24[:, 0:16], in_=dm["bqkv_fm"]), "sL", 16)
                        b.inc(sync.dma_start(out=bias24[:, 16:24], in_=dm["bproj_fm"]), "sL", 16)
                    else:
                        b.inc(sync.dma_start(out=bias24[:, 0:8], in_=dm["bq_fm"]), "sL", 16)
                        b.inc(sync.dma_start(out=bias24[:, 8:16], in_=dm["bk_fm"]), "sL", 16)
                        b.inc(sync.dma_start(out=bias24[:, 16:24], in_=dm["bo_fm"]), "sL", 16)
                    return None
                b.sec("sync", [], _)
                mbias = b.mark("sL")

                mbtQ = None
                if L1:
                    def _(sync):
                        b.inc(sync.dma_start(out=wstage[0:8, :],
                                             in_=dm["bTall"][:, 0:D]), "sL", 16)
                        return None
                    b.sec("sync", [], _)
                    mw = b.mark("sL")

                    def _(v):
                        return v.tensor_scalar(bT_cur[0:8, :], wstage[0:8, :],
                                               0.125, None, ALU.mult)
                    mbtQ = b.sec("vector", [mw], _)
                qwsrc = dm["wqkvT"] if not L1 else dm["wqT"]
                qcv = []

                def emit_qchunks(ics):
                    for ic in ics:
                        def _(sync, ic=ic):
                            b.inc(sync.dma_start(
                                out=wbf[:, ic * 1024:(ic + 1) * 1024],
                                in_=qwsrc[ic * 128:(ic + 1) * 128, 0:D]), "sL", 16)
                            return None
                        b.sec("sync", [mbias] if ic == 0 else [], _)
                        qcv.append(b.mark("sL"))

                b.tag = "ln"
                mxn = []
                mu_q = {}
                mu_v = {}
                mu_prev = [None]
                qev = []
                mlast_q = [None]

                def emit_qtb(tb):
                    if L1:
                        for (abf, wA, dsts, mud) in (
                            (a1_bf, 64, (u_q, u_k), mu_q),
                            (av_bf, 8, (u_v, None), mu_v),
                        ):
                            def _(t, abf=abf, wA=wA, tb=tb):
                                i = None
                                for ic in range(NCH):
                                    i = t.matmul(ps[0:wA, 3072:3584],
                                                 abf[:, ic * wA:(ic + 1) * wA],
                                                 xn_sb[:, ic * T + tb * QC: ic * T + (tb + 1) * QC],
                                                 start=(ic == 0), stop=(ic == NCH - 1))
                                return i
                            w = [mxn[tb]] + ([mu_prev[0]] if mu_prev[0] else [])
                            mpe = b.sec("tensor", w, _)

                            def _(v, dsts=dsts, tb=tb):
                                i = v.tensor_copy(dsts[0][0:8, tb * QC:(tb + 1) * QC],
                                                  ps[0:8, 3072:3584])
                                if dsts[1] is not None:
                                    i = v.tensor_copy(dsts[1][0:8, tb * QC:(tb + 1) * QC],
                                                      ps[32:40, 3072:3584])
                                return i
                            mud[tb] = mu_prev[0] = b.sec("vector", [mpe], _)
                    for oc in range(NCH):
                        bk = (tb * NCH + oc) % 2

                        def _(t, tb=tb, oc=oc, bk=bk):
                            i = None
                            for ic in range(NCH):
                                i = t.matmul(bank[bk], wbf[:, ic * 1024 + oc * 128: ic * 1024 + (oc + 1) * 128],
                                             xn_sb[:, ic * T + tb * QC: ic * T + (tb + 1) * QC],
                                             start=(ic == 0), stop=(not L1 and ic == NCH - 1))
                            if L1:
                                i = t.matmul(bank[bk], bT_cur[0:8, oc * 128:(oc + 1) * 128],
                                             u_q[0:8, tb * QC:(tb + 1) * QC], start=False, stop=True)
                            return i
                        w = [qcv[-1], mxn[tb]] + ([qev[-2]] if len(qev) >= 2 else [])
                        if L1:
                            w.append(mu_q[tb])
                            if tb == 0 and oc == 0 and mbtQ is not None:
                                w.append(mbtQ)
                        mpe = b.sec("tensor", w, _)
                        mlast_q[0] = mpe

                        def _(v, tb=tb, oc=oc, bk=bk):
                            return v.tensor_scalar(
                                q_sb[:, oc * T + tb * QC: oc * T + (tb + 1) * QC],
                                bank[bk], bias24[:, oc: oc + 1], None, ALU.add)
                        qev.append(b.sec("vector", [mpe], _))
                mx_used = None
                mxu = {}
                mst_prev = [None]
                mbc_prev = [None]
                mrb_prev = [None]
                for tb in range(NQC):
                    xb = x_blk if tb % 2 == 0 else x_blk1

                    if tb == 0 and "mx0" in pre:
                        mxl = pre.pop("mx0")
                    else:
                        def _(sync, tb=tb, xb=xb):
                            src = xin[seq].rearrange("(c p) t -> p c t", p=128)[
                                :, :, tb * QC:(tb + 1) * QC]
                            b.inc(sync.dma_start(
                                out=xb[:].rearrange("p (c t) -> p c t", c=NCH), in_=src), "sS", 16)
                            return None
                        w = [mxu[tb - 2]] if tb >= 2 else []
                        b.sec("sync2", w, _)
                        mxl = b.mark("sS")

                    if tb == 0:
                        emit_qchunks((0, 1, 2, 3))
                    elif tb == 1:
                        emit_qchunks((4, 5, 6, 7))

                    msq = []
                    mpe_list = []
                    for ic in range(NCH):
                        def _(v, ic=ic, xb=xb):
                            return v.tensor_tensor(
                                xsq_c[:, (ic % 2) * 512:(ic % 2 + 1) * 512],
                                xb[:, ic * 512:(ic + 1) * 512],
                                xb[:, ic * 512:(ic + 1) * 512], ALU.mult)
                        w = [mxl] + ([mpe_list[ic - 2]] if ic >= 2 else [])
                        msq.append(b.sec("vector", w, _))

                        def _(t, ic=ic, tb=tb, xb=xb, mstv=(mst_prev[0] if mst_prev[0] is not None else None)):
                            if ic == 0 and mstv is not None:
                                t.wait_ge(b.sems["sV"], mstv[1])
                            t.matmul(ps[0:1, 0:512], ones_r[:], xb[:, ic * 512:(ic + 1) * 512],
                                     start=(ic == 0), stop=(ic == NCH - 1))
                            return t.matmul(ps[0:1, 512:1024], ones_r[:],
                                            xsq_c[:, (ic % 2) * 512:(ic % 2 + 1) * 512],
                                            start=(ic == 0), stop=(ic == NCH - 1))
                        mpe_list.append(b.sec("tensor", [mxl, msq[ic]], _))
                    mpe_stats = mpe_list[-1]

                    def _(v, tb=tb):
                        ms = st_tmp[0:1, 0:512]
                        v.tensor_scalar(ms, ps[0:1, 0:512], 1.0 / D, None, ALU.mult)
                        v.tensor_scalar(st_tmp[0:1, 512:1024], ps[0:1, 512:1024], 1.0 / D, None, ALU.mult)
                        v.tensor_tensor(scr2[0:1, 0:512], ms, ms, ALU.mult)
                        return v.tensor_tensor(st_tmp[0:1, 512:1024], st_tmp[0:1, 512:1024],
                                               scr2[0:1, 0:512], ALU.subtract)
                    mst = b.sec("vector", [mpe_stats] + ([mbc_prev[0]] if mbc_prev[0] else []), _)
                    mst_prev[0] = mst

                    def _(s, tb=tb):
                        s.activation(scr1[0:1, :], st_tmp[0:1, 512:1024], AF.Ln,
                                     bias=eps_sb[:], scale=1.0)
                        return s.activation(r_r[0:1, :],
                                            scr1[0:1, :], AF.Exp, scale=-0.5)
                    mr_ = b.sec("scalar", [mst] + ([mrb_prev[0]] if mrb_prev[0] else []), _)

                    def _(v, tb=tb):
                        return v.tensor_tensor(mr_r[0:1, :],
                                               st_tmp[0:1, 0:512],
                                               r_r[0:1, :].bitcast(F32), ALU.mult)
                    mmr = b.sec("vector", [mr_], _)
                    mbc_prev[0] = mmr

                    def _(t, tb=tb):
                        t.matmul(ps[:, 1024:1536], ones_rr[:], r_r[0:1, :], start=True, stop=True)
                        return t.matmul(ps[:, 1536:2048], ones_rr[:], mr_r[0:1, :],
                                        start=True, stop=True)
                    mrb = b.sec("tensor", [mmr] + ([mx_used] if mx_used else []), _)
                    mrb_prev[0] = mrb

                    def _(v, tb=tb, xb=xb):
                        i = None
                        for ic in range(NCH):
                            v.tensor_tensor(scr1[:], xb[:, ic * 512:(ic + 1) * 512],
                                            ps[:, 1024:1536], ALU.mult)
                            i = v.tensor_tensor(
                                xn_sb[:, ic * T + tb * QC: ic * T + (tb + 1) * QC],
                                scr1[:], ps[:, 1536:2048], ALU.subtract)
                        return i
                    mx_used = b.sec("vector", [mrb], _)
                    mxu[tb] = mx_used
                    mxn.append(mx_used)
                    if tb >= 1:
                        emit_qtb(tb - 1)

                emit_qtb(NQC - 1)
                mLN = mx_used
                if DEBUG and seq == 0 and lyr == layers[0] and not dbg_done[0]:
                    def _(g):
                        b.inc(g.dma_start(out=dm["dbg_p2"], in_=fv_sb[:]), "sG", 16)
                        return None
                    b.sec("gpsimd", [mLN], _)

                b.tag = "qkv"
                if PHASES < 2:
                    b.barrier()
                    continue
                mQK = qev[-1]
                mlast_mm = mlast_q[0]
                mbt = None
                if L1:
                    def _(sync):
                        b.inc(sync.dma_start(out=wstage[0:8, :],
                                             in_=dm["bTall"][:, D:2 * D]), "sL", 16)
                        return None
                    b.sec("sync", [mQK, qcv[6]], _)
                    mw = b.mark("sL")

                    def _(v):
                        return v.tensor_scalar(bT_cur[0:8, :], wstage[0:8, :],
                                               0.125, None, ALU.mult)
                    mbt = b.sec("vector", [mw, mQK], _)
                wsrc = dm["wqkvT"] if not L1 else dm["wkT"]
                coff = D if not L1 else 0
                mcv = []
                for ic in range(NCH):
                    def _(sync, ic=ic, wsrc=wsrc, coff=coff):
                        b.inc(sync.dma_start(
                            out=wbf[:, ic * 1024:(ic + 1) * 1024],
                            in_=wsrc[ic * 128:(ic + 1) * 128, coff:coff + D]), "sL", 16)
                        return None
                    b.sec("sync", [mlast_mm], _)
                    mcv.append(b.mark("sL"))

                mev = []
                for tb in range(NQC):
                    for oc in range(NCH):
                        bk = (tb * NCH + oc) % 2

                        def _(t, tb=tb, oc=oc, bk=bk):
                            i = None
                            for ic in range(NCH):
                                i = t.matmul(bank[bk], wbf[:, ic * 1024 + oc * 128: ic * 1024 + (oc + 1) * 128],
                                             xn_sb[:, ic * T + tb * QC: ic * T + (tb + 1) * QC],
                                             start=(ic == 0), stop=(not L1 and ic == NCH - 1))
                            if L1:
                                i = t.matmul(bank[bk], bT_cur[0:8, oc * 128:(oc + 1) * 128],
                                             u_k[0:8, tb * QC:(tb + 1) * QC], start=False, stop=True)
                            return i
                        w = [mcv[-1], mxn[tb]] + ([mev[-2]] if len(mev) >= 2 else [])
                        if L1:
                            w.append(mu_q[tb])
                            if tb == 0 and oc == 0 and mbt is not None:
                                w.append(mbt)
                        mpe = b.sec("tensor", w, _)
                        mlast_mm = mpe

                        def _(v, tb=tb, oc=oc, bk=bk):
                            return v.tensor_scalar(
                                k_sb[:, oc * T + tb * QC: oc * T + (tb + 1) * QC],
                                bank[bk], bias24[:, 8 + oc: 9 + oc], None, ALU.add)
                        mev.append(b.sec("vector", [mpe], _))
                mQK = mev[-1]

                mbt = None
                if QKV_SUB < 3:
                    b.barrier()
                    continue
                mcvK = mcv
                if L1:
                    def _(sync):
                        b.inc(sync.dma_start(out=wstage[0:8, :],
                                             in_=dm["bTall"][:, 2 * D:3 * D]), "sL", 16)
                        return None
                    b.sec("sync", [mQK, mcvK[6]], _)
                    mw = b.mark("sL")

                    def _(v):
                        return v.tensor_scalar(bT_cur[0:8, :], wstage[0:8, :], 0.125, None, ALU.mult)
                    mbt = b.sec("vector", [mw, mQK], _)
                mcv = []
                for ic in range(NCH):
                    def _(sync, ic=ic):
                        wsrc = dm["wqkvT"] if not L1 else dm["wvT"]
                        coff = 2 * D if not L1 else 0
                        b.inc(sync.dma_start(
                            out=wbf[:, ic * 1024:(ic + 1) * 1024],
                            in_=wsrc[ic * 128:(ic + 1) * 128, coff:coff + D]), "sL", 16)
                        return None
                    b.sec("sync", [mlast_mm], _)
                    mcv.append(b.mark("sL"))

                if DEBUG and seq == 0 and lyr == layers[0] and not dbg_done[0]:
                    def _(g):
                        b.inc(g.dma_start(out=dm["dbg_p3"], in_=fv_sb[:]), "sG", 16)
                        return None
                    b.sec("gpsimd", [mcv[-1] if mcv else mLN], _)

                def _(sync):
                    b.inc(sync.dma_start(out=scr2[:], in_=dm["bvrep"][lyr, :, 0:512]), "sL", 16)
                    b.inc(sync.dma_start(out=scr3[:], in_=dm["bvrep"][lyr, :, 512:1024]), "sL", 16)
                    return None
                b.sec("sync", [mLN], _)
                mvb = b.mark("sL")
                if DEBUG and seq == 0 and lyr == layers[0] and not dbg_done[0]:
                    def _(g):
                        b.inc(g.dma_start(out=dm["dbg_fv1"], in_=fv_sb[:]), "sG", 16)
                        b.inc(g.dma_start(out=dm["dbg_fvdram"], in_=dm["fvrep"]), "sG", 16)
                        return None
                    b.sec("gpsimd", [mvb], _)

                mev = []
                mfb = {}
                maug = []
                for tt in range(NKT):
                    if L1:
                        def _(sync, tt=tt):
                            b.inc(sync.dma_start(
                                out=fbb_sb[:, (tt % 2) * D:(tt % 2 + 1) * D],
                                in_=dm["fexp"][tt]), "sS", 16)
                            return None
                        w = [maug[tt - 2]] if tt >= 2 else []
                        b.sec("sync2", w, _)
                        mfl = b.mark("sS")
                    for oh in range(2):
                        bk = (tt * 2 + oh) % 2

                        def _(t, tt=tt, oh=oh, bk=bk):
                            i = None
                            for ic in range(NCH):
                                i = t.matmul(bank[bk],
                                             xn_sb[:, ic * T + tt * 128: ic * T + (tt + 1) * 128],
                                             wbf[:, ic * 1024 + oh * 512: ic * 1024 + (oh + 1) * 512],
                                             start=(ic == 0), stop=(not L1 and ic == NCH - 1))
                            if L1:
                                i = t.matmul(bank[bk], u_v[0:8, tt * 128:(tt + 1) * 128],
                                             bT_cur[0:8, oh * 512:(oh + 1) * 512],
                                             start=False, stop=True)
                            return i
                        w = [mcv[-1], mvb, mxn[tt // 4]] + ([mev[-2]] if len(mev) >= 2 else [])
                        if L1:
                            w.append(mu_v[tt // 4])
                            if tt == 0 and oh == 0:
                                w.append(mbt)
                        mpe = b.sec("tensor", w, _)

                        vdst = v_sb[:].rearrange("p (kt h e) -> p kt h e", kt=NKT, h=NH)[
                            :, tt, oh * 8:(oh + 1) * 8, 0:64]
                        psv = bank[bk].rearrange("p (h e) -> p h e", h=8)

                        if not L1:
                            def _(v, vdst=vdst, psv=psv, oh=oh):
                                vb = scr2 if oh == 0 else scr3
                                if NO_VEVAC:
                                    return v.tensor_copy(scr1[:], vb[:])
                                return v.tensor_tensor(
                                    vdst, psv, vb[:].rearrange("p (h e) -> p h e", h=8),
                                    ALU.add)
                            mev.append(b.sec("vector", [mpe], _))
                        else:
                            def _(v, vdst=vdst, psv=psv, oh=oh, tt=tt):
                                vb = scr2 if oh == 0 else scr3
                                if NO_VEVAC:
                                    return v.tensor_copy(scr1[:], vb[:])
                                v.tensor_tensor(scr1[:], bank[(tt * 2 + oh) % 2],
                                                vb[:], ALU.add)
                                return v.tensor_tensor(
                                    vdst, scr1[:].rearrange("p (h e) -> p h e", h=8),
                                    fbb_sb[:, (tt % 2) * D + oh * 512:(tt % 2) * D + (oh + 1) * 512]
                                    .rearrange("p (h e) -> p h e", h=8), ALU.mult)
                            mev.append(b.sec("vector", [mpe, mfl], _))
                    if L1:
                        def _(v, tt=tt):
                            vaug = v_sb[:].rearrange("p (kt h e) -> p kt h e", kt=NKT, h=NH)[
                                :, tt, :, 64:65]
                            srcv = fbb_sb[:, (tt % 2) * D:(tt % 2) * D + 1024].rearrange(
                                "p (h e) -> p h e", h=16)[:, :, 0:1]
                            return v.tensor_copy(vaug, srcv)
                        maug.append(b.sec("vector", [mev[-1], mfl], _))
                        mfb[tt] = maug[-1]

                if QKV_SUB < 4:
                    b.barrier()
                    continue
                if not L1:
                    def _(v):
                        vaug = v_sb[:].rearrange("p (kt h e) -> p kt h e", kt=NKT, h=NH)[:, :, :, 64:65]
                        return v.memset(vaug, 1.0)
                    mV = b.sec("vector", [mev[-1]], _)
                else:
                    mV = maug[-1]
                if DEBUG and seq == 0 and lyr == layers[0] and not dbg_done[0]:
                    def _(g):
                        b.inc(g.dma_start(out=dm["dbg_fv2"], in_=fv_sb[:]), "sG", 16)
                        return None
                    b.sec("gpsimd", [mV], _)

                b.barrier()

                b.tag = "attn"
                if PHASES < 3:
                    b.barrier()
                    continue
                halves = []
                cellct = 0
                for qc in range(NQC):
                    for pc in range(NH // 2):
                        ktmax = NKT if L1 else 4 * qc + 4
                        nhv = (ktmax + 7) // 8
                        for hi in range(nhv):
                            kts = list(range(hi * 8, min(ktmax, hi * 8 + 8)))
                            halves.append((pc, qc, hi, kts, cellct,
                                           hi == nhv - 1, ktmax))
                        cellct += 1

                mexpk = {}
                mqkk = {}
                myev = {}
                mlt = {}
                mlt2 = {}
                mys = {}
                mys2 = {}
                mrec_qc = {}
                g_ = [0]
                mpv_last = [None]

                def emit_qk(hx, half):
                    pc, qc, hi, kts, cc, islast, ktmax = half
                    buf = (hx % 2) * 8192
                    ch = pc
                    for kt in kts:
                        g = g_[0]
                        s_ = g % 3

                        def _(t, kt=kt, ch=ch, qc=qc, s_=s_, first=(kt == kts[0])):
                            if first and DRAINS:
                                t.drain()
                            t.matmul(
                                ps[:, s_ * 1024: s_ * 1024 + 512],
                                k_sb[0:64, ch * T + kt * 128: ch * T + (kt + 1) * 128],
                                q_sb[0:64, ch * T + qc * QC: ch * T + (qc + 1) * QC],
                                start=True, stop=True, tile_position=(0, 0))
                            return t.matmul(
                                ps[:, s_ * 1024 + 512: (s_ + 1) * 1024],
                                k_sb[64:128, ch * T + kt * 128: ch * T + (kt + 1) * 128],
                                q_sb[64:128, ch * T + qc * QC: ch * T + (qc + 1) * QC],
                                start=True, stop=True, tile_position=(64, 0))
                        w = []
                        if hx == 0 and kt == kts[0]:
                            w += [mV, mQK]
                        if g >= 3:
                            w.append(mexpk[g - 3])
                        mqkk[g] = b.sec("tensor", w, _)

                        j = kt - hi * 8

                        def _(s, g=g, s_=s_, buf=buf, j=j):
                            return s.activation(
                                p1_sb[:, buf + j * 1024: buf + (j + 1) * 1024],
                                ps[:, s_ * 1024:(s_ + 1) * 1024],
                                AF.Exp, scale=0.125)
                        mexpk[g] = b.sec("scalar", [mqkk[g]], _)
                        g_[0] += 1

                    mmask = None
                    if not L1:
                        dk = [kt for kt in kts if 4 * qc <= kt < 4 * qc + 4]
                        if dk:
                            def _(v, dk=dk, hi=hi, qc=qc, buf=buf):
                                i = None
                                for kt in dk:
                                    j = kt - hi * 8
                                    sl = p1_sb[:, buf + j * 1024: buf + (j + 1) * 1024]
                                    i = v.tensor_tensor(
                                        sl, sl,
                                        masks_sb[:, (kt - 4 * qc) * 1024:
                                                 (kt - 4 * qc + 1) * 1024], ALU.mult)
                                return i
                            mmask = b.sec("vector", [mexpk[g_[0] - 1]], _)
                    return mmask

                def emit_pv(hx, half, mmask, mlastexp):
                    pc, qc, hi, kts, cc, islast, ktmax = half
                    buf = (hx % 2) * 8192
                    pvm = None
                    for hh in range(2):
                        yb = 6 + hh
                        H = pc * 2 + hh

                        def _(t, hh=hh, yb=yb, H=H, kts=kts, hi=hi, ktmax=ktmax, buf=buf):
                            if hh == 0 and DRAINS:
                                t.drain()
                            i = None
                            for kt in kts:
                                j = kt - hi * 8
                                i = t.matmul(
                                    ps[0:65, yb * 512:(yb + 1) * 512],
                                    v_sb[:, kt * VS + H * 65: kt * VS + (H + 1) * 65],
                                    p1_sb[:, buf + j * 1024 + hh * 512:
                                          buf + j * 1024 + (hh + 1) * 512],
                                    start=(kt == 0), stop=(kt == ktmax - 1),
                                    skip_group_check=True)
                            return i
                        w = [mlastexp]
                        if mmask is not None:
                            w.append(mmask)
                        if hi == 0 and yb in myev:
                            w.append(myev[yb])
                        pvm = b.sec("tensor", w, _)
                    mpv_last[0] = pvm

                    if islast:
                        p_ = cc % 2
                        ystg = ystgA if p_ == 0 else ystgB
                        for hh in range(2):
                            yb = 6 + hh
                            H = pc * 2 + hh

                            def _(v, yb=yb, hh=hh, p_=p_, ystg=ystg):
                                v.tensor_copy(ltmp[0:1, p_ * QC:(p_ + 1) * QC],
                                              ps[64:65, yb * 512:(yb + 1) * 512])
                                return v.tensor_copy(ystg[:, hh * 512:(hh + 1) * 512],
                                                     ps[0:64, yb * 512:(yb + 1) * 512])
                            w = [pvm]
                            if p_ in mlt2:
                                w.append(mlt2[p_])
                            if (p_, hh) in mys2:
                                w.append(mys2[(p_, hh)])
                            myev[yb] = b.sec("vector", w, _)

                            def _(sync, hh=hh, H=H, qc=qc, p_=p_):
                                b.inc(sync.dma_start(out=dm["lscr"][qc, H].unsqueeze(0),
                                                     in_=ltmp[0:1, p_ * QC:(p_ + 1) * QC]), "sL", 16)
                                return None
                            b.sec("sync", [myev[yb]], _)
                            mlt[0] = mlt2[p_] = b.mark("sL")

                            def _(g, hh=hh, H=H, qc=qc, ystg=ystg):
                                b.inc(g.dma_start(
                                    out=dm["ybuf"][H * 64:(H + 1) * 64, qc * QC:(qc + 1) * QC],
                                    in_=ystg[:, hh * 512:(hh + 1) * 512]), "sG", 16)
                                return None
                            b.sec("gpsimd", [myev[yb]], _)
                            mys[hh] = mys2[(p_, hh)] = b.mark("sG")
                    if islast and pc == NH // 2 - 1 and qc < 3:
                        ro = (0, 32, 64)[qc]

                        def _(g, qc=qc):
                            b.inc(g.dma_start(out=scr3[0:16, :],
                                              in_=dm["lscr"][qc]), "sG", 16)
                            return None
                        w = [mlt[0]] + ([mrec_qc[qc - 1]] if qc > 0 else [])
                        b.sec("gpsimd", w, _)
                        mlld = b.mark("sG")

                        def _(v, ro=ro):
                            sl = scr3[0:16, :]
                            v.reciprocal(sl, sl)
                            return v.tensor_copy(rpAr[ro:ro + 16, :], sl)
                        mrec_qc[qc] = b.sec("vector", [mlld], _)

                pend = None
                for hx, half in enumerate(halves):
                    mmask = emit_qk(hx, half)
                    mlastexp = mexpk[g_[0] - 1]
                    if pend is not None:
                        emit_pv(*pend)
                    pend = (hx, half, mmask, mlastexp)
                emit_pv(*pend)
                mcell_pv = mpv_last[0]

                if DEBUG and seq == 0 and lyr == layers[0] and not dbg_done[0]:
                    def _(g):
                        b.inc(g.dma_start(out=dm["dbg_xn"], in_=xn_sb[:]), "sG", 16)
                        b.inc(g.dma_start(out=dm["dbg_q"], in_=q_sb[:]), "sG", 16)
                        b.inc(g.dma_start(out=dm["dbg_k"], in_=k_sb[:]), "sG", 16)
                        b.inc(g.dma_start(out=dm["dbg_v"], in_=v_sb[:]), "sG", 16)
                        b.inc(g.dma_start(out=dm["dbg_fv"], in_=fv_sb[:]), "sG", 16)
                        b.inc(g.dma_start(out=dm["dbg_lp"], in_=dm["lscr"].rearrange("a b t -> (a b) t")), "sG", 16)
                        return None
                    b.sec("gpsimd", [mlt[0], mys[0], mys[1]], _)

                b.barrier()

                ust = None
                if not L1:
                    ust = unfold_start([])
                mbt = None
                if L1:
                    def _(sync):
                        b.inc(sync.dma_start(out=wstage[0:8, :],
                                             in_=dm["bTall"][:, 3 * D:4 * D]), "sL", 16)
                        return None
                    b.sec("sync", [], _)
                    mw = b.mark("sL")

                    def _(v):
                        return v.tensor_scalar(bT_cur[0:8, :], wstage[0:8, :], 0.125, None, ALU.mult)
                    mbt = b.sec("vector", [mw], _)

                def _(sync):
                    b.inc(sync.dma_start(
                        out=yblk[:].rearrange("p (c t) -> p c t", c=NCH),
                        in_=dm["ybuf"].rearrange("(c p) t -> p c t", p=128)[
                            :, :, 0:QC]), "sS", 16)
                    return None
                b.sec("sync2", [], _)
                myld0 = b.mark("sS")

                def _(sync):
                    src = xin[seq].rearrange("(c p) t -> p c t", p=128)[:, :, 0:QC]
                    b.inc(sync.dma_start(
                        out=x_blk[:].rearrange("p (c t) -> p c t", c=NCH), in_=src), "sS", 16)
                    return None
                b.sec("sync2", [], _)
                mxf0 = b.mark("sS")

                mcv = []
                for ic in range(NCH):
                    slot = wstage if ic % 2 == 0 else wstage2

                    def _(sync, ic=ic, slot=slot):
                        wsrc = dm["wprojT"] if not L1 else dm["woT"]
                        b.inc(sync.dma_start(
                            out=slot[:],
                            in_=wsrc[ic * 128:(ic + 1) * 128, :]), "sL", 16)
                        return None
                    w = ([mcv[ic - 2]] if ic >= 2 else []) + ([mbt] if (mbt and ic == 0) else [])
                    b.sec("sync", w, _)
                    mwl = b.mark("sL")

                    def _(s, ic=ic, slot=slot):
                        return s.activation(wbf[:, ic * 1024:(ic + 1) * 1024],
                                            slot[:], AF.Copy)
                    mcv.append(b.sec("scalar", [mwl], _))

                msw_hold = [None]
                sgn = {"mswc": []}

                def emit_sign_chunk(j):
                    slot = wstage if j % 2 == 0 else wstage2

                    def _(sync, j=j, slot=slot):
                        b.inc(sync.dma_start(out=slot[:],
                                             in_=dm["wfT"][j * 128:(j + 1) * 128, :]), "sL", 16)
                        return None
                    w = [mcv[6 + j]] if j < 2 else [sgn["mswc"][j - 2]]
                    b.sec("sync", w, _)
                    mswl = b.mark("sL")

                    def _(v, j=j, slot=slot):
                        dst = sw_bf[:, j * 1024:(j + 1) * 1024]
                        v.tensor_scalar(dst, slot[:], 0.0, 2.0, ALU.is_ge, ALU.mult)
                        return v.tensor_scalar(dst, dst, 1.0, None, ALU.subtract)
                    sgn["mswc"].append(b.sec("vector", [mswl], _))

                mev = []
                myld = None
                muo_prev = [None]
                mpe_tb_last = {}
                for tb in range(NQC):
                    ybt = yblk if tb % 2 == 0 else yblk2
                    if tb == 0:
                        myld = myld0
                    else:
                        def _(sync, tb=tb, ybt=ybt):
                            b.inc(sync.dma_start(
                                out=ybt[:].rearrange("p (c t) -> p c t", c=NCH),
                                in_=dm["ybuf"].rearrange("(c p) t -> p c t", p=128)[
                                    :, :, tb * QC:(tb + 1) * QC]), "sS", 16)
                            return None
                        w = [mpe_tb_last[tb - 2]] if tb >= 2 else []
                        b.sec("sync2", w, _)
                        myld = b.mark("sS")

                    if tb == 3:
                        def _(g):
                            b.inc(g.dma_start(out=scr3[0:16, :],
                                              in_=dm["lscr"][3]), "sG", 16)
                            return None
                        b.sec("gpsimd", [mrn[-1], mrec_qc[2]], _)
                        mlld3 = b.mark("sG")

                        def _(v):
                            sl = scr3[0:16, :]
                            v.reciprocal(sl, sl)
                            return v.tensor_copy(rpAr[64:80, :], sl)
                        mrec_qc[3] = b.sec("vector", [mlld3], _)

                    mrn = []
                    for ic in range(NCH):
                        half = ic % 2

                        def _(t, ic=ic, half=half, tb=tb):
                            sb = (0, 32, 64, 64)[tb]
                            return t.matmul(
                                ps[:, (4 + half) * 512:(5 + half) * 512],
                                sel_r[sb:sb + 32, ic * 128:(ic + 1) * 128],
                                rpAr[sb:sb + 32, :],
                                start=True, stop=True)
                        w = [myld, mrec_qc[tb]] + ([mrn[-2]] if len(mrn) >= 2 else [])
                        mbc = b.sec("tensor", w, _)

                        def _(v, ic=ic, half=half, ybt=ybt):
                            sl = ybt[:, ic * 512:(ic + 1) * 512]
                            return v.scalar_tensor_tensor(
                                sl, sl, 1.0, ps[:, (4 + half) * 512:(5 + half) * 512],
                                ALU.mult, ALU.mult)
                        mrn.append(b.sec("vector", [mbc], _))

                    if DEBUG and seq == 0 and lyr == layers[0] and tb == 1 and not dbg_done[0]:
                        def _(g):
                            b.inc(g.dma_start(out=dm["dbg_yblk"], in_=yblk[:]), "sG", 16)
                            b.inc(g.dma_start(out=dm["dbg_rp"], in_=rpAr[:].bitcast(F32)), "sG", 16)
                            return None
                        b.sec("gpsimd", [mrn[-1]], _)

                    if L1:
                        emit_sign_chunk(2 * tb)
                        emit_sign_chunk(2 * tb + 1)
                    else:
                        if tb < 2:
                            if tb == 0:
                                unfold_tr(ust, 0)
                            unfold_cc(ust, 0, 2 * tb)
                            unfold_cc(ust, 0, 2 * tb + 1)
                            if tb == 1:
                                unfold_tr(ust, 1)
                        else:
                            unfold_cc(ust, 1, 2 * (tb - 2))
                            unfold_cc(ust, 1, 2 * (tb - 2) + 1)
                    muo = None
                    if L1:
                        def _(t, tb=tb, ybt=ybt):
                            i = None
                            for ic in range(NCH):
                                i = t.matmul(ps[0:8, 1024:1536], ao_bf[:, ic * 8:(ic + 1) * 8],
                                             ybt[:, ic * 512:(ic + 1) * 512],
                                             start=(ic == 0), stop=(ic == NCH - 1))
                            return i
                        mpe = b.sec("tensor", [mrn[-1]] + ([muo_prev[0]] if muo_prev[0] else []), _)

                        def _(v, tb=tb):
                            return v.tensor_copy(u_o[0:8, tb * QC:(tb + 1) * QC], ps[0:8, 1024:1536])
                        muo = b.sec("vector", [mpe], _)
                        muo_prev[0] = muo

                    for oc in range(NCH):
                        bk = (tb * NCH + oc) % 2

                        def _(t, tb=tb, oc=oc, bk=bk, ybt=ybt):
                            i = None
                            for ic in range(NCH):
                                i = t.matmul(bank[bk], wbf[:, ic * 1024 + oc * 128: ic * 1024 + (oc + 1) * 128],
                                             ybt[:, ic * 512:(ic + 1) * 512],
                                             start=(ic == 0), stop=(not L1 and ic == NCH - 1))
                            if L1:
                                i = t.matmul(bank[bk], bT_cur[0:8, oc * 128:(oc + 1) * 128],
                                             u_o[0:8, tb * QC:(tb + 1) * QC], start=False, stop=True)
                            return i
                        w = [mrn[-1], mcv[-1]] + ([muo] if muo else []) + ([mev[-2]] if len(mev) >= 2 else [])
                        mpe = b.sec("tensor", w, _)
                        mpe_tb_last[tb] = mpe

                        def _(v, tb=tb, oc=oc, bk=bk):
                            dst = t1_sb[:, oc * T + tb * QC: oc * T + (tb + 1) * QC]
                            if not L1:
                                return v.tensor_scalar(dst, bank[bk],
                                                       bias24[:, 16 + oc: 17 + oc], None, ALU.add)
                            return v.tensor_scalar(dst, bank[bk],
                                                   bias24[:, 16 + oc: 17 + oc], rz_sb[:],
                                                   ALU.add, ALU.mult)
                        mev.append(b.sec("vector", [mpe], _))
                msw_hold[0] = sgn["mswc"][-1] if L1 else ust["mprev"]
                mPROJ = mev[-1]

                if DEBUG and seq == 0 and lyr == layers[0] and not dbg_done[0]:
                    def _(g):
                        b.inc(g.dma_start(out=dm["dbg_t1"], in_=t1_sb[:]), "sG", 16)
                        return None
                    b.sec("gpsimd", [mPROJ], _)
                    def _(g):
                        b.inc(g.dma_start(out=dm["dbg_y"], in_=dm["ybuf"]), "sG", 16)
                        return None
                    b.sec("gpsimd", [mPROJ], _)

                b.barrier()

                b.tag = "ffn"
                if PHASES < 5:
                    b.barrier()
                    continue
                msw = msw_hold[0]
                mst_out = []
                mfin_tb = {}
                mfin_hist = []
                mfin_prev = [None]
                mg_prev = [None]
                mcl_hist = []
                for tb in range(NQC):
                    xb = x_blk if tb % 2 == 0 else x_blk1

                    if tb == 0:
                        mxl = mxf0
                    else:
                        def _(sync, tb=tb, xb=xb):
                            src = xin[seq].rearrange("(c p) t -> p c t", p=128)[
                                :, :, tb * QC:(tb + 1) * QC]
                            b.inc(sync.dma_start(
                                out=xb[:].rearrange("p (c t) -> p c t", c=NCH), in_=src), "sS", 16)
                            return None
                        w = [mfin_tb[tb - 2]] if tb >= 2 else []
                        b.sec("sync2", w, _)
                        mxl = b.mark("sS")

                    for po in range(4):
                        oc0 = 2 * po
                        bks = (po % 2) * 2

                        def _(t, tb=tb, oc0=oc0, bks=bks):
                            i = None
                            for j in range(2):
                                oc = oc0 + j
                                if not L1:
                                    for ic in range(NCH):
                                        if oc < 4:
                                            wt = (wr_bf if ic < 4 else win_bf)
                                            col = (ic % 4) * 512 + oc * 128
                                        else:
                                            wt = (wi_bf if ic < 4 else wr_bf)
                                            col = (ic % 4) * 512 + (oc - 4) * 128
                                        i = t.matmul(bank[bks + j], wt[:, col:col + 128],
                                                     xn_sb[:, ic * T + tb * QC: ic * T + (tb + 1) * QC],
                                                     start=(ic == 0), stop=(ic == NCH - 1))
                                else:
                                    for ic in range(NCH):
                                        i = t.matmul(bank[bks + j],
                                                     sw_bf[:, ic * 1024 + oc * 128: ic * 1024 + (oc + 1) * 128],
                                                     xn_sb[:, ic * T + tb * QC: ic * T + (tb + 1) * QC],
                                                     start=(ic == 0), stop=(ic == NCH - 1))
                            return i
                        w = [msw] + ([mcl_hist[-2]] if len(mcl_hist) >= 2 else [])
                        mpe = b.sec("tensor", w, _)

                        if not L1:
                            fB = ffB if len(mst_out) % 2 == 0 else ffA

                            def _(s, oc0=oc0, bks=bks, fB=fB):
                                s.activation(fB[:, 0:512], bank[bks], AF.Silu,
                                             bias=ffb_sb[:, oc0:oc0 + 1], scale=1.0)
                                return s.activation(fB[:, 512:1024], bank[bks + 1], AF.Silu,
                                                    bias=ffb_sb[:, oc0 + 1:oc0 + 2], scale=1.0)
                            w = [mpe] + ([mfin_hist[-2]] if len(mfin_hist) >= 2 else [])
                            mg = b.sec("scalar", w, _)
                            mcl_hist.append(mg)

                            fC = ffC if len(mst_out) % 2 == 0 else ffC2

                            def _(v, tb=tb, oc0=oc0, fC=fC, xb=xb, fB=fB):
                                t1p = t1_sb[:].rearrange("p (c t) -> p c t", c=NCH)[
                                    :, oc0:oc0 + 2, tb * QC:(tb + 1) * QC]
                                fv3 = fC[:].rearrange("p (c t) -> p c t", c=2)
                                v.tensor_tensor(fv3, fB[:].rearrange("p (c t) -> p c t", c=2),
                                                t1p, ALU.mult)
                                return v.tensor_tensor(fC[:], fC[:],
                                                       xb[:, oc0 * 512:(oc0 + 2) * 512], ALU.add)
                            w = [mg, mxl] + ([mst_out[-2]] if len(mst_out) >= 2 else [])
                            mfin = b.sec("vector", w, _)
                            mfin_hist.append(mfin)
                            mfin_prev[0] = mfin
                        else:
                            def _(v, oc0=oc0, bks=bks):
                                v.tensor_scalar(ffA[:, 0:512], bank[bks], fsc_sb[:, oc0:oc0 + 1],
                                                fbi_sb[:, oc0:oc0 + 1], ALU.mult, ALU.add)
                                v.tensor_scalar(ffA[:, 512:1024], bank[bks + 1],
                                                fsc_sb[:, oc0 + 1:oc0 + 2],
                                                fbi_sb[:, oc0 + 1:oc0 + 2], ALU.mult, ALU.add)
                                return v.tensor_scalar(ffA[:], ffA[:], -100.0, 100.0,
                                                       ALU.max, ALU.min)
                            w = [mpe] + ([mg_prev[0]] if mg_prev[0] else [])
                            mcl = b.sec("vector", w, _)
                            mcl_hist.append(mcl)

                            def _(s):
                                return s.activation(ffB[:], ffA[:], AF.Silu)
                            w = [mcl] + ([mfin_prev[0]] if mfin_prev[0] else [])
                            mg = b.sec("scalar", w, _)
                            mg_prev[0] = mg

                            fC = ffC if len(mst_out) % 2 == 0 else ffC2

                            def _(v, tb=tb, oc0=oc0, fC=fC, xb=xb):
                                fv3 = fC[:].rearrange("p (c t) -> p c t", c=2)
                                xn3 = xn_sb[:].rearrange("p (c t) -> p c t", c=NCH)[
                                    :, oc0:oc0 + 2, tb * QC:(tb + 1) * QC]
                                t1p = t1_sb[:].rearrange("p (c t) -> p c t", c=NCH)[
                                    :, oc0:oc0 + 2, tb * QC:(tb + 1) * QC]
                                v.tensor_tensor(fv3, xn3, t1p, ALU.add)
                                v.tensor_tensor(fC[:], ffB[:], fC[:], ALU.mult)
                                return v.tensor_tensor(fC[:], fC[:],
                                                       xb[:, oc0 * 512:(oc0 + 2) * 512], ALU.add)
                            w = [mg, mxl] + ([mst_out[-2]] if len(mst_out) >= 2 else [])
                            mfin = b.sec("vector", w, _)
                            mfin_prev[0] = mfin

                        def _(g, tb=tb, oc0=oc0, fC=fC):
                            b.inc(g.dma_start(
                                out=xout[seq, oc0 * 128:(oc0 + 2) * 128,
                                         tb * QC:(tb + 1) * QC].rearrange(
                                             "(c p) t -> p c t", p=128),
                                in_=fC[:].rearrange("p (c t) -> p c t", c=2)), "sG", 16)
                            return None
                        b.sec("gpsimd", [mfin], _)
                        mst_out.append(b.mark("sG"))
                    mfin_tb[tb] = mfin

                if not (seq == NSEQ - 1 and lyr == layers[-1]):
                    nseq, nlyr = (seq, layers[-1]) if lyr == layers[0] and len(layers) > 1 else (seq + 1, layers[0])

                    def _(sync, nseq=nseq, nlyr=nlyr):
                        src = xsrc[nlyr][nseq].rearrange("(c p) t -> p c t", p=128)[
                            :, :, 0:QC]
                        b.inc(sync.dma_start(
                            out=x_blk[:].rearrange("p (c t) -> p c t", c=NCH), in_=src), "sS", 16)
                        return None
                    b.sec("sync2", [mfin_tb[2], mst_out[3]], _)
                    pre["mx0"] = b.mark("sS")

                if DEBUG and seq == 0 and lyr == layers[0]:
                    dbg_done[0] = True
                b.barrier()

        def _(g):
            g.wait_ge(b.sems["sG"], b.cnt["sG"])
            return None
        b.sec("gpsimd", [], _)
        b.flush()
        nc._ledger = dict(b.cnt)
        try:
            nc._addrs = {}
            for _nm, _h in [("arenaA", arenaA), ("arenaB", arenaB), ("arenaC", arenaC),
                            ("xn_sb", xn_sb),
                            ("masks_sb", masks_sb), ("eps_sb", eps_sb), ("ones_r", ones_r),
                            ("st_tmp", st_tmp), ("r_r", r_r), ("mr_r", mr_r),
                            ("ones_rr", ones_rr),
                            ("sel_r", sel_r), ("rpAr", rpAr), ("ltmp", ltmp),
                            ("fbb_sb", fbb_sb), ("stg", stg),
                            ("bias24", bias24), ("fsc_sb", fsc_sb), ("fbi_sb", fbi_sb),
                            ("ffb_sb", ffb_sb), ("rz_sb", rz_sb), ("u_v", u_v), ("u_o", u_o),
                            ("a1_bf", a1_bf), ("av_bf", av_bf), ("ao_bf", ao_bf),
                            ("bT_cur", bT_cur), ("scr1", scr1), ("scr2", scr2), ("scr3", scr3)]:
                _m = nc.lookup_mloc(_h)
                nc._addrs[_nm] = (_m.addr, _h.shape, str(_h.dtype))
        except Exception as _e:
            nc._addrs = {"error": str(_e)}

    return nc




def _host_constants():
    slopes = 0.5 ** np.arange(NH, dtype=np.float64)
    p = np.arange(128, dtype=np.float64)
    fvals_ = np.ones((2, 256), dtype=np.float64)
    for kt in range(NKT):
        for h in range(NH):
            fvals_[1, kt * NH + h] = np.exp(slopes[h] * (128.0 * kt - 1920.0))
    fexp = np.ones((NKT, 128, D), dtype=np.float32)
    for kt in range(NKT):
        arg = slopes[None, :, None] * (128.0 * kt + p[:, None, None] - 2047.0)
        fexp[kt] = np.broadcast_to(np.exp(arg).astype(BF).astype(np.float32),
                                   (128, NH, HD)).reshape(128, D)
    masks = np.zeros((128, 8 * QC), dtype=np.float32)
    t = np.arange(QC)
    m1 = np.zeros((128, QC), dtype=np.float32)
    for j in range(4):
        m1[:] = (p[:, None] + 128 * j <= t[None, :])
        masks[:, j * 2 * QC: j * 2 * QC + QC] = m1
        masks[:, j * 2 * QC + QC: (j + 1) * 2 * QC] = m1
    repD = np.zeros((32, 4, 128), dtype=np.float32)
    for cc in range(4):
        for pp in range(128):
            repD[cc * 8 + pp // 16, cc, pp] = 1.0
    repG = np.zeros((16, 128), dtype=np.float32)
    for pp in range(128):
        repG[pp % 16, pp] = 1.0
    idn = np.eye(128, dtype=np.float32)
    sel = np.zeros((128, 8 * 128), dtype=np.float32)
    for ic in range(8):
        for p in range(128):
            for off in (0, 32, 64):
                sel[off + 2 * ic + p // 64, ic * 128 + p] = 1.0
    return {
        "idn": idn,
        "fvrep": np.ascontiguousarray(np.broadcast_to(
            fvals_.astype(BF)[:, None, :], (2, 128, 256))),
        "fexp": fexp,
        "masks": masks.astype(np.uint8), "repD": repD, "repG": repG,
        "eps_t": np.full((1, 1), 1e-5, dtype=np.float32),
        "onescol": np.ones((128, 1), dtype=np.float32),
        "onesrr": np.ones((1, 128), dtype=np.float32),
        "selr": sel,
    }


def _fm(vec):
    v = np.asarray(vec, dtype=np.float32)
    n = v.shape[0] // 128
    return np.ascontiguousarray(v.reshape(n, 128).T)


def _host_inputs(inputs, layers=(0, 1)):
    first, last = (0 in layers), (1 in layers)
    d = dict(_host_constants())
    bvrep = np.zeros((2, 128, D), dtype=np.float32)
    if first:
        Wqkv = inputs["L0_attn_W_qkv"]
        d["wqkvT"] = np.ascontiguousarray(Wqkv.T)
        bq = inputs["L0_attn_b_qkv"]
        d["bqkv_fm"] = np.concatenate([_fm(bq[0:D]), _fm(bq[D:2 * D])], axis=1)
        d["bproj_fm"] = _fm(inputs["L0_attn_b_proj"])
        d["wprojT"] = np.ascontiguousarray(inputs["L0_attn_W_proj"].T)
        g1r = inputs["L0_ffn_G1_R"][0].reshape(32, 128)
        g1i = inputs["L0_ffn_G1_I"][0].reshape(32, 128)
        d["g1"] = np.ascontiguousarray(np.concatenate([g1r, g1i], axis=1))
        g2r = inputs["L0_ffn_G2_R"][:, :, 0, :].reshape(64, 16)
        g2i = inputs["L0_ffn_G2_I"][:, :, 0, :].reshape(64, 16)
        d["g2"] = np.ascontiguousarray(np.concatenate([g2r, g2i], axis=1))
        d["fbias_fm"] = _fm(np.concatenate([inputs["L0_ffn_bias_real"],
                                            inputs["L0_ffn_bias_imag"]]))
        bvrep[0] = np.broadcast_to(bq[2 * D:3 * D], (128, D))
    if last:
        d["wqT"] = np.ascontiguousarray(inputs["L1_attn_Wq"].T)
        d["wkT"] = np.ascontiguousarray(inputs["L1_attn_Wk"].T)
        d["wvT"] = np.ascontiguousarray(inputs["L1_attn_Wv"].T)
        d["woT"] = np.ascontiguousarray(inputs["L1_attn_Wo"].T)
        d["bq_fm"] = _fm(inputs["L1_attn_bq"])
        d["bk_fm"] = _fm(inputs["L1_attn_bk"])
        d["bo_fm"] = _fm(inputs["L1_attn_bo"])
        a1 = np.zeros((D, 64), dtype=np.float32)
        a1[:, 0:8] = inputs["L1_attn_Aq"].T
        a1[:, 32:40] = inputs["L1_attn_Ak"].T
        d["a1T"] = a1
        d["avT"] = np.ascontiguousarray(inputs["L1_attn_Av"].T)
        d["aoT"] = np.ascontiguousarray(inputs["L1_attn_Ao"].T)
        d["bTall"] = np.ascontiguousarray(np.concatenate(
            [inputs["L1_attn_Bq"].T, inputs["L1_attn_Bk"].T,
             inputs["L1_attn_Bv"].T, inputs["L1_attn_Bo"].T], axis=1))
        d["rzrep"] = np.ascontiguousarray(np.broadcast_to(
            inputs["L1_attn_rezero"].reshape(1, 1), (128, 1))).astype(np.float32)
        d["wfT"] = np.ascontiguousarray(inputs["L1_ffn_weights"].T)
        d["fscale_fm"] = _fm(inputs["L1_ffn_scale"])
        d["fbias1_fm"] = _fm(inputs["L1_ffn_bias"])
        bvrep[1] = np.broadcast_to(inputs["L1_attn_bv"], (128, D))
    d["bvrep"] = bvrep
    return d


def kernel(**inputs):
    shared0 = _host_inputs(inputs, (0, 1) if MODE == "fused" else (0,))
    x = np.ascontiguousarray(inputs["L0_attn_x"], dtype=np.float32)

    if MODE == "fused":
        nc = build_program((0, 1))
        in_maps = []
        for c in range(NCORES):
            m = dict(shared0)
            m["x"] = np.ascontiguousarray(
                x[c * NSEQ:(c + 1) * NSEQ].transpose(0, 2, 1))
            in_maps.append(m)
        res = run_bass_kernel_spmd(nc, in_maps, core_ids=list(range(NCORES)))
        outs = [r["out"].transpose(0, 2, 1) for r in res.results]
        return np.ascontiguousarray(np.concatenate(outs, axis=0), dtype=np.float32)

    nc0 = build_program((0,))
    in_maps = []
    for c in range(NCORES):
        m = dict(shared0)
        m["x"] = np.ascontiguousarray(x[c * NSEQ:(c + 1) * NSEQ].transpose(0, 2, 1))
        in_maps.append(m)
    res0 = run_bass_kernel_spmd(nc0, in_maps, core_ids=list(range(NCORES)))
    xmids = [r["xmid"] for r in res0.results]

    shared1 = _host_inputs(inputs, (1,))
    nc1 = build_program((1,))
    in_maps = []
    for c in range(NCORES):
        m = dict(shared1)
        m["xmid"] = np.ascontiguousarray(xmids[c])
        in_maps.append(m)
    res1 = run_bass_kernel_spmd(nc1, in_maps, core_ids=list(range(NCORES)))
    outs = [r["out"].transpose(0, 2, 1) for r in res1.results]
    return np.ascontiguousarray(np.concatenate(outs, axis=0), dtype=np.float32)


# version 24
# speedup vs baseline: 1.1896x; 1.1896x over previous
import numpy as np
import ml_dtypes
from contextlib import ExitStack
import concourse.bass as bass
import concourse.mybir as mybir
from concourse.bass_utils import run_bass_kernel_spmd

F32 = mybir.dt.float32
F32R = mybir.dt.float32r
BF16 = mybir.dt.bfloat16
U8 = mybir.dt.uint8
AF = mybir.ActivationFunctionType
ALU = mybir.AluOpType
AX = mybir.AxisListType
BF = ml_dtypes.bfloat16

NCORES = 8
NSEQ = 2
D = 1024
T = 2048
NCH = 8
NH = 16
HD = 64
QC = 512
NQC = T // QC
NKT = T // 128
VS = NH * (HD + 1)

MODE = "fused"
PHASES = 99
QKV_SUB = 9
Q_SUB = 9
DEBUG = False
SERIALIZE = False
SER_TAGS = set()
SER_ENGS = {"tensor", "vector", "scalar", "gpsimd", "sync"}
EVAC_FULLWAIT = False
NO_AUG = False
NO_VEVAC = False
DMA_QUEUE = "gpsimd"
DRAINS = False

ENG_SEM = {"tensor": "sP", "vector": "sV", "scalar": "sA", "gpsimd": "sO", "sync": "sL"}


class _Mock:
    def __getattr__(self, name):
        return lambda *a, **kw: self
_MOCK = _Mock()


class Bld:

    def __init__(self, nc, block, sems):
        self.nc, self.block, self.sems = nc, block, sems
        self.cnt = {k: 0 for k in sems}
        self.tag = ""
        self.pend = {e: [] for e in ("tensor", "vector", "scalar", "gpsimd", "sync")}

    def inc(self, instr, s, k):
        if instr is _MOCK:
            self.cnt[s] += k
        else:
            instr.then_inc(self.sems[s], k)

    def mark(self, s):
        return (s, self.cnt[s])

    def sec(self, eng, waits, emit, sem=None):
        if eng == "sync2":
            eng = "sync"
            sem = sem or "sS"
        elif eng == "sync":
            eng = DMA_QUEUE
        """One section on `eng`: waits (markers) then emit(handle). emit returns
        the last instruction (gets +1 on the engine sem) or None (emit did its
        own incs, e.g. DMA +16s). Returns a completion marker."""
        s = sem or ENG_SEM[eng]
        me = self
        ser = []
        if SERIALIZE or (self.tag in SER_TAGS and eng in SER_ENGS):
            ser = [(ss, self.cnt[ss]) for ss in ("sL", "sG", "sP", "sV", "sA", "sO")
                   if self.cnt[ss] > 0 and ss != s]
        if emit(_MOCK) is not None:
            self.cnt[s] += 1
        wl = ser + list(waits)

        def body(h):
            seen = {}
            for (ws, wv) in wl:
                if wv > 0 and seen.get(ws, -1) < wv:
                    h.wait_ge(me.sems[ws], wv)
                    seen[ws] = wv
            last = emit(h)
            if last is not None:
                last.then_inc(me.sems[s], 1)
        self.pend[eng].append(body)
        return self.mark(s)

    def flush(self):
        for e in ("sync", "gpsimd", "tensor", "vector", "scalar"):
            fns = self.pend[e]
            if not fns:
                continue
            self.pend[e] = []

            def body(h, fns=fns):
                for f in fns:
                    f(h)
            getattr(self.block, e)(body)

    def barrier(self):
        self.flush()
        tgt = self.cnt["bar"] + 5
        nG, nL, nS = self.cnt["sG"], self.cnt["sL"], self.cnt["sS"]

        def _(h):
            h.wait_ge(self.sems["sG"], nG)
        self.block.gpsimd(_)

        def _(h):
            h.wait_ge(self.sems["sL"], nL)
            h.wait_ge(self.sems["sS"], nS)
        self.block.sync(_)
        for e in ENG_SEM:
            def _(h):
                h.sem_inc(self.sems["bar"], 1)
            getattr(self.block, e)(_)
        self.cnt["bar"] = tgt
        for e in ENG_SEM:
            def _(h, tgt=tgt):
                h.wait_ge(self.sems["bar"], tgt)
            getattr(self.block, e)(_)


def build_program(layers=(0, 1)):
    first, last = (0 in layers), (1 in layers)
    nc = bass.Bass("TRN2", target_bir_lowering=False)
    dm = {}

    def din(name, shape, dt=F32, kind="ExternalInput"):
        dm[name] = nc.dram_tensor(name, shape, dt, kind=kind).ap()

    if first:
        din("x", [NSEQ, D, T])
    if first and last:
        din("xmid", [NSEQ, D, T], kind="Internal")
    elif first:
        din("xmid", [NSEQ, D, T], kind="ExternalOutput")
    else:
        din("xmid", [NSEQ, D, T], kind="ExternalInput")
    if last:
        din("out", [NSEQ, D, T], kind="ExternalOutput")
    din("ybuf", [D, T], BF16, kind="Internal")

    if first:
        din("wqkvT", [D, 3 * D])
        din("bqkv_fm", [128, 16])
        din("bproj_fm", [128, 8])
        din("wprojT", [D, D])
        din("g1", [32, 256])
        din("g2", [64, 32])
        din("fbias_fm", [128, 8])
        din("repD", [32, 4, 128])
        din("repG", [16, 128])
    if last:
        for p in "qkvo":
            din("w%sT" % p, [D, D])
        din("bq_fm", [128, 8]); din("bk_fm", [128, 8]); din("bo_fm", [128, 8])
        din("a1T", [D, 64]); din("avT", [D, 8]); din("aoT", [D, 8])
        din("bTall", [8, 4 * D])
        din("rzrep", [128, 1])
        din("wfT", [D, D]); din("fscale_fm", [128, 8]); din("fbias1_fm", [128, 8])
    din("bvrep", [2, 128, D])
    din("idn", [128, 128])
    din("fvrep", [2, 128, 256], BF16)
    din("fexp", [NKT, 128, D])
    din("masks", [128, 8 * QC], U8)
    din("eps_t", [1, 1])
    din("onescol", [128, 1])
    din("onesrr", [1, 128])
    din("selr", [128, 8 * 128])
    dm["lscr"] = nc.dram_tensor("lscr", [4, 16, QC], F32, kind="Internal").ap()
    if DEBUG:
        din("dbg_xn", [128, NCH * T], BF16, kind="ExternalOutput")
        din("dbg_q", [128, NCH * T], BF16, kind="ExternalOutput")
        din("dbg_k", [128, NCH * T], BF16, kind="ExternalOutput")
        din("dbg_v", [128, NKT * VS], BF16, kind="ExternalOutput")
        din("dbg_lp", [64, QC], kind="ExternalOutput")
        din("dbg_t1", [128, NCH * T], BF16, kind="ExternalOutput")
        din("dbg_yblk", [128, NCH * QC], BF16, kind="ExternalOutput")
        din("dbg_rp", [64, QC], kind="ExternalOutput")
        din("dbg_y", [D, T], BF16, kind="ExternalOutput")
        din("dbg_fv", [128, 256], BF16, kind="ExternalOutput")
        din("dbg_fv1", [128, 256], BF16, kind="ExternalOutput")
        din("dbg_fv2", [128, 256], BF16, kind="ExternalOutput")
        din("dbg_p1", [128, 256], BF16, kind="ExternalOutput")
        din("dbg_p2", [128, 256], BF16, kind="ExternalOutput")
        din("dbg_p3", [128, 256], BF16, kind="ExternalOutput")
        din("dbg_fvdram", [2, 128, 256], BF16, kind="ExternalOutput")

    xsrc = {0: dm.get("x"), 1: dm["xmid"]}
    xdst = {0: dm["xmid"], 1: dm.get("out")}

    es = ExitStack()
    with es:
        def _al(f, *a):
            return es.enter_context(f(*a))
        arenaA = _al(nc.sbuf_tensor, "arenaA", [128, 65536], U8)
        arenaB = _al(nc.sbuf_tensor, "arenaB", [128, 33280], U8)
        arenaC = _al(nc.sbuf_tensor, "arenaC", [128, 32768], U8)
        xn_sb = _al(nc.sbuf_tensor, "xn_sb", [128, NCH * T], BF16)
        if DEBUG:
            fv_sb = _al(nc.sbuf_tensor, "fv_sb", [128, 256], BF16)
        masks_sb = _al(nc.sbuf_tensor, "masks_sb", [128, 8 * QC], U8)
        eps_sb = _al(nc.sbuf_tensor, "eps_sb", [1, 1], F32)
        ones_r = _al(nc.sbuf_tensor, "ones_r", [128, 1], F32)
        st_tmp = _al(nc.sbuf_tensor, "st_tmp", [1, 2 * QC], F32)
        r_r = _al(nc.sbuf_tensor, "r_r", [1, QC], F32R)
        mr_r = _al(nc.sbuf_tensor, "mr_r", [1, QC], F32R)
        ones_rr = _al(nc.sbuf_tensor, "ones_rr", [1, 128], F32R)
        sel_r = _al(nc.sbuf_tensor, "sel_r", [128, 8 * 128], F32R)
        rpAr = _al(nc.sbuf_tensor, "rpAr", [128, QC], F32R)
        ltmp = _al(nc.sbuf_tensor, "ltmp", [1, 2 * QC], F32)
        fbb_sb = _al(nc.sbuf_tensor, "fbb_sb", [128, 2 * D], F32)
        stg = _al(nc.sbuf_tensor, "stg", [128, 2048], U8)
        ystgA = stg[:, 0:2048].bitcast(BF16)[0:64, :]
        ystgB = stg[:, 0:2048].bitcast(BF16)[64:128, :]
        x2stg = stg[:, 0:2048].bitcast(F32)
        bias24 = _al(nc.sbuf_tensor, "bias24", [128, 24], F32)
        fsc_sb = _al(nc.sbuf_tensor, "fsc_sb", [128, 8], F32)
        fbi_sb = _al(nc.sbuf_tensor, "fbi_sb", [128, 8], F32)
        ffb_sb = _al(nc.sbuf_tensor, "ffb_sb", [128, 8], F32)
        rz_sb = _al(nc.sbuf_tensor, "rz_sb", [128, 1], F32)
        u_v = _al(nc.sbuf_tensor, "u_v", [8, T], BF16)
        u_o = u_v
        a1_bf = _al(nc.sbuf_tensor, "a1_bf", [128, NCH * 64], BF16)
        av_bf = _al(nc.sbuf_tensor, "av_bf", [128, NCH * 8], BF16)
        ao_bf = _al(nc.sbuf_tensor, "ao_bf", [128, NCH * 8], BF16)
        bT_cur = _al(nc.sbuf_tensor, "bT_cur", [8, D], BF16)
        scr1 = _al(nc.sbuf_tensor, "scr1", [128, 512], F32)
        scr2 = _al(nc.sbuf_tensor, "scr2", [128, 512], F32)
        scr3 = _al(nc.sbuf_tensor, "scr3", [128, 512], F32)
        ps = _al(nc.psum_tensor, "ps", [128, 4096], F32)
        _sL = _al(nc.semaphore, "sL")
        _sG = _al(nc.semaphore, "sG")
        _sP = _al(nc.semaphore, "sP")
        _sV = _al(nc.semaphore, "sV")
        _sA = _al(nc.semaphore, "sA")
        _sO = _al(nc.semaphore, "sO")
        _sS = _al(nc.semaphore, "sS")
        _bar = _al(nc.semaphore, "bar")
        block = _al(nc.Block)
        b = Bld(nc, block, dict(sL=_sL, sG=_sG, sP=_sP, sV=_sV, sA=_sA, sO=_sO, sS=_sS, bar=_bar))
        bank = [ps[:, i * 512:(i + 1) * 512] for i in range(8)]

        q_sb = arenaA[:, 0:32768].bitcast(BF16)
        k_sb = arenaA[:, 32768:65536].bitcast(BF16)
        t1_sb = arenaA[:, 0:32768].bitcast(BF16)
        sw_bf = arenaA[:, 32768:65536].bitcast(BF16)
        wr_bf = arenaA[:, 32768:36864].bitcast(BF16)
        wi_bf = arenaA[:, 36864:40960].bitcast(BF16)
        win_bf = arenaA[:, 40960:45056].bitcast(BF16)
        v_sb = arenaB[:, 0:2 * NKT * VS].bitcast(BF16)
        x_blk = arenaB[:, 0:16384].bitcast(F32)
        x_blk1 = arenaB[:, 16384:32768].bitcast(F32)
        xsq_c = arenaC[:, 28672:32768].bitcast(F32)
        p1_sb = arenaC[:, 0:32768].bitcast(BF16)
        wbf = arenaC[:, 0:16384].bitcast(BF16)
        wstage = arenaC[:, 16384:20480].bitcast(F32)
        wstage2 = arenaC[:, 28672:32768].bitcast(F32)
        ffA = arenaC[:, 4096:8192].bitcast(F32)
        ffB = arenaC[:, 8192:12288].bitcast(F32)
        ffC = arenaC[:, 12288:16384].bitcast(F32)
        ffC2 = arenaC[:, 0:4096].bitcast(F32)
        yblk = arenaC[:, 20480:28672].bitcast(BF16)
        yblk2 = arenaB[:, 16384:24576].bitcast(BF16)
        u_q = arenaC[:, 20480:24576].bitcast(BF16)
        u_k = arenaC[:, 24576:28672].bitcast(BF16)
        idn_sb = arenaB[:, 29824:30336].bitcast(F32)
        gld = arenaB[:, 24576:25728].bitcast(F32)[0:64, :]
        g1t = arenaB[:, 25728:26240].bitcast(F32)[0:32, :]
        g2t = arenaB[:, 26240:26496].bitcast(F32)[0:16, :]
        repD_sb = arenaB[:, 26496:28544].bitcast(F32)[0:32, :]
        repG_sb = arenaB[:, 28544:29056].bitcast(F32)[0:16, :]
        g1rep = arenaB[:, 29056:29568].bitcast(F32)
        g2rep = arenaB[:, 29568:29824].bitcast(F32)

        def _(sync):
            for (dst, src) in [
                (masks_sb[:], dm["masks"]), (eps_sb[:], dm["eps_t"]),
            ]:
                b.inc(sync.dma_start(out=dst, in_=src), "sL", 16)
            b.inc(sync.dma_start(out=ones_r[:], in_=dm["onescol"]), "sL", 16)
            b.inc(sync.dma_start(out=ones_rr[:], in_=dm["onesrr"].bitcast(F32R)), "sL", 16)
            b.inc(sync.dma_start(out=sel_r[:], in_=dm["selr"].bitcast(F32R)), "sL", 16)
            return None
        b.sec("sync", [], _)

        def _(v):
            v.memset(scr3[:], 1.0)
            return v.tensor_copy(rpAr[:], scr3[:])
        b.sec("vector", [], _)

        def _(sync):
            if first:
                b.inc(sync.dma_start(out=ffb_sb[:], in_=dm["fbias_fm"]), "sL", 16)
            if last:
                b.inc(sync.dma_start(out=rz_sb[:], in_=dm["rzrep"]), "sL", 16)
                b.inc(sync.dma_start(out=fsc_sb[:], in_=dm["fscale_fm"]), "sL", 16)
                b.inc(sync.dma_start(out=fbi_sb[:], in_=dm["fbias1_fm"]), "sL", 16)
                pass
            return None
        b.sec("sync", [], _)
        mSetup = b.mark("sL")

        if last:
            mprev = mSetup
            for (dst, srcn, w_) in (
                (a1_bf, "a1T", 64), (av_bf, "avT", 8), (ao_bf, "aoT", 8),
            ):
                def _(sync, srcn=srcn, w_=w_):
                    for ic in range(NCH):
                        b.inc(sync.dma_start(out=wstage[:, ic * w_:(ic + 1) * w_],
                                             in_=dm[srcn][ic * 128:(ic + 1) * 128, :]), "sL", 16)
                    return None
                b.sec("sync", [mprev], _)
                mw = b.mark("sL")

                def _(v, dst=dst, w_=w_):
                    return v.tensor_copy(dst[:], wstage[:, 0:NCH * w_])
                mprev = b.sec("vector", [mw], _)

        def unfold_start(waits):
            def _(sync):
                b.inc(sync.dma_start(out=gld[0:32, 0:256], in_=dm["g1"]), "sL", 16)
                b.inc(sync.dma_start(out=gld[0:64, 256:288], in_=dm["g2"]), "sL", 16)
                b.inc(sync.dma_start(out=repD_sb[:].rearrange("d (c p) -> d c p", c=4),
                                     in_=dm["repD"]), "sL", 16)
                b.inc(sync.dma_start(out=repG_sb[:], in_=dm["repG"]), "sL", 16)
                b.inc(sync.dma_start(out=idn_sb[:], in_=dm["idn"]), "sL", 16)
                return None
            b.sec("sync", waits, _)
            return {"mld": b.mark("sL"), "mprev": None}

        def unfold_tr(st, mi):
            def _(t, mi=mi):
                for c in range(4):
                    t.transpose(ps[0:32, 3584 + c * 32:3584 + (c + 1) * 32],
                                gld[0:32, mi * 128 + c * 32:mi * 128 + (c + 1) * 32],
                                idn_sb[0:32, 0:32])
                return t.transpose(ps[0:16, 3712:3776],
                                   gld[0:64, 256 + mi * 16:256 + (mi + 1) * 16],
                                   idn_sb[0:64, 0:64])
            mT = b.sec("tensor", [st["mld"]] + ([st["mprev"]] if st["mprev"] else []), _)

            def _(v):
                v.tensor_copy(g1t[0:32, :], ps[0:32, 3584:3712])
                return v.tensor_copy(g2t[0:16, :], ps[0:16, 3712:3776])
            st["mprev"] = b.sec("vector", [mT], _)

        def unfold_cc(st, mi, cc):
            wdst, wneg = ((wr_bf, None) if mi == 0 else (wi_bf, win_bf))

            def _(t, cc=cc):
                i = None
                for c in range(4):
                    t.matmul(ps[:, 3776 + c * 32:3776 + (c + 1) * 32],
                             repD_sb[:, cc * 128:(cc + 1) * 128],
                             g1t[0:32, c * 32:(c + 1) * 32], start=True, stop=True)
                    i = t.matmul(ps[:, 3904 + c * 16:3904 + (c + 1) * 16],
                                 repG_sb[:], g2t[0:16, c * 16:(c + 1) * 16],
                                 start=True, stop=True)
                return i
            mrep = b.sec("tensor", [st["mprev"]], _)

            def _(v, cc=cc, wdst=wdst, wneg=wneg):
                v.tensor_copy(g1rep[:], ps[:, 3776:3904])
                v.tensor_copy(g2rep[:], ps[:, 3904:3968])
                for c in range(4):
                    av = g1rep[:, c * 32:(c + 1) * 32].unsqueeze(2).broadcast_to([128, 32, 16])
                    bv = g2rep[:, c * 16:(c + 1) * 16].unsqueeze(1).broadcast_to([128, 32, 16])
                    dst = (scr1 if c == 0 else scr2)[:, 0:512].rearrange(
                        "p (i j) -> p i j", i=32)
                    v.tensor_tensor(dst, av, bv, ALU.mult)
                    if c > 0:
                        v.tensor_tensor(scr1[:], scr1[:], scr2[:], ALU.add)
                i = v.tensor_copy(wdst[:, cc * 512:(cc + 1) * 512], scr1[:])
                if wneg is not None:
                    i = v.tensor_scalar(wneg[:, cc * 512:(cc + 1) * 512],
                                        scr1[:], -1.0, None, ALU.mult)
                return i
            st["mprev"] = b.sec("vector", [mrep], _)

        if DEBUG:
            def _(v):
                return v.memset(fv_sb[:], 7.0)
            msent = b.sec("vector", [], _)

            def _(g):
                b.inc(g.dma_start(out=dm["dbg_p1"], in_=fv_sb[:]), "sG", 16)
                return None
            b.sec("gpsimd", [msent], _)
        b.barrier()

        dbg_done = [False]
        pre = {}

        if first:
            def _(sync):
                src = dm["x"][0].rearrange("(c p) t -> p c t", p=128)[:, :, 0:QC]
                b.inc(sync.dma_start(
                    out=x_blk[:].rearrange("p (c t) -> p c t", c=NCH), in_=src), "sS", 16)
                return None
            b.sec("sync2", [], _)
            pre["mx0"] = b.mark("sS")

        for seq in range(NSEQ):
            for lyr in layers:
                L1 = (lyr == 1)
                xin = xsrc[lyr]
                xout = xdst[lyr]

                def _(sync):
                    if not L1:
                        b.inc(sync.dma_start(out=bias24[:, 0:16], in_=dm["bqkv_fm"]), "sL", 16)
                        b.inc(sync.dma_start(out=bias24[:, 16:24], in_=dm["bproj_fm"]), "sL", 16)
                    else:
                        b.inc(sync.dma_start(out=bias24[:, 0:8], in_=dm["bq_fm"]), "sL", 16)
                        b.inc(sync.dma_start(out=bias24[:, 8:16], in_=dm["bk_fm"]), "sL", 16)
                        b.inc(sync.dma_start(out=bias24[:, 16:24], in_=dm["bo_fm"]), "sL", 16)
                    return None
                b.sec("sync", [], _)
                mbias = b.mark("sL")

                mbtQ = None
                if L1:
                    def _(sync):
                        b.inc(sync.dma_start(out=wstage[0:8, :],
                                             in_=dm["bTall"][:, 0:D]), "sL", 16)
                        return None
                    b.sec("sync", [], _)
                    mw = b.mark("sL")

                    def _(v):
                        return v.tensor_scalar(bT_cur[0:8, :], wstage[0:8, :],
                                               0.125, None, ALU.mult)
                    mbtQ = b.sec("vector", [mw], _)
                qwsrc = dm["wqkvT"] if not L1 else dm["wqT"]
                qcv = []

                def emit_qchunks(ics):
                    for ic in ics:
                        def _(sync, ic=ic):
                            b.inc(sync.dma_start(
                                out=wbf[:, ic * 1024:(ic + 1) * 1024],
                                in_=qwsrc[ic * 128:(ic + 1) * 128, 0:D]), "sL", 16)
                            return None
                        b.sec("sync", [mbias] if ic == 0 else [], _)
                        qcv.append(b.mark("sL"))

                b.tag = "ln"
                mxn = []
                mu_q = {}
                mu_v = {}
                mu_prev = [None]
                qev = []
                mlast_q = [None]

                def emit_qtb(tb):
                    if L1:
                        for (abf, wA, dsts, mud) in (
                            (a1_bf, 64, (u_q, u_k), mu_q),
                            (av_bf, 8, (u_v, None), mu_v),
                        ):
                            def _(t, abf=abf, wA=wA, tb=tb):
                                i = None
                                for ic in range(NCH):
                                    i = t.matmul(ps[0:wA, 3072:3584],
                                                 abf[:, ic * wA:(ic + 1) * wA],
                                                 xn_sb[:, ic * T + tb * QC: ic * T + (tb + 1) * QC],
                                                 start=(ic == 0), stop=(ic == NCH - 1))
                                return i
                            w = [mxn[tb]] + ([mu_prev[0]] if mu_prev[0] else [])
                            mpe = b.sec("tensor", w, _)

                            def _(v, dsts=dsts, tb=tb):
                                i = v.tensor_copy(dsts[0][0:8, tb * QC:(tb + 1) * QC],
                                                  ps[0:8, 3072:3584])
                                if dsts[1] is not None:
                                    i = v.tensor_copy(dsts[1][0:8, tb * QC:(tb + 1) * QC],
                                                      ps[32:40, 3072:3584])
                                return i
                            mud[tb] = mu_prev[0] = b.sec("vector", [mpe], _)
                    for oc in range(NCH):
                        bk = (tb * NCH + oc) % 2

                        def _(t, tb=tb, oc=oc, bk=bk):
                            i = None
                            for ic in range(NCH):
                                i = t.matmul(bank[bk], wbf[:, ic * 1024 + oc * 128: ic * 1024 + (oc + 1) * 128],
                                             xn_sb[:, ic * T + tb * QC: ic * T + (tb + 1) * QC],
                                             start=(ic == 0), stop=(not L1 and ic == NCH - 1))
                            if L1:
                                i = t.matmul(bank[bk], bT_cur[0:8, oc * 128:(oc + 1) * 128],
                                             u_q[0:8, tb * QC:(tb + 1) * QC], start=False, stop=True)
                            return i
                        w = [qcv[-1], mxn[tb]] + ([qev[-2]] if len(qev) >= 2 else [])
                        if L1:
                            w.append(mu_q[tb])
                            if tb == 0 and oc == 0 and mbtQ is not None:
                                w.append(mbtQ)
                        mpe = b.sec("tensor", w, _)
                        mlast_q[0] = mpe

                        def _(v, tb=tb, oc=oc, bk=bk):
                            return v.tensor_scalar(
                                q_sb[:, oc * T + tb * QC: oc * T + (tb + 1) * QC],
                                bank[bk], bias24[:, oc: oc + 1], None, ALU.add)
                        qev.append(b.sec("vector", [mpe], _))
                mx_used = None
                mxu = {}
                mst_prev = [None]
                mbc_prev = [None]
                mrb_prev = [None]
                for tb in range(NQC):
                    xb = x_blk if tb % 2 == 0 else x_blk1

                    if tb == 0 and "mx0" in pre:
                        mxl = pre.pop("mx0")
                    else:
                        def _(sync, tb=tb, xb=xb):
                            src = xin[seq].rearrange("(c p) t -> p c t", p=128)[
                                :, :, tb * QC:(tb + 1) * QC]
                            b.inc(sync.dma_start(
                                out=xb[:].rearrange("p (c t) -> p c t", c=NCH), in_=src), "sS", 16)
                            return None
                        w = [mxu[tb - 2]] if tb >= 2 else []
                        b.sec("sync2", w, _)
                        mxl = b.mark("sS")

                    if tb == 0:
                        emit_qchunks((0, 1, 2, 3))
                    elif tb == 1:
                        emit_qchunks((4, 5, 6, 7))

                    msq = []
                    mpe_list = []
                    for ic in range(NCH):
                        def _(s, ic=ic, xb=xb):
                            return s.activation(
                                xsq_c[:, (ic % 2) * 512:(ic % 2 + 1) * 512],
                                xb[:, ic * 512:(ic + 1) * 512], AF.Square)
                        w = [mxl] + ([mpe_list[ic - 2]] if ic >= 2 else [])
                        msq.append(b.sec("scalar", w, _))

                        def _(t, ic=ic, tb=tb, xb=xb, mstv=(mst_prev[0] if mst_prev[0] is not None else None)):
                            if ic == 0 and mstv is not None:
                                t.wait_ge(b.sems["sV"], mstv[1])
                            t.matmul(ps[0:1, 0:512], ones_r[:], xb[:, ic * 512:(ic + 1) * 512],
                                     start=(ic == 0), stop=(ic == NCH - 1))
                            return t.matmul(ps[0:1, 512:1024], ones_r[:],
                                            xsq_c[:, (ic % 2) * 512:(ic % 2 + 1) * 512],
                                            start=(ic == 0), stop=(ic == NCH - 1))
                        mpe_list.append(b.sec("tensor", [mxl, msq[ic]], _))
                    mpe_stats = mpe_list[-1]

                    def _(v, tb=tb):
                        ms = st_tmp[0:1, 0:512]
                        v.tensor_scalar(ms, ps[0:1, 0:512], 1.0 / D, None, ALU.mult)
                        v.tensor_scalar(st_tmp[0:1, 512:1024], ps[0:1, 512:1024], 1.0 / D, None, ALU.mult)
                        v.tensor_tensor(scr2[0:1, 0:512], ms, ms, ALU.mult)
                        return v.tensor_tensor(st_tmp[0:1, 512:1024], st_tmp[0:1, 512:1024],
                                               scr2[0:1, 0:512], ALU.subtract)
                    mst = b.sec("vector", [mpe_stats] + ([mbc_prev[0]] if mbc_prev[0] else []), _)
                    mst_prev[0] = mst

                    def _(s, tb=tb):
                        s.activation(scr1[0:1, :], st_tmp[0:1, 512:1024], AF.Ln,
                                     bias=eps_sb[:], scale=1.0)
                        return s.activation(r_r[0:1, :],
                                            scr1[0:1, :], AF.Exp, scale=-0.5)
                    mr_ = b.sec("scalar", [mst] + ([mrb_prev[0]] if mrb_prev[0] else []), _)

                    def _(v, tb=tb):
                        return v.tensor_tensor(mr_r[0:1, :],
                                               st_tmp[0:1, 0:512],
                                               r_r[0:1, :].bitcast(F32), ALU.mult)
                    mmr = b.sec("vector", [mr_], _)
                    mbc_prev[0] = mmr

                    def _(t, tb=tb):
                        t.matmul(ps[:, 1024:1536], ones_rr[:], r_r[0:1, :], start=True, stop=True)
                        return t.matmul(ps[:, 1536:2048], ones_rr[:], mr_r[0:1, :],
                                        start=True, stop=True)
                    mrb = b.sec("tensor", [mmr] + ([mx_used] if mx_used else []), _)
                    mrb_prev[0] = mrb

                    def _(v, tb=tb, xb=xb):
                        i = None
                        for ic in range(NCH):
                            v.tensor_tensor(scr1[:], xb[:, ic * 512:(ic + 1) * 512],
                                            ps[:, 1024:1536], ALU.mult)
                            i = v.tensor_tensor(
                                xn_sb[:, ic * T + tb * QC: ic * T + (tb + 1) * QC],
                                scr1[:], ps[:, 1536:2048], ALU.subtract)
                        return i
                    mx_used = b.sec("vector", [mrb], _)
                    mxu[tb] = mx_used
                    mxn.append(mx_used)
                    if tb >= 1:
                        emit_qtb(tb - 1)

                emit_qtb(NQC - 1)
                mLN = mx_used
                if DEBUG and seq == 0 and lyr == layers[0] and not dbg_done[0]:
                    def _(g):
                        b.inc(g.dma_start(out=dm["dbg_p2"], in_=fv_sb[:]), "sG", 16)
                        return None
                    b.sec("gpsimd", [mLN], _)

                b.tag = "qkv"
                if PHASES < 2:
                    b.barrier()
                    continue
                mQK = qev[-1]
                mlast_mm = mlast_q[0]
                mbt = None
                if L1:
                    def _(sync):
                        b.inc(sync.dma_start(out=wstage[0:8, :],
                                             in_=dm["bTall"][:, D:2 * D]), "sL", 16)
                        return None
                    b.sec("sync", [mQK, qcv[6]], _)
                    mw = b.mark("sL")

                    def _(v):
                        return v.tensor_scalar(bT_cur[0:8, :], wstage[0:8, :],
                                               0.125, None, ALU.mult)
                    mbt = b.sec("vector", [mw, mQK], _)
                wsrc = dm["wqkvT"] if not L1 else dm["wkT"]
                coff = D if not L1 else 0
                mcv = []
                for ic in range(NCH):
                    def _(sync, ic=ic, wsrc=wsrc, coff=coff):
                        b.inc(sync.dma_start(
                            out=wbf[:, ic * 1024:(ic + 1) * 1024],
                            in_=wsrc[ic * 128:(ic + 1) * 128, coff:coff + D]), "sL", 16)
                        return None
                    b.sec("sync", [mlast_mm], _)
                    mcv.append(b.mark("sL"))

                mev = []
                for tb in range(NQC):
                    for oc in range(NCH):
                        bk = (tb * NCH + oc) % 2

                        def _(t, tb=tb, oc=oc, bk=bk):
                            i = None
                            for ic in range(NCH):
                                i = t.matmul(bank[bk], wbf[:, ic * 1024 + oc * 128: ic * 1024 + (oc + 1) * 128],
                                             xn_sb[:, ic * T + tb * QC: ic * T + (tb + 1) * QC],
                                             start=(ic == 0), stop=(not L1 and ic == NCH - 1))
                            if L1:
                                i = t.matmul(bank[bk], bT_cur[0:8, oc * 128:(oc + 1) * 128],
                                             u_k[0:8, tb * QC:(tb + 1) * QC], start=False, stop=True)
                            return i
                        w = [mcv[-1], mxn[tb]] + ([mev[-2]] if len(mev) >= 2 else [])
                        if L1:
                            w.append(mu_q[tb])
                            if tb == 0 and oc == 0 and mbt is not None:
                                w.append(mbt)
                        mpe = b.sec("tensor", w, _)
                        mlast_mm = mpe

                        def _(v, tb=tb, oc=oc, bk=bk):
                            return v.tensor_scalar(
                                k_sb[:, oc * T + tb * QC: oc * T + (tb + 1) * QC],
                                bank[bk], bias24[:, 8 + oc: 9 + oc], None, ALU.add)
                        mev.append(b.sec("vector", [mpe], _))
                mQK = mev[-1]

                mbt = None
                if QKV_SUB < 3:
                    b.barrier()
                    continue
                mcvK = mcv
                if L1:
                    def _(sync):
                        b.inc(sync.dma_start(out=wstage[0:8, :],
                                             in_=dm["bTall"][:, 2 * D:3 * D]), "sL", 16)
                        return None
                    b.sec("sync", [mQK, mcvK[6]], _)
                    mw = b.mark("sL")

                    def _(v):
                        return v.tensor_scalar(bT_cur[0:8, :], wstage[0:8, :], 0.125, None, ALU.mult)
                    mbt = b.sec("vector", [mw, mQK], _)
                mcv = []
                for ic in range(NCH):
                    def _(sync, ic=ic):
                        wsrc = dm["wqkvT"] if not L1 else dm["wvT"]
                        coff = 2 * D if not L1 else 0
                        b.inc(sync.dma_start(
                            out=wbf[:, ic * 1024:(ic + 1) * 1024],
                            in_=wsrc[ic * 128:(ic + 1) * 128, coff:coff + D]), "sL", 16)
                        return None
                    b.sec("sync", [mlast_mm], _)
                    mcv.append(b.mark("sL"))

                if DEBUG and seq == 0 and lyr == layers[0] and not dbg_done[0]:
                    def _(g):
                        b.inc(g.dma_start(out=dm["dbg_p3"], in_=fv_sb[:]), "sG", 16)
                        return None
                    b.sec("gpsimd", [mcv[-1] if mcv else mLN], _)

                def _(sync):
                    b.inc(sync.dma_start(out=scr2[:], in_=dm["bvrep"][lyr, :, 0:512]), "sL", 16)
                    b.inc(sync.dma_start(out=scr3[:], in_=dm["bvrep"][lyr, :, 512:1024]), "sL", 16)
                    return None
                b.sec("sync", [mLN], _)
                mvb = b.mark("sL")
                if DEBUG and seq == 0 and lyr == layers[0] and not dbg_done[0]:
                    def _(g):
                        b.inc(g.dma_start(out=dm["dbg_fv1"], in_=fv_sb[:]), "sG", 16)
                        b.inc(g.dma_start(out=dm["dbg_fvdram"], in_=dm["fvrep"]), "sG", 16)
                        return None
                    b.sec("gpsimd", [mvb], _)

                mev = []
                mfb = {}
                maug = []
                for tt in range(NKT):
                    if L1:
                        def _(sync, tt=tt):
                            b.inc(sync.dma_start(
                                out=fbb_sb[:, (tt % 2) * D:(tt % 2 + 1) * D],
                                in_=dm["fexp"][tt]), "sS", 16)
                            return None
                        w = [maug[tt - 2]] if tt >= 2 else []
                        b.sec("sync2", w, _)
                        mfl = b.mark("sS")
                    for oh in range(2):
                        bk = (tt * 2 + oh) % 2

                        def _(t, tt=tt, oh=oh, bk=bk):
                            i = None
                            for ic in range(NCH):
                                i = t.matmul(bank[bk],
                                             xn_sb[:, ic * T + tt * 128: ic * T + (tt + 1) * 128],
                                             wbf[:, ic * 1024 + oh * 512: ic * 1024 + (oh + 1) * 512],
                                             start=(ic == 0), stop=(not L1 and ic == NCH - 1))
                            if L1:
                                i = t.matmul(bank[bk], u_v[0:8, tt * 128:(tt + 1) * 128],
                                             bT_cur[0:8, oh * 512:(oh + 1) * 512],
                                             start=False, stop=True)
                            return i
                        w = [mcv[-1], mvb, mxn[tt // 4]] + ([mev[-2]] if len(mev) >= 2 else [])
                        if L1:
                            w.append(mu_v[tt // 4])
                            if tt == 0 and oh == 0:
                                w.append(mbt)
                        mpe = b.sec("tensor", w, _)

                        vdst = v_sb[:].rearrange("p (kt h e) -> p kt h e", kt=NKT, h=NH)[
                            :, tt, oh * 8:(oh + 1) * 8, 0:64]
                        psv = bank[bk].rearrange("p (h e) -> p h e", h=8)

                        if not L1:
                            def _(v, vdst=vdst, psv=psv, oh=oh):
                                vb = scr2 if oh == 0 else scr3
                                if NO_VEVAC:
                                    return v.tensor_copy(scr1[:], vb[:])
                                return v.tensor_tensor(
                                    vdst, psv, vb[:].rearrange("p (h e) -> p h e", h=8),
                                    ALU.add)
                            mev.append(b.sec("vector", [mpe], _))
                        else:
                            def _(v, vdst=vdst, psv=psv, oh=oh, tt=tt):
                                vb = scr2 if oh == 0 else scr3
                                if NO_VEVAC:
                                    return v.tensor_copy(scr1[:], vb[:])
                                v.tensor_tensor(scr1[:], bank[(tt * 2 + oh) % 2],
                                                vb[:], ALU.add)
                                return v.tensor_tensor(
                                    vdst, scr1[:].rearrange("p (h e) -> p h e", h=8),
                                    fbb_sb[:, (tt % 2) * D + oh * 512:(tt % 2) * D + (oh + 1) * 512]
                                    .rearrange("p (h e) -> p h e", h=8), ALU.mult)
                            mev.append(b.sec("vector", [mpe, mfl], _))
                    if L1:
                        def _(v, tt=tt):
                            vaug = v_sb[:].rearrange("p (kt h e) -> p kt h e", kt=NKT, h=NH)[
                                :, tt, :, 64:65]
                            srcv = fbb_sb[:, (tt % 2) * D:(tt % 2) * D + 1024].rearrange(
                                "p (h e) -> p h e", h=16)[:, :, 0:1]
                            return v.tensor_copy(vaug, srcv)
                        maug.append(b.sec("vector", [mev[-1], mfl], _))
                        mfb[tt] = maug[-1]

                if QKV_SUB < 4:
                    b.barrier()
                    continue
                if not L1:
                    def _(v):
                        vaug = v_sb[:].rearrange("p (kt h e) -> p kt h e", kt=NKT, h=NH)[:, :, :, 64:65]
                        return v.memset(vaug, 1.0)
                    mV = b.sec("vector", [mev[-1]], _)
                else:
                    mV = maug[-1]
                if DEBUG and seq == 0 and lyr == layers[0] and not dbg_done[0]:
                    def _(g):
                        b.inc(g.dma_start(out=dm["dbg_fv2"], in_=fv_sb[:]), "sG", 16)
                        return None
                    b.sec("gpsimd", [mV], _)

                b.barrier()

                b.tag = "attn"
                if PHASES < 3:
                    b.barrier()
                    continue
                halves = []
                cellct = 0
                for qc in range(NQC):
                    for pc in range(NH // 2):
                        ktmax = NKT if L1 else 4 * qc + 4
                        nhv = (ktmax + 7) // 8
                        for hi in range(nhv):
                            kts = list(range(hi * 8, min(ktmax, hi * 8 + 8)))
                            halves.append((pc, qc, hi, kts, cellct,
                                           hi == nhv - 1, ktmax))
                        cellct += 1

                myev = {}
                mlt = {}
                mlt2 = {}
                mys = {}
                mys2 = {}
                mrec_qc = {}
                mpv_last = [None]
                lastexp = {"A": None, "B": None, "half": None}

                def emit_qk(hx, half):
                    pc, qc, hi, kts, cc, islast, ktmax = half
                    buf = (hx % 2) * 8192
                    ch = pc
                    pos = 0
                    pend = []

                    def flush(which, items, buf=buf):
                        (s0, j0, m0) = items[0]
                        width = 1024 * len(items)

                        def _(s, s0=s0, j0=j0, width=width, buf=buf):
                            return s.activation(
                                p1_sb[:, buf + j0 * 1024: buf + j0 * 1024 + width],
                                ps[:, s0 * 1024: s0 * 1024 + width],
                                AF.Exp, scale=0.125)
                        m = b.sec("scalar", [it[2] for it in items], _)
                        lastexp[which] = lastexp["half"] = m

                    for idx, kt in enumerate(kts):
                        s_ = (0, 1, 2)[pos]
                        which = "A" if pos < 2 else "B"

                        def _(t, kt=kt, ch=ch, qc=qc, s_=s_):
                            t.matmul(
                                ps[:, s_ * 1024: s_ * 1024 + 512],
                                k_sb[0:64, ch * T + kt * 128: ch * T + (kt + 1) * 128],
                                q_sb[0:64, ch * T + qc * QC: ch * T + (qc + 1) * QC],
                                start=True, stop=True, tile_position=(0, 0))
                            return t.matmul(
                                ps[:, s_ * 1024 + 512: (s_ + 1) * 1024],
                                k_sb[64:128, ch * T + kt * 128: ch * T + (kt + 1) * 128],
                                q_sb[64:128, ch * T + qc * QC: ch * T + (qc + 1) * QC],
                                start=True, stop=True, tile_position=(64, 0))
                        w = []
                        if hx == 0 and idx == 0:
                            w += [mV, mQK]
                        if pos != 1 and lastexp[which] is not None:
                            w.append(lastexp[which])
                        mq = b.sec("tensor", w, _)
                        pend.append((s_, kt - hi * 8, mq))

                        if pos == 1:
                            flush("A", pend)
                            pend = []
                            pos = 2
                        elif pos == 2:
                            flush("B", pend)
                            pend = []
                            pos = 0
                        else:
                            pos = 1
                    if pend:
                        flush("A", pend)
                        pend = []

                    mmask = None
                    if not L1:
                        dk = [kt for kt in kts if 4 * qc <= kt < 4 * qc + 4]
                        if dk:
                            def _(v, dk=dk, hi=hi, qc=qc, buf=buf):
                                i = None
                                for kt in dk:
                                    j = kt - hi * 8
                                    sl = p1_sb[:, buf + j * 1024: buf + (j + 1) * 1024]
                                    i = v.tensor_tensor(
                                        sl, sl,
                                        masks_sb[:, (kt - 4 * qc) * 1024:
                                                 (kt - 4 * qc + 1) * 1024], ALU.mult)
                                return i
                            mmask = b.sec("vector", [lastexp["half"]], _)
                    return mmask

                def emit_pv(hx, half, mmask, mlastexp):
                    pc, qc, hi, kts, cc, islast, ktmax = half
                    buf = (hx % 2) * 8192
                    pvm = None
                    for hh in range(2):
                        yb = 6 + hh
                        H = pc * 2 + hh

                        def _(t, hh=hh, yb=yb, H=H, kts=kts, hi=hi, ktmax=ktmax, buf=buf):
                            if hh == 0 and DRAINS:
                                t.drain()
                            i = None
                            for kt in kts:
                                j = kt - hi * 8
                                i = t.matmul(
                                    ps[0:65, yb * 512:(yb + 1) * 512],
                                    v_sb[:, kt * VS + H * 65: kt * VS + (H + 1) * 65],
                                    p1_sb[:, buf + j * 1024 + hh * 512:
                                          buf + j * 1024 + (hh + 1) * 512],
                                    start=(kt == 0), stop=(kt == ktmax - 1),
                                    skip_group_check=True)
                            return i
                        w = [mlastexp]
                        if mmask is not None:
                            w.append(mmask)
                        if hi == 0 and yb in myev:
                            w.append(myev[yb])
                        pvm = b.sec("tensor", w, _)
                    mpv_last[0] = pvm

                    if islast:
                        p_ = cc % 2
                        ystg = ystgA if p_ == 0 else ystgB
                        for hh in range(2):
                            yb = 6 + hh
                            H = pc * 2 + hh

                            def _(v, yb=yb, hh=hh, p_=p_, ystg=ystg):
                                v.tensor_copy(ltmp[0:1, p_ * QC:(p_ + 1) * QC],
                                              ps[64:65, yb * 512:(yb + 1) * 512])
                                return v.tensor_copy(ystg[:, hh * 512:(hh + 1) * 512],
                                                     ps[0:64, yb * 512:(yb + 1) * 512])
                            w = [pvm]
                            if p_ in mlt2:
                                w.append(mlt2[p_])
                            if (p_, hh) in mys2:
                                w.append(mys2[(p_, hh)])
                            myev[yb] = b.sec("vector", w, _)

                            def _(sync, hh=hh, H=H, qc=qc, p_=p_):
                                b.inc(sync.dma_start(out=dm["lscr"][qc, H].unsqueeze(0),
                                                     in_=ltmp[0:1, p_ * QC:(p_ + 1) * QC]), "sL", 16)
                                return None
                            b.sec("sync", [myev[yb]], _)
                            mlt[0] = mlt2[p_] = b.mark("sL")

                            def _(g, hh=hh, H=H, qc=qc, ystg=ystg):
                                b.inc(g.dma_start(
                                    out=dm["ybuf"][H * 64:(H + 1) * 64, qc * QC:(qc + 1) * QC],
                                    in_=ystg[:, hh * 512:(hh + 1) * 512]), "sG", 16)
                                return None
                            b.sec("gpsimd", [myev[yb]], _)
                            mys[hh] = mys2[(p_, hh)] = b.mark("sG")
                    if islast and pc == NH // 2 - 1 and qc < 3:
                        ro = (0, 32, 64)[qc]

                        def _(g, qc=qc):
                            b.inc(g.dma_start(out=scr3[0:16, :],
                                              in_=dm["lscr"][qc]), "sG", 16)
                            return None
                        w = [mlt[0]] + ([mrec_qc[qc - 1]] if qc > 0 else [])
                        b.sec("gpsimd", w, _)
                        mlld = b.mark("sG")

                        def _(v, ro=ro):
                            sl = scr3[0:16, :]
                            v.reciprocal(sl, sl)
                            return v.tensor_copy(rpAr[ro:ro + 16, :], sl)
                        mrec_qc[qc] = b.sec("vector", [mlld], _)

                pend = None
                for hx, half in enumerate(halves):
                    mmask = emit_qk(hx, half)
                    mlastexp = lastexp["half"]
                    if pend is not None:
                        emit_pv(*pend)
                    pend = (hx, half, mmask, mlastexp)
                emit_pv(*pend)
                mcell_pv = mpv_last[0]

                if DEBUG and seq == 0 and lyr == layers[0] and not dbg_done[0]:
                    def _(g):
                        b.inc(g.dma_start(out=dm["dbg_xn"], in_=xn_sb[:]), "sG", 16)
                        b.inc(g.dma_start(out=dm["dbg_q"], in_=q_sb[:]), "sG", 16)
                        b.inc(g.dma_start(out=dm["dbg_k"], in_=k_sb[:]), "sG", 16)
                        b.inc(g.dma_start(out=dm["dbg_v"], in_=v_sb[:]), "sG", 16)
                        b.inc(g.dma_start(out=dm["dbg_fv"], in_=fv_sb[:]), "sG", 16)
                        b.inc(g.dma_start(out=dm["dbg_lp"], in_=dm["lscr"].rearrange("a b t -> (a b) t")), "sG", 16)
                        return None
                    b.sec("gpsimd", [mlt[0], mys[0], mys[1]], _)

                b.barrier()

                ust = None
                if not L1:
                    ust = unfold_start([])
                mbt = None
                if L1:
                    def _(sync):
                        b.inc(sync.dma_start(out=wstage[0:8, :],
                                             in_=dm["bTall"][:, 3 * D:4 * D]), "sL", 16)
                        return None
                    b.sec("sync", [], _)
                    mw = b.mark("sL")

                    def _(v):
                        return v.tensor_scalar(bT_cur[0:8, :], wstage[0:8, :], 0.125, None, ALU.mult)
                    mbt = b.sec("vector", [mw], _)

                def _(sync):
                    b.inc(sync.dma_start(
                        out=yblk[:].rearrange("p (c t) -> p c t", c=NCH),
                        in_=dm["ybuf"].rearrange("(c p) t -> p c t", p=128)[
                            :, :, 0:QC]), "sS", 16)
                    return None
                b.sec("sync2", [], _)
                myld0 = b.mark("sS")

                def _(sync):
                    src = xin[seq].rearrange("(c p) t -> p c t", p=128)[:, :, 0:QC]
                    b.inc(sync.dma_start(
                        out=x_blk[:].rearrange("p (c t) -> p c t", c=NCH), in_=src), "sS", 16)
                    return None
                b.sec("sync2", [], _)
                mxf0 = b.mark("sS")

                mcv = []
                for ic in range(NCH):
                    slot = wstage if ic % 2 == 0 else wstage2

                    def _(sync, ic=ic, slot=slot):
                        wsrc = dm["wprojT"] if not L1 else dm["woT"]
                        b.inc(sync.dma_start(
                            out=slot[:],
                            in_=wsrc[ic * 128:(ic + 1) * 128, :]), "sL", 16)
                        return None
                    w = ([mcv[ic - 2]] if ic >= 2 else []) + ([mbt] if (mbt and ic == 0) else [])
                    b.sec("sync", w, _)
                    mwl = b.mark("sL")

                    def _(s, ic=ic, slot=slot):
                        return s.activation(wbf[:, ic * 1024:(ic + 1) * 1024],
                                            slot[:], AF.Copy)
                    mcv.append(b.sec("scalar", [mwl], _))

                msw_hold = [None]
                sgn = {"mswc": []}

                def emit_sign_chunk(j):
                    slot = wstage if j % 2 == 0 else wstage2

                    def _(sync, j=j, slot=slot):
                        b.inc(sync.dma_start(out=slot[:],
                                             in_=dm["wfT"][j * 128:(j + 1) * 128, :]), "sL", 16)
                        return None
                    w = [mcv[6 + j]] if j < 2 else [sgn["mswc"][j - 2]]
                    b.sec("sync", w, _)
                    mswl = b.mark("sL")

                    def _(v, j=j, slot=slot):
                        dst = sw_bf[:, j * 1024:(j + 1) * 1024]
                        v.tensor_scalar(dst, slot[:], 0.0, 2.0, ALU.is_ge, ALU.mult)
                        return v.tensor_scalar(dst, dst, 1.0, None, ALU.subtract)
                    sgn["mswc"].append(b.sec("vector", [mswl], _))

                mev = []
                myld = None
                muo_prev = [None]
                mpe_tb_last = {}
                for tb in range(NQC):
                    ybt = yblk if tb % 2 == 0 else yblk2
                    if tb == 0:
                        myld = myld0
                    else:
                        def _(sync, tb=tb, ybt=ybt):
                            b.inc(sync.dma_start(
                                out=ybt[:].rearrange("p (c t) -> p c t", c=NCH),
                                in_=dm["ybuf"].rearrange("(c p) t -> p c t", p=128)[
                                    :, :, tb * QC:(tb + 1) * QC]), "sS", 16)
                            return None
                        w = [mpe_tb_last[tb - 2]] if tb >= 2 else []
                        b.sec("sync2", w, _)
                        myld = b.mark("sS")

                    if tb == 3:
                        def _(g):
                            b.inc(g.dma_start(out=scr3[0:16, :],
                                              in_=dm["lscr"][3]), "sG", 16)
                            return None
                        b.sec("gpsimd", [mrn[-1], mrec_qc[2]], _)
                        mlld3 = b.mark("sG")

                        def _(v):
                            sl = scr3[0:16, :]
                            v.reciprocal(sl, sl)
                            return v.tensor_copy(rpAr[64:80, :], sl)
                        mrec_qc[3] = b.sec("vector", [mlld3], _)

                    mrn = []
                    for ic in range(NCH):
                        half = ic % 2

                        def _(t, ic=ic, half=half, tb=tb):
                            sb = (0, 32, 64, 64)[tb]
                            return t.matmul(
                                ps[:, (4 + half) * 512:(5 + half) * 512],
                                sel_r[sb:sb + 32, ic * 128:(ic + 1) * 128],
                                rpAr[sb:sb + 32, :],
                                start=True, stop=True)
                        w = [myld, mrec_qc[tb]] + ([mrn[-2]] if len(mrn) >= 2 else [])
                        mbc = b.sec("tensor", w, _)

                        def _(v, ic=ic, half=half, ybt=ybt):
                            sl = ybt[:, ic * 512:(ic + 1) * 512]
                            return v.scalar_tensor_tensor(
                                sl, sl, 1.0, ps[:, (4 + half) * 512:(5 + half) * 512],
                                ALU.mult, ALU.mult)
                        mrn.append(b.sec("vector", [mbc], _))

                    if DEBUG and seq == 0 and lyr == layers[0] and tb == 1 and not dbg_done[0]:
                        def _(g):
                            b.inc(g.dma_start(out=dm["dbg_yblk"], in_=yblk[:]), "sG", 16)
                            b.inc(g.dma_start(out=dm["dbg_rp"], in_=rpAr[:].bitcast(F32)), "sG", 16)
                            return None
                        b.sec("gpsimd", [mrn[-1]], _)

                    if L1:
                        emit_sign_chunk(2 * tb)
                        emit_sign_chunk(2 * tb + 1)
                    else:
                        if tb < 2:
                            if tb == 0:
                                unfold_tr(ust, 0)
                            unfold_cc(ust, 0, 2 * tb)
                            unfold_cc(ust, 0, 2 * tb + 1)
                            if tb == 1:
                                unfold_tr(ust, 1)
                        else:
                            unfold_cc(ust, 1, 2 * (tb - 2))
                            unfold_cc(ust, 1, 2 * (tb - 2) + 1)
                    muo = None
                    if L1:
                        def _(t, tb=tb, ybt=ybt):
                            i = None
                            for ic in range(NCH):
                                i = t.matmul(ps[0:8, 1024:1536], ao_bf[:, ic * 8:(ic + 1) * 8],
                                             ybt[:, ic * 512:(ic + 1) * 512],
                                             start=(ic == 0), stop=(ic == NCH - 1))
                            return i
                        mpe = b.sec("tensor", [mrn[-1]] + ([muo_prev[0]] if muo_prev[0] else []), _)

                        def _(v, tb=tb):
                            return v.tensor_copy(u_o[0:8, tb * QC:(tb + 1) * QC], ps[0:8, 1024:1536])
                        muo = b.sec("vector", [mpe], _)
                        muo_prev[0] = muo

                    for oc in range(NCH):
                        bk = (tb * NCH + oc) % 2

                        def _(t, tb=tb, oc=oc, bk=bk, ybt=ybt):
                            i = None
                            for ic in range(NCH):
                                i = t.matmul(bank[bk], wbf[:, ic * 1024 + oc * 128: ic * 1024 + (oc + 1) * 128],
                                             ybt[:, ic * 512:(ic + 1) * 512],
                                             start=(ic == 0), stop=(not L1 and ic == NCH - 1))
                            if L1:
                                i = t.matmul(bank[bk], bT_cur[0:8, oc * 128:(oc + 1) * 128],
                                             u_o[0:8, tb * QC:(tb + 1) * QC], start=False, stop=True)
                            return i
                        w = [mrn[-1], mcv[-1]] + ([muo] if muo else []) + ([mev[-2]] if len(mev) >= 2 else [])
                        mpe = b.sec("tensor", w, _)
                        mpe_tb_last[tb] = mpe

                        def _(v, tb=tb, oc=oc, bk=bk):
                            dst = t1_sb[:, oc * T + tb * QC: oc * T + (tb + 1) * QC]
                            if not L1:
                                return v.tensor_scalar(dst, bank[bk],
                                                       bias24[:, 16 + oc: 17 + oc], None, ALU.add)
                            return v.tensor_scalar(dst, bank[bk],
                                                   bias24[:, 16 + oc: 17 + oc], rz_sb[:],
                                                   ALU.add, ALU.mult)
                        mev.append(b.sec("vector", [mpe], _))
                msw_hold[0] = sgn["mswc"][-1] if L1 else ust["mprev"]
                mPROJ = mev[-1]

                if DEBUG and seq == 0 and lyr == layers[0] and not dbg_done[0]:
                    def _(g):
                        b.inc(g.dma_start(out=dm["dbg_t1"], in_=t1_sb[:]), "sG", 16)
                        return None
                    b.sec("gpsimd", [mPROJ], _)
                    def _(g):
                        b.inc(g.dma_start(out=dm["dbg_y"], in_=dm["ybuf"]), "sG", 16)
                        return None
                    b.sec("gpsimd", [mPROJ], _)

                b.barrier()

                b.tag = "ffn"
                if PHASES < 5:
                    b.barrier()
                    continue
                msw = msw_hold[0]
                mst_out = []
                mfin_tb = {}
                mfin_hist = []
                mfin_prev = [None]
                mg_prev = [None]
                mcl_hist = []
                for tb in range(NQC):
                    xb = x_blk if tb % 2 == 0 else x_blk1

                    if tb == 0:
                        mxl = mxf0
                    else:
                        def _(sync, tb=tb, xb=xb):
                            src = xin[seq].rearrange("(c p) t -> p c t", p=128)[
                                :, :, tb * QC:(tb + 1) * QC]
                            b.inc(sync.dma_start(
                                out=xb[:].rearrange("p (c t) -> p c t", c=NCH), in_=src), "sS", 16)
                            return None
                        w = [mfin_tb[tb - 2]] if tb >= 2 else []
                        b.sec("sync2", w, _)
                        mxl = b.mark("sS")

                    for po in range(4):
                        oc0 = 2 * po
                        bks = (po % 2) * 2

                        def _(t, tb=tb, oc0=oc0, bks=bks):
                            i = None
                            for j in range(2):
                                oc = oc0 + j
                                if not L1:
                                    for ic in range(NCH):
                                        if oc < 4:
                                            wt = (wr_bf if ic < 4 else win_bf)
                                            col = (ic % 4) * 512 + oc * 128
                                        else:
                                            wt = (wi_bf if ic < 4 else wr_bf)
                                            col = (ic % 4) * 512 + (oc - 4) * 128
                                        i = t.matmul(bank[bks + j], wt[:, col:col + 128],
                                                     xn_sb[:, ic * T + tb * QC: ic * T + (tb + 1) * QC],
                                                     start=(ic == 0), stop=(ic == NCH - 1))
                                else:
                                    for ic in range(NCH):
                                        i = t.matmul(bank[bks + j],
                                                     sw_bf[:, ic * 1024 + oc * 128: ic * 1024 + (oc + 1) * 128],
                                                     xn_sb[:, ic * T + tb * QC: ic * T + (tb + 1) * QC],
                                                     start=(ic == 0), stop=(ic == NCH - 1))
                            return i
                        w = [msw] + ([mcl_hist[-2]] if len(mcl_hist) >= 2 else [])
                        mpe = b.sec("tensor", w, _)

                        if not L1:
                            fB = ffB if len(mst_out) % 2 == 0 else ffA

                            def _(s, oc0=oc0, bks=bks, fB=fB):
                                s.activation(fB[:, 0:512], bank[bks], AF.Silu,
                                             bias=ffb_sb[:, oc0:oc0 + 1], scale=1.0)
                                return s.activation(fB[:, 512:1024], bank[bks + 1], AF.Silu,
                                                    bias=ffb_sb[:, oc0 + 1:oc0 + 2], scale=1.0)
                            w = [mpe] + ([mfin_hist[-2]] if len(mfin_hist) >= 2 else [])
                            mg = b.sec("scalar", w, _)
                            mcl_hist.append(mg)

                            fC = ffC if len(mst_out) % 2 == 0 else ffC2

                            def _(v, tb=tb, oc0=oc0, fC=fC, xb=xb, fB=fB):
                                t1p = t1_sb[:].rearrange("p (c t) -> p c t", c=NCH)[
                                    :, oc0:oc0 + 2, tb * QC:(tb + 1) * QC]
                                fv3 = fC[:].rearrange("p (c t) -> p c t", c=2)
                                v.tensor_tensor(fv3, fB[:].rearrange("p (c t) -> p c t", c=2),
                                                t1p, ALU.mult)
                                return v.tensor_tensor(fC[:], fC[:],
                                                       xb[:, oc0 * 512:(oc0 + 2) * 512], ALU.add)
                            w = [mg, mxl] + ([mst_out[-2]] if len(mst_out) >= 2 else [])
                            mfin = b.sec("vector", w, _)
                            mfin_hist.append(mfin)
                            mfin_prev[0] = mfin
                        else:
                            def _(v, oc0=oc0, bks=bks):
                                v.tensor_scalar(ffA[:, 0:512], bank[bks], fsc_sb[:, oc0:oc0 + 1],
                                                fbi_sb[:, oc0:oc0 + 1], ALU.mult, ALU.add)
                                v.tensor_scalar(ffA[:, 512:1024], bank[bks + 1],
                                                fsc_sb[:, oc0 + 1:oc0 + 2],
                                                fbi_sb[:, oc0 + 1:oc0 + 2], ALU.mult, ALU.add)
                                return v.tensor_scalar(ffA[:], ffA[:], -100.0, 100.0,
                                                       ALU.max, ALU.min)
                            w = [mpe] + ([mg_prev[0]] if mg_prev[0] else [])
                            mcl = b.sec("vector", w, _)
                            mcl_hist.append(mcl)

                            def _(s):
                                return s.activation(ffB[:], ffA[:], AF.Silu)
                            w = [mcl] + ([mfin_prev[0]] if mfin_prev[0] else [])
                            mg = b.sec("scalar", w, _)
                            mg_prev[0] = mg

                            fC = ffC if len(mst_out) % 2 == 0 else ffC2

                            def _(v, tb=tb, oc0=oc0, fC=fC, xb=xb):
                                fv3 = fC[:].rearrange("p (c t) -> p c t", c=2)
                                xn3 = xn_sb[:].rearrange("p (c t) -> p c t", c=NCH)[
                                    :, oc0:oc0 + 2, tb * QC:(tb + 1) * QC]
                                t1p = t1_sb[:].rearrange("p (c t) -> p c t", c=NCH)[
                                    :, oc0:oc0 + 2, tb * QC:(tb + 1) * QC]
                                v.tensor_tensor(fv3, xn3, t1p, ALU.add)
                                v.tensor_tensor(fC[:], ffB[:], fC[:], ALU.mult)
                                return v.tensor_tensor(fC[:], fC[:],
                                                       xb[:, oc0 * 512:(oc0 + 2) * 512], ALU.add)
                            w = [mg, mxl] + ([mst_out[-2]] if len(mst_out) >= 2 else [])
                            mfin = b.sec("vector", w, _)
                            mfin_prev[0] = mfin

                        def _(g, tb=tb, oc0=oc0, fC=fC):
                            b.inc(g.dma_start(
                                out=xout[seq, oc0 * 128:(oc0 + 2) * 128,
                                         tb * QC:(tb + 1) * QC].rearrange(
                                             "(c p) t -> p c t", p=128),
                                in_=fC[:].rearrange("p (c t) -> p c t", c=2)), "sG", 16)
                            return None
                        b.sec("gpsimd", [mfin], _)
                        mst_out.append(b.mark("sG"))
                    mfin_tb[tb] = mfin

                if not (seq == NSEQ - 1 and lyr == layers[-1]):
                    nseq, nlyr = (seq, layers[-1]) if lyr == layers[0] and len(layers) > 1 else (seq + 1, layers[0])

                    def _(sync, nseq=nseq, nlyr=nlyr):
                        src = xsrc[nlyr][nseq].rearrange("(c p) t -> p c t", p=128)[
                            :, :, 0:QC]
                        b.inc(sync.dma_start(
                            out=x_blk[:].rearrange("p (c t) -> p c t", c=NCH), in_=src), "sS", 16)
                        return None
                    b.sec("sync2", [mfin_tb[2], mst_out[3]], _)
                    pre["mx0"] = b.mark("sS")

                if DEBUG and seq == 0 and lyr == layers[0]:
                    dbg_done[0] = True
                b.barrier()

        def _(g):
            g.wait_ge(b.sems["sG"], b.cnt["sG"])
            return None
        b.sec("gpsimd", [], _)
        b.flush()
        nc._ledger = dict(b.cnt)
        try:
            nc._addrs = {}
            for _nm, _h in [("arenaA", arenaA), ("arenaB", arenaB), ("arenaC", arenaC),
                            ("xn_sb", xn_sb),
                            ("masks_sb", masks_sb), ("eps_sb", eps_sb), ("ones_r", ones_r),
                            ("st_tmp", st_tmp), ("r_r", r_r), ("mr_r", mr_r),
                            ("ones_rr", ones_rr),
                            ("sel_r", sel_r), ("rpAr", rpAr), ("ltmp", ltmp),
                            ("fbb_sb", fbb_sb), ("stg", stg),
                            ("bias24", bias24), ("fsc_sb", fsc_sb), ("fbi_sb", fbi_sb),
                            ("ffb_sb", ffb_sb), ("rz_sb", rz_sb), ("u_v", u_v), ("u_o", u_o),
                            ("a1_bf", a1_bf), ("av_bf", av_bf), ("ao_bf", ao_bf),
                            ("bT_cur", bT_cur), ("scr1", scr1), ("scr2", scr2), ("scr3", scr3)]:
                _m = nc.lookup_mloc(_h)
                nc._addrs[_nm] = (_m.addr, _h.shape, str(_h.dtype))
        except Exception as _e:
            nc._addrs = {"error": str(_e)}

    return nc




def _host_constants():
    slopes = 0.5 ** np.arange(NH, dtype=np.float64)
    p = np.arange(128, dtype=np.float64)
    fvals_ = np.ones((2, 256), dtype=np.float64)
    for kt in range(NKT):
        for h in range(NH):
            fvals_[1, kt * NH + h] = np.exp(slopes[h] * (128.0 * kt - 1920.0))
    fexp = np.ones((NKT, 128, D), dtype=np.float32)
    for kt in range(NKT):
        arg = slopes[None, :, None] * (128.0 * kt + p[:, None, None] - 2047.0)
        fexp[kt] = np.broadcast_to(np.exp(arg).astype(BF).astype(np.float32),
                                   (128, NH, HD)).reshape(128, D)
    masks = np.zeros((128, 8 * QC), dtype=np.float32)
    t = np.arange(QC)
    m1 = np.zeros((128, QC), dtype=np.float32)
    for j in range(4):
        m1[:] = (p[:, None] + 128 * j <= t[None, :])
        masks[:, j * 2 * QC: j * 2 * QC + QC] = m1
        masks[:, j * 2 * QC + QC: (j + 1) * 2 * QC] = m1
    repD = np.zeros((32, 4, 128), dtype=np.float32)
    for cc in range(4):
        for pp in range(128):
            repD[cc * 8 + pp // 16, cc, pp] = 1.0
    repG = np.zeros((16, 128), dtype=np.float32)
    for pp in range(128):
        repG[pp % 16, pp] = 1.0
    idn = np.eye(128, dtype=np.float32)
    sel = np.zeros((128, 8 * 128), dtype=np.float32)
    for ic in range(8):
        for p in range(128):
            for off in (0, 32, 64):
                sel[off + 2 * ic + p // 64, ic * 128 + p] = 1.0
    return {
        "idn": idn,
        "fvrep": np.ascontiguousarray(np.broadcast_to(
            fvals_.astype(BF)[:, None, :], (2, 128, 256))),
        "fexp": fexp,
        "masks": masks.astype(np.uint8), "repD": repD, "repG": repG,
        "eps_t": np.full((1, 1), 1e-5, dtype=np.float32),
        "onescol": np.ones((128, 1), dtype=np.float32),
        "onesrr": np.ones((1, 128), dtype=np.float32),
        "selr": sel,
    }


def _fm(vec):
    v = np.asarray(vec, dtype=np.float32)
    n = v.shape[0] // 128
    return np.ascontiguousarray(v.reshape(n, 128).T)


def _host_inputs(inputs, layers=(0, 1)):
    first, last = (0 in layers), (1 in layers)
    d = dict(_host_constants())
    bvrep = np.zeros((2, 128, D), dtype=np.float32)
    if first:
        Wqkv = inputs["L0_attn_W_qkv"]
        d["wqkvT"] = np.ascontiguousarray(Wqkv.T)
        bq = inputs["L0_attn_b_qkv"]
        d["bqkv_fm"] = np.concatenate([_fm(bq[0:D]), _fm(bq[D:2 * D])], axis=1)
        d["bproj_fm"] = _fm(inputs["L0_attn_b_proj"])
        d["wprojT"] = np.ascontiguousarray(inputs["L0_attn_W_proj"].T)
        g1r = inputs["L0_ffn_G1_R"][0].reshape(32, 128)
        g1i = inputs["L0_ffn_G1_I"][0].reshape(32, 128)
        d["g1"] = np.ascontiguousarray(np.concatenate([g1r, g1i], axis=1))
        g2r = inputs["L0_ffn_G2_R"][:, :, 0, :].reshape(64, 16)
        g2i = inputs["L0_ffn_G2_I"][:, :, 0, :].reshape(64, 16)
        d["g2"] = np.ascontiguousarray(np.concatenate([g2r, g2i], axis=1))
        d["fbias_fm"] = _fm(np.concatenate([inputs["L0_ffn_bias_real"],
                                            inputs["L0_ffn_bias_imag"]]))
        bvrep[0] = np.broadcast_to(bq[2 * D:3 * D], (128, D))
    if last:
        d["wqT"] = np.ascontiguousarray(inputs["L1_attn_Wq"].T)
        d["wkT"] = np.ascontiguousarray(inputs["L1_attn_Wk"].T)
        d["wvT"] = np.ascontiguousarray(inputs["L1_attn_Wv"].T)
        d["woT"] = np.ascontiguousarray(inputs["L1_attn_Wo"].T)
        d["bq_fm"] = _fm(inputs["L1_attn_bq"])
        d["bk_fm"] = _fm(inputs["L1_attn_bk"])
        d["bo_fm"] = _fm(inputs["L1_attn_bo"])
        a1 = np.zeros((D, 64), dtype=np.float32)
        a1[:, 0:8] = inputs["L1_attn_Aq"].T
        a1[:, 32:40] = inputs["L1_attn_Ak"].T
        d["a1T"] = a1
        d["avT"] = np.ascontiguousarray(inputs["L1_attn_Av"].T)
        d["aoT"] = np.ascontiguousarray(inputs["L1_attn_Ao"].T)
        d["bTall"] = np.ascontiguousarray(np.concatenate(
            [inputs["L1_attn_Bq"].T, inputs["L1_attn_Bk"].T,
             inputs["L1_attn_Bv"].T, inputs["L1_attn_Bo"].T], axis=1))
        d["rzrep"] = np.ascontiguousarray(np.broadcast_to(
            inputs["L1_attn_rezero"].reshape(1, 1), (128, 1))).astype(np.float32)
        d["wfT"] = np.ascontiguousarray(inputs["L1_ffn_weights"].T)
        d["fscale_fm"] = _fm(inputs["L1_ffn_scale"])
        d["fbias1_fm"] = _fm(inputs["L1_ffn_bias"])
        bvrep[1] = np.broadcast_to(inputs["L1_attn_bv"], (128, D))
    d["bvrep"] = bvrep
    return d


def kernel(**inputs):
    shared0 = _host_inputs(inputs, (0, 1) if MODE == "fused" else (0,))
    x = np.ascontiguousarray(inputs["L0_attn_x"], dtype=np.float32)

    if MODE == "fused":
        nc = build_program((0, 1))
        in_maps = []
        for c in range(NCORES):
            m = dict(shared0)
            m["x"] = np.ascontiguousarray(
                x[c * NSEQ:(c + 1) * NSEQ].transpose(0, 2, 1))
            in_maps.append(m)
        res = run_bass_kernel_spmd(nc, in_maps, core_ids=list(range(NCORES)))
        outs = [r["out"].transpose(0, 2, 1) for r in res.results]
        return np.ascontiguousarray(np.concatenate(outs, axis=0), dtype=np.float32)

    nc0 = build_program((0,))
    in_maps = []
    for c in range(NCORES):
        m = dict(shared0)
        m["x"] = np.ascontiguousarray(x[c * NSEQ:(c + 1) * NSEQ].transpose(0, 2, 1))
        in_maps.append(m)
    res0 = run_bass_kernel_spmd(nc0, in_maps, core_ids=list(range(NCORES)))
    xmids = [r["xmid"] for r in res0.results]

    shared1 = _host_inputs(inputs, (1,))
    nc1 = build_program((1,))
    in_maps = []
    for c in range(NCORES):
        m = dict(shared1)
        m["xmid"] = np.ascontiguousarray(xmids[c])
        in_maps.append(m)
    res1 = run_bass_kernel_spmd(nc1, in_maps, core_ids=list(range(NCORES)))
    outs = [r["out"].transpose(0, 2, 1) for r in res1.results]
    return np.ascontiguousarray(np.concatenate(outs, axis=0), dtype=np.float32)


# version 25
# speedup vs baseline: 1.2068x; 1.0144x over previous
import numpy as np
import ml_dtypes
from contextlib import ExitStack
import concourse.bass as bass
import concourse.mybir as mybir
from concourse.bass_utils import run_bass_kernel_spmd

F32 = mybir.dt.float32
F32R = mybir.dt.float32r
BF16 = mybir.dt.bfloat16
U8 = mybir.dt.uint8
AF = mybir.ActivationFunctionType
ALU = mybir.AluOpType
AX = mybir.AxisListType
BF = ml_dtypes.bfloat16

NCORES = 8
NSEQ = 2
D = 1024
T = 2048
NCH = 8
NH = 16
HD = 64
QC = 512
NQC = T // QC
NKT = T // 128
VS = NH * (HD + 1)

MODE = "fused"
PHASES = 99
QKV_SUB = 9
Q_SUB = 9
DEBUG = False
SERIALIZE = False
SER_TAGS = set()
SER_ENGS = {"tensor", "vector", "scalar", "gpsimd", "sync"}
EVAC_FULLWAIT = False
NO_AUG = False
NO_VEVAC = False
DMA_QUEUE = "gpsimd"
DRAINS = False

ENG_SEM = {"tensor": "sP", "vector": "sV", "scalar": "sA", "gpsimd": "sO", "sync": "sL"}


class _Mock:
    def __getattr__(self, name):
        return lambda *a, **kw: self
_MOCK = _Mock()


class Bld:

    def __init__(self, nc, block, sems):
        self.nc, self.block, self.sems = nc, block, sems
        self.cnt = {k: 0 for k in sems}
        self.tag = ""
        self.pend = {e: [] for e in ("tensor", "vector", "scalar", "gpsimd", "sync")}

    def inc(self, instr, s, k):
        if instr is _MOCK:
            self.cnt[s] += k
        else:
            instr.then_inc(self.sems[s], k)

    def mark(self, s):
        return (s, self.cnt[s])

    def sec(self, eng, waits, emit, sem=None):
        if eng == "sync2":
            eng = "sync"
            sem = sem or "sS"
        elif eng == "sync":
            eng = DMA_QUEUE
        """One section on `eng`: waits (markers) then emit(handle). emit returns
        the last instruction (gets +1 on the engine sem) or None (emit did its
        own incs, e.g. DMA +16s). Returns a completion marker."""
        s = sem or ENG_SEM[eng]
        me = self
        ser = []
        if SERIALIZE or (self.tag in SER_TAGS and eng in SER_ENGS):
            ser = [(ss, self.cnt[ss]) for ss in ("sL", "sG", "sP", "sV", "sA", "sO")
                   if self.cnt[ss] > 0 and ss != s]
        if emit(_MOCK) is not None:
            self.cnt[s] += 1
        wl = ser + list(waits)

        def body(h):
            seen = {}
            for (ws, wv) in wl:
                if wv > 0 and seen.get(ws, -1) < wv:
                    h.wait_ge(me.sems[ws], wv)
                    seen[ws] = wv
            last = emit(h)
            if last is not None:
                last.then_inc(me.sems[s], 1)
        self.pend[eng].append(body)
        return self.mark(s)

    def flush(self):
        for e in ("sync", "gpsimd", "tensor", "vector", "scalar"):
            fns = self.pend[e]
            if not fns:
                continue
            self.pend[e] = []

            def body(h, fns=fns):
                for f in fns:
                    f(h)
            getattr(self.block, e)(body)

    def barrier(self):
        self.flush()
        tgt = self.cnt["bar"] + 5
        nG, nL, nS = self.cnt["sG"], self.cnt["sL"], self.cnt["sS"]

        def _(h):
            h.wait_ge(self.sems["sG"], nG)
        self.block.gpsimd(_)

        def _(h):
            h.wait_ge(self.sems["sL"], nL)
            h.wait_ge(self.sems["sS"], nS)
        self.block.sync(_)
        for e in ENG_SEM:
            def _(h):
                h.sem_inc(self.sems["bar"], 1)
            getattr(self.block, e)(_)
        self.cnt["bar"] = tgt
        for e in ENG_SEM:
            def _(h, tgt=tgt):
                h.wait_ge(self.sems["bar"], tgt)
            getattr(self.block, e)(_)


def build_program(layers=(0, 1)):
    first, last = (0 in layers), (1 in layers)
    nc = bass.Bass("TRN2", target_bir_lowering=False)
    dm = {}

    def din(name, shape, dt=F32, kind="ExternalInput"):
        dm[name] = nc.dram_tensor(name, shape, dt, kind=kind).ap()

    if first:
        din("x", [NSEQ, D, T])
    if first and last:
        din("xmid", [NSEQ, D, T], kind="Internal")
    elif first:
        din("xmid", [NSEQ, D, T], kind="ExternalOutput")
    else:
        din("xmid", [NSEQ, D, T], kind="ExternalInput")
    if last:
        din("out", [NSEQ, D, T], kind="ExternalOutput")
    din("ybuf", [D, T], BF16, kind="Internal")

    if first:
        din("wqkvT", [D, 3 * D])
        din("bqkv_fm", [128, 16])
        din("bproj_fm", [128, 8])
        din("wprojT", [D, D])
        din("g1", [32, 256])
        din("g2", [64, 32])
        din("fbias_fm", [128, 8])
        din("repD", [32, 4, 128])
        din("repG", [16, 128])
    if last:
        for p in "qkvo":
            din("w%sT" % p, [D, D])
        din("bq_fm", [128, 8]); din("bk_fm", [128, 8]); din("bo_fm", [128, 8])
        din("a1T", [D, 64]); din("avT", [D, 8]); din("aoT", [D, 8])
        din("bTall", [8, 4 * D])
        din("rzrep", [128, 1])
        din("wfT", [D, D]); din("fscale_fm", [128, 8]); din("fbias1_fm", [128, 8])
    din("bvrep", [2, 128, D])
    din("idn", [128, 128])
    din("fvrep", [2, 128, 256], BF16)
    din("fexp", [NKT, 128, D])
    din("masks", [128, 8 * QC], U8)
    din("eps_t", [1, 1])
    din("onescol", [128, 1])
    din("onesrr", [1, 128])
    din("selr", [128, 8 * 128])
    dm["lscr"] = nc.dram_tensor("lscr", [4, 16, QC], F32, kind="Internal").ap()
    if DEBUG:
        din("dbg_xn", [128, NCH * T], BF16, kind="ExternalOutput")
        din("dbg_q", [128, NCH * T], BF16, kind="ExternalOutput")
        din("dbg_k", [128, NCH * T], BF16, kind="ExternalOutput")
        din("dbg_v", [128, NKT * VS], BF16, kind="ExternalOutput")
        din("dbg_lp", [64, QC], kind="ExternalOutput")
        din("dbg_t1", [128, NCH * T], BF16, kind="ExternalOutput")
        din("dbg_yblk", [128, NCH * QC], BF16, kind="ExternalOutput")
        din("dbg_rp", [64, QC], kind="ExternalOutput")
        din("dbg_y", [D, T], BF16, kind="ExternalOutput")
        din("dbg_fv", [128, 256], BF16, kind="ExternalOutput")
        din("dbg_fv1", [128, 256], BF16, kind="ExternalOutput")
        din("dbg_fv2", [128, 256], BF16, kind="ExternalOutput")
        din("dbg_p1", [128, 256], BF16, kind="ExternalOutput")
        din("dbg_p2", [128, 256], BF16, kind="ExternalOutput")
        din("dbg_p3", [128, 256], BF16, kind="ExternalOutput")
        din("dbg_fvdram", [2, 128, 256], BF16, kind="ExternalOutput")

    xsrc = {0: dm.get("x"), 1: dm["xmid"]}
    xdst = {0: dm["xmid"], 1: dm.get("out")}

    es = ExitStack()
    with es:
        def _al(f, *a):
            return es.enter_context(f(*a))
        arenaA = _al(nc.sbuf_tensor, "arenaA", [128, 65536], U8)
        arenaB = _al(nc.sbuf_tensor, "arenaB", [128, 33280], U8)
        arenaC = _al(nc.sbuf_tensor, "arenaC", [128, 32768], U8)
        xn_sb = _al(nc.sbuf_tensor, "xn_sb", [128, NCH * T], BF16)
        if DEBUG:
            fv_sb = _al(nc.sbuf_tensor, "fv_sb", [128, 256], BF16)
        masks_sb = _al(nc.sbuf_tensor, "masks_sb", [128, 8 * QC], U8)
        eps_sb = _al(nc.sbuf_tensor, "eps_sb", [1, 1], F32)
        ones_r = _al(nc.sbuf_tensor, "ones_r", [128, 1], F32)
        st_tmp = _al(nc.sbuf_tensor, "st_tmp", [1, 2 * QC], F32)
        r_r = _al(nc.sbuf_tensor, "r_r", [1, QC], F32R)
        mr_r = _al(nc.sbuf_tensor, "mr_r", [1, QC], F32R)
        ones_rr = _al(nc.sbuf_tensor, "ones_rr", [1, 128], F32R)
        sel_r = _al(nc.sbuf_tensor, "sel_r", [128, 8 * 128], F32R)
        rpAr = _al(nc.sbuf_tensor, "rpAr", [128, QC], F32R)
        ltmp = _al(nc.sbuf_tensor, "ltmp", [1, 2 * QC], F32)
        fbb_sb = _al(nc.sbuf_tensor, "fbb_sb", [128, 2 * D], F32)
        stg = _al(nc.sbuf_tensor, "stg", [128, 2048], U8)
        ystgA = stg[:, 0:2048].bitcast(BF16)[0:64, :]
        ystgB = stg[:, 0:2048].bitcast(BF16)[64:128, :]
        x2stg = stg[:, 0:2048].bitcast(F32)
        bias24 = _al(nc.sbuf_tensor, "bias24", [128, 24], F32)
        fsc_sb = _al(nc.sbuf_tensor, "fsc_sb", [128, 8], F32)
        fbi_sb = _al(nc.sbuf_tensor, "fbi_sb", [128, 8], F32)
        ffb_sb = _al(nc.sbuf_tensor, "ffb_sb", [128, 8], F32)
        rz_sb = _al(nc.sbuf_tensor, "rz_sb", [128, 1], F32)
        u_v = _al(nc.sbuf_tensor, "u_v", [8, T], BF16)
        u_o = u_v
        a1_bf = _al(nc.sbuf_tensor, "a1_bf", [128, NCH * 64], BF16)
        av_bf = _al(nc.sbuf_tensor, "av_bf", [128, NCH * 8], BF16)
        ao_bf = _al(nc.sbuf_tensor, "ao_bf", [128, NCH * 8], BF16)
        bT_cur = _al(nc.sbuf_tensor, "bT_cur", [8, D], BF16)
        scr1 = _al(nc.sbuf_tensor, "scr1", [128, 512], F32)
        scr2 = _al(nc.sbuf_tensor, "scr2", [128, 512], F32)
        scr3 = _al(nc.sbuf_tensor, "scr3", [128, 512], F32)
        ps = _al(nc.psum_tensor, "ps", [128, 4096], F32)
        _sL = _al(nc.semaphore, "sL")
        _sG = _al(nc.semaphore, "sG")
        _sP = _al(nc.semaphore, "sP")
        _sV = _al(nc.semaphore, "sV")
        _sA = _al(nc.semaphore, "sA")
        _sO = _al(nc.semaphore, "sO")
        _sS = _al(nc.semaphore, "sS")
        _sW = _al(nc.semaphore, "sW")
        _bar = _al(nc.semaphore, "bar")
        block = _al(nc.Block)
        b = Bld(nc, block, dict(sL=_sL, sG=_sG, sP=_sP, sV=_sV, sA=_sA, sO=_sO, sS=_sS, sW=_sW, bar=_bar))
        bank = [ps[:, i * 512:(i + 1) * 512] for i in range(8)]

        q_sb = arenaA[:, 0:32768].bitcast(BF16)
        k_sb = arenaA[:, 32768:65536].bitcast(BF16)
        t1_sb = arenaA[:, 0:32768].bitcast(BF16)
        sw_bf = arenaA[:, 32768:65536].bitcast(BF16)
        wr_bf = arenaA[:, 32768:36864].bitcast(BF16)
        wi_bf = arenaA[:, 36864:40960].bitcast(BF16)
        win_bf = arenaA[:, 40960:45056].bitcast(BF16)
        v_sb = arenaB[:, 0:2 * NKT * VS].bitcast(BF16)
        x_blk = arenaB[:, 0:16384].bitcast(F32)
        x_blk1 = arenaB[:, 16384:32768].bitcast(F32)
        xsq_c = arenaC[:, 28672:32768].bitcast(F32)
        p1_sb = arenaC[:, 0:32768].bitcast(BF16)
        wbf = arenaC[:, 0:16384].bitcast(BF16)
        wstage = arenaC[:, 16384:20480].bitcast(F32)
        wstage2 = arenaC[:, 28672:32768].bitcast(F32)
        ffA = arenaC[:, 4096:8192].bitcast(F32)
        ffB = arenaC[:, 8192:12288].bitcast(F32)
        ffC = arenaC[:, 12288:16384].bitcast(F32)
        ffC2 = arenaC[:, 0:4096].bitcast(F32)
        yblk = arenaC[:, 20480:28672].bitcast(BF16)
        yblk2 = arenaB[:, 16384:24576].bitcast(BF16)
        u_q = arenaC[:, 20480:24576].bitcast(BF16)
        u_k = arenaC[:, 24576:28672].bitcast(BF16)
        idn_sb = arenaB[:, 29824:30336].bitcast(F32)
        gld = arenaB[:, 24576:25728].bitcast(F32)[0:64, :]
        g1t = arenaB[:, 25728:26240].bitcast(F32)[0:32, :]
        g2t = arenaB[:, 26240:26496].bitcast(F32)[0:16, :]
        repD_sb = arenaB[:, 26496:28544].bitcast(F32)[0:32, :]
        repG_sb = arenaB[:, 28544:29056].bitcast(F32)[0:16, :]
        g1rep = arenaB[:, 29056:29568].bitcast(F32)
        g2rep = arenaB[:, 29568:29824].bitcast(F32)

        def _(sync):
            for (dst, src) in [
                (masks_sb[:], dm["masks"]), (eps_sb[:], dm["eps_t"]),
            ]:
                b.inc(sync.dma_start(out=dst, in_=src), "sL", 16)
            b.inc(sync.dma_start(out=ones_r[:], in_=dm["onescol"]), "sL", 16)
            b.inc(sync.dma_start(out=ones_rr[:], in_=dm["onesrr"].bitcast(F32R)), "sL", 16)
            b.inc(sync.dma_start(out=sel_r[:], in_=dm["selr"].bitcast(F32R)), "sL", 16)
            return None
        b.sec("sync", [], _)

        def _(v):
            v.memset(scr3[:], 1.0)
            return v.tensor_copy(rpAr[:], scr3[:])
        b.sec("vector", [], _)

        def _(sync):
            if first:
                b.inc(sync.dma_start(out=ffb_sb[:], in_=dm["fbias_fm"]), "sL", 16)
            if last:
                b.inc(sync.dma_start(out=rz_sb[:], in_=dm["rzrep"]), "sL", 16)
                b.inc(sync.dma_start(out=fsc_sb[:], in_=dm["fscale_fm"]), "sL", 16)
                b.inc(sync.dma_start(out=fbi_sb[:], in_=dm["fbias1_fm"]), "sL", 16)
                pass
            return None
        b.sec("sync", [], _)
        mSetup = b.mark("sL")

        if last:
            mprev = mSetup
            for (dst, srcn, w_) in (
                (a1_bf, "a1T", 64), (av_bf, "avT", 8), (ao_bf, "aoT", 8),
            ):
                def _(sync, srcn=srcn, w_=w_):
                    for ic in range(NCH):
                        b.inc(sync.dma_start(out=wstage[:, ic * w_:(ic + 1) * w_],
                                             in_=dm[srcn][ic * 128:(ic + 1) * 128, :]), "sL", 16)
                    return None
                b.sec("sync", [mprev], _)
                mw = b.mark("sL")

                def _(v, dst=dst, w_=w_):
                    return v.tensor_copy(dst[:], wstage[:, 0:NCH * w_])
                mprev = b.sec("vector", [mw], _)

        def unfold_start(waits):
            def _(sync):
                b.inc(sync.dma_start(out=gld[0:32, 0:256], in_=dm["g1"]), "sL", 16)
                b.inc(sync.dma_start(out=gld[0:64, 256:288], in_=dm["g2"]), "sL", 16)
                b.inc(sync.dma_start(out=repD_sb[:].rearrange("d (c p) -> d c p", c=4),
                                     in_=dm["repD"]), "sL", 16)
                b.inc(sync.dma_start(out=repG_sb[:], in_=dm["repG"]), "sL", 16)
                b.inc(sync.dma_start(out=idn_sb[:], in_=dm["idn"]), "sL", 16)
                return None
            b.sec("sync", waits, _)
            return {"mld": b.mark("sL"), "mprev": None}

        def unfold_tr(st, mi):
            def _(t, mi=mi):
                for c in range(4):
                    t.transpose(ps[0:32, 3584 + c * 32:3584 + (c + 1) * 32],
                                gld[0:32, mi * 128 + c * 32:mi * 128 + (c + 1) * 32],
                                idn_sb[0:32, 0:32])
                return t.transpose(ps[0:16, 3712:3776],
                                   gld[0:64, 256 + mi * 16:256 + (mi + 1) * 16],
                                   idn_sb[0:64, 0:64])
            mT = b.sec("tensor", [st["mld"]] + ([st["mprev"]] if st["mprev"] else []), _)

            def _(v):
                v.tensor_copy(g1t[0:32, :], ps[0:32, 3584:3712])
                return v.tensor_copy(g2t[0:16, :], ps[0:16, 3712:3776])
            st["mprev"] = b.sec("vector", [mT], _)

        def unfold_cc(st, mi, cc):
            wdst, wneg = ((wr_bf, None) if mi == 0 else (wi_bf, win_bf))

            def _(t, cc=cc):
                i = None
                for c in range(4):
                    t.matmul(ps[:, 3776 + c * 32:3776 + (c + 1) * 32],
                             repD_sb[:, cc * 128:(cc + 1) * 128],
                             g1t[0:32, c * 32:(c + 1) * 32], start=True, stop=True)
                    i = t.matmul(ps[:, 3904 + c * 16:3904 + (c + 1) * 16],
                                 repG_sb[:], g2t[0:16, c * 16:(c + 1) * 16],
                                 start=True, stop=True)
                return i
            mrep = b.sec("tensor", [st["mprev"]], _)

            def _(v, cc=cc, wdst=wdst, wneg=wneg):
                v.tensor_copy(g1rep[:], ps[:, 3776:3904])
                v.tensor_copy(g2rep[:], ps[:, 3904:3968])
                for c in range(4):
                    av = g1rep[:, c * 32:(c + 1) * 32].unsqueeze(2).broadcast_to([128, 32, 16])
                    bv = g2rep[:, c * 16:(c + 1) * 16].unsqueeze(1).broadcast_to([128, 32, 16])
                    dst = (scr1 if c == 0 else scr2)[:, 0:512].rearrange(
                        "p (i j) -> p i j", i=32)
                    v.tensor_tensor(dst, av, bv, ALU.mult)
                    if c > 0:
                        v.tensor_tensor(scr1[:], scr1[:], scr2[:], ALU.add)
                i = v.tensor_copy(wdst[:, cc * 512:(cc + 1) * 512], scr1[:])
                if wneg is not None:
                    i = v.tensor_scalar(wneg[:, cc * 512:(cc + 1) * 512],
                                        scr1[:], -1.0, None, ALU.mult)
                return i
            st["mprev"] = b.sec("vector", [mrep], _)

        if DEBUG:
            def _(v):
                return v.memset(fv_sb[:], 7.0)
            msent = b.sec("vector", [], _)

            def _(g):
                b.inc(g.dma_start(out=dm["dbg_p1"], in_=fv_sb[:]), "sG", 16)
                return None
            b.sec("gpsimd", [msent], _)
        b.barrier()

        dbg_done = [False]
        pre = {}

        if first:
            def _(sync):
                src = dm["x"][0].rearrange("(c p) t -> p c t", p=128)[:, :, 0:QC]
                b.inc(sync.dma_start(
                    out=x_blk[:].rearrange("p (c t) -> p c t", c=NCH), in_=src), "sS", 16)
                return None
            b.sec("sync2", [], _)
            pre["mx0"] = b.mark("sS")

        for seq in range(NSEQ):
            for lyr in layers:
                L1 = (lyr == 1)
                xin = xsrc[lyr]
                xout = xdst[lyr]

                def _(sync):
                    if not L1:
                        b.inc(sync.dma_start(out=bias24[:, 0:16], in_=dm["bqkv_fm"]), "sL", 16)
                        b.inc(sync.dma_start(out=bias24[:, 16:24], in_=dm["bproj_fm"]), "sL", 16)
                    else:
                        b.inc(sync.dma_start(out=bias24[:, 0:8], in_=dm["bq_fm"]), "sL", 16)
                        b.inc(sync.dma_start(out=bias24[:, 8:16], in_=dm["bk_fm"]), "sL", 16)
                        b.inc(sync.dma_start(out=bias24[:, 16:24], in_=dm["bo_fm"]), "sL", 16)
                    return None
                b.sec("sync", [], _)
                mbias = b.mark("sL")

                mbtQ = None
                if L1:
                    def _(sync):
                        b.inc(sync.dma_start(out=wstage[0:8, :],
                                             in_=dm["bTall"][:, 0:D]), "sL", 16)
                        return None
                    b.sec("sync", [], _)
                    mw = b.mark("sL")

                    def _(v):
                        return v.tensor_scalar(bT_cur[0:8, :], wstage[0:8, :],
                                               0.125, None, ALU.mult)
                    mbtQ = b.sec("vector", [mw], _)
                qwsrc = dm["wqkvT"] if not L1 else dm["wqT"]
                qcv = []

                def emit_qchunks(ics):
                    for ic in ics:
                        def _(sync, ic=ic):
                            b.inc(sync.dma_start(
                                out=wbf[:, ic * 1024:(ic + 1) * 1024],
                                in_=qwsrc[ic * 128:(ic + 1) * 128, 0:D]), "sL", 16)
                            return None
                        b.sec("sync", [mbias] if ic == 0 else [], _)
                        qcv.append(b.mark("sL"))

                b.tag = "ln"
                mxn = []
                mu_q = {}
                mu_v = {}
                mu_prev = [None]
                qev = []
                mlast_q = [None]

                def emit_qtb(tb):
                    if L1:
                        for (abf, wA, dsts, mud) in (
                            (a1_bf, 64, (u_q, u_k), mu_q),
                            (av_bf, 8, (u_v, None), mu_v),
                        ):
                            def _(t, abf=abf, wA=wA, tb=tb):
                                i = None
                                for ic in range(NCH):
                                    i = t.matmul(ps[0:wA, 3072:3584],
                                                 abf[:, ic * wA:(ic + 1) * wA],
                                                 xn_sb[:, ic * T + tb * QC: ic * T + (tb + 1) * QC],
                                                 start=(ic == 0), stop=(ic == NCH - 1))
                                return i
                            w = [mxn[tb]] + ([mu_prev[0]] if mu_prev[0] else [])
                            mpe = b.sec("tensor", w, _)

                            def _(v, dsts=dsts, tb=tb):
                                i = v.tensor_copy(dsts[0][0:8, tb * QC:(tb + 1) * QC],
                                                  ps[0:8, 3072:3584])
                                if dsts[1] is not None:
                                    i = v.tensor_copy(dsts[1][0:8, tb * QC:(tb + 1) * QC],
                                                      ps[32:40, 3072:3584])
                                return i
                            mud[tb] = mu_prev[0] = b.sec("vector", [mpe], _)
                    for oc in range(NCH):
                        bk = (tb * NCH + oc) % 2

                        def _(t, tb=tb, oc=oc, bk=bk):
                            i = None
                            for ic in range(NCH):
                                i = t.matmul(bank[bk], wbf[:, ic * 1024 + oc * 128: ic * 1024 + (oc + 1) * 128],
                                             xn_sb[:, ic * T + tb * QC: ic * T + (tb + 1) * QC],
                                             start=(ic == 0), stop=(not L1 and ic == NCH - 1))
                            if L1:
                                i = t.matmul(bank[bk], bT_cur[0:8, oc * 128:(oc + 1) * 128],
                                             u_q[0:8, tb * QC:(tb + 1) * QC], start=False, stop=True)
                            return i
                        w = [qcv[-1], mxn[tb]] + ([qev[-2]] if len(qev) >= 2 else [])
                        if L1:
                            w.append(mu_q[tb])
                            if tb == 0 and oc == 0 and mbtQ is not None:
                                w.append(mbtQ)
                        mpe = b.sec("tensor", w, _)
                        mlast_q[0] = mpe

                        def _(v, tb=tb, oc=oc, bk=bk):
                            return v.tensor_scalar(
                                q_sb[:, oc * T + tb * QC: oc * T + (tb + 1) * QC],
                                bank[bk], bias24[:, oc: oc + 1], None, ALU.add)
                        qev.append(b.sec("vector", [mpe], _))
                mx_used = None
                mxu = {}
                mst_prev = [None]
                mbc_prev = [None]
                mrb_prev = [None]
                for tb in range(NQC):
                    xb = x_blk if tb % 2 == 0 else x_blk1

                    if tb == 0 and "mx0" in pre:
                        mxl = pre.pop("mx0")
                    else:
                        def _(sync, tb=tb, xb=xb):
                            src = xin[seq].rearrange("(c p) t -> p c t", p=128)[
                                :, :, tb * QC:(tb + 1) * QC]
                            b.inc(sync.dma_start(
                                out=xb[:].rearrange("p (c t) -> p c t", c=NCH), in_=src), "sS", 16)
                            return None
                        w = [mxu[tb - 2]] if tb >= 2 else []
                        b.sec("sync2", w, _)
                        mxl = b.mark("sS")

                    if tb == 0:
                        emit_qchunks((0, 1, 2, 3))
                    elif tb == 1:
                        emit_qchunks((4, 5, 6, 7))

                    msq = []
                    mpe_list = []
                    for ic in range(NCH):
                        def _(s, ic=ic, xb=xb):
                            return s.activation(
                                xsq_c[:, (ic % 2) * 512:(ic % 2 + 1) * 512],
                                xb[:, ic * 512:(ic + 1) * 512], AF.Square)
                        w = [mxl] + ([mpe_list[ic - 2]] if ic >= 2 else [])
                        msq.append(b.sec("scalar", w, _))

                        def _(t, ic=ic, tb=tb, xb=xb, mstv=(mst_prev[0] if mst_prev[0] is not None else None)):
                            if ic == 0 and mstv is not None:
                                t.wait_ge(b.sems["sV"], mstv[1])
                            t.matmul(ps[0:1, 0:512], ones_r[:], xb[:, ic * 512:(ic + 1) * 512],
                                     start=(ic == 0), stop=(ic == NCH - 1))
                            return t.matmul(ps[0:1, 512:1024], ones_r[:],
                                            xsq_c[:, (ic % 2) * 512:(ic % 2 + 1) * 512],
                                            start=(ic == 0), stop=(ic == NCH - 1))
                        mpe_list.append(b.sec("tensor", [mxl, msq[ic]], _))
                    mpe_stats = mpe_list[-1]

                    def _(v, tb=tb):
                        ms = st_tmp[0:1, 0:512]
                        v.tensor_scalar(ms, ps[0:1, 0:512], 1.0 / D, None, ALU.mult)
                        v.tensor_scalar(st_tmp[0:1, 512:1024], ps[0:1, 512:1024], 1.0 / D, None, ALU.mult)
                        v.tensor_tensor(scr2[0:1, 0:512], ms, ms, ALU.mult)
                        return v.tensor_tensor(st_tmp[0:1, 512:1024], st_tmp[0:1, 512:1024],
                                               scr2[0:1, 0:512], ALU.subtract)
                    mst = b.sec("vector", [mpe_stats] + ([mbc_prev[0]] if mbc_prev[0] else []), _)
                    mst_prev[0] = mst

                    def _(s, tb=tb):
                        s.activation(scr1[0:1, :], st_tmp[0:1, 512:1024], AF.Ln,
                                     bias=eps_sb[:], scale=1.0)
                        return s.activation(r_r[0:1, :],
                                            scr1[0:1, :], AF.Exp, scale=-0.5)
                    mr_ = b.sec("scalar", [mst] + ([mrb_prev[0]] if mrb_prev[0] else []), _)

                    def _(v, tb=tb):
                        return v.tensor_tensor(mr_r[0:1, :],
                                               st_tmp[0:1, 0:512],
                                               r_r[0:1, :].bitcast(F32), ALU.mult)
                    mmr = b.sec("vector", [mr_], _)
                    mbc_prev[0] = mmr

                    def _(t, tb=tb):
                        t.matmul(ps[:, 1024:1536], ones_rr[:], r_r[0:1, :], start=True, stop=True)
                        return t.matmul(ps[:, 1536:2048], ones_rr[:], mr_r[0:1, :],
                                        start=True, stop=True)
                    mrb = b.sec("tensor", [mmr] + ([mx_used] if mx_used else []), _)
                    mrb_prev[0] = mrb

                    def _(v, tb=tb, xb=xb):
                        i = None
                        for ic in range(NCH):
                            v.tensor_tensor(scr1[:], xb[:, ic * 512:(ic + 1) * 512],
                                            ps[:, 1024:1536], ALU.mult)
                            i = v.tensor_tensor(
                                xn_sb[:, ic * T + tb * QC: ic * T + (tb + 1) * QC],
                                scr1[:], ps[:, 1536:2048], ALU.subtract)
                        return i
                    mx_used = b.sec("vector", [mrb], _)
                    mxu[tb] = mx_used
                    mxn.append(mx_used)
                    if tb >= 1:
                        emit_qtb(tb - 1)

                emit_qtb(NQC - 1)
                mLN = mx_used
                if DEBUG and seq == 0 and lyr == layers[0] and not dbg_done[0]:
                    def _(g):
                        b.inc(g.dma_start(out=dm["dbg_p2"], in_=fv_sb[:]), "sG", 16)
                        return None
                    b.sec("gpsimd", [mLN], _)

                b.tag = "qkv"
                if PHASES < 2:
                    b.barrier()
                    continue
                mQK = qev[-1]
                mlast_mm = mlast_q[0]
                mbt = None
                if L1:
                    def _(sync):
                        b.inc(sync.dma_start(out=wstage[0:8, :],
                                             in_=dm["bTall"][:, D:2 * D]), "sL", 16)
                        return None
                    b.sec("sync", [mQK, qcv[6]], _)
                    mw = b.mark("sL")

                    def _(v):
                        return v.tensor_scalar(bT_cur[0:8, :], wstage[0:8, :],
                                               0.125, None, ALU.mult)
                    mbt = b.sec("vector", [mw, mQK], _)
                wsrc = dm["wqkvT"] if not L1 else dm["wkT"]
                coff = D if not L1 else 0
                mcv = []
                for ic in range(NCH):
                    def _(sync, ic=ic, wsrc=wsrc, coff=coff):
                        b.inc(sync.dma_start(
                            out=wbf[:, ic * 1024:(ic + 1) * 1024],
                            in_=wsrc[ic * 128:(ic + 1) * 128, coff:coff + D]), "sL", 16)
                        return None
                    b.sec("sync", [mlast_mm], _)
                    mcv.append(b.mark("sL"))

                mev = []
                for tb in range(NQC):
                    for oc in range(NCH):
                        bk = (tb * NCH + oc) % 2

                        def _(t, tb=tb, oc=oc, bk=bk):
                            i = None
                            for ic in range(NCH):
                                i = t.matmul(bank[bk], wbf[:, ic * 1024 + oc * 128: ic * 1024 + (oc + 1) * 128],
                                             xn_sb[:, ic * T + tb * QC: ic * T + (tb + 1) * QC],
                                             start=(ic == 0), stop=(not L1 and ic == NCH - 1))
                            if L1:
                                i = t.matmul(bank[bk], bT_cur[0:8, oc * 128:(oc + 1) * 128],
                                             u_k[0:8, tb * QC:(tb + 1) * QC], start=False, stop=True)
                            return i
                        w = [mcv[-1], mxn[tb]] + ([mev[-2]] if len(mev) >= 2 else [])
                        if L1:
                            w.append(mu_q[tb])
                            if tb == 0 and oc == 0 and mbt is not None:
                                w.append(mbt)
                        mpe = b.sec("tensor", w, _)
                        mlast_mm = mpe

                        def _(v, tb=tb, oc=oc, bk=bk):
                            return v.tensor_scalar(
                                k_sb[:, oc * T + tb * QC: oc * T + (tb + 1) * QC],
                                bank[bk], bias24[:, 8 + oc: 9 + oc], None, ALU.add)
                        mev.append(b.sec("vector", [mpe], _))
                mQK = mev[-1]

                mbt = None
                if QKV_SUB < 3:
                    b.barrier()
                    continue
                mcvK = mcv
                if L1:
                    def _(sync):
                        b.inc(sync.dma_start(out=wstage[0:8, :],
                                             in_=dm["bTall"][:, 2 * D:3 * D]), "sL", 16)
                        return None
                    b.sec("sync", [mQK, mcvK[6]], _)
                    mw = b.mark("sL")

                    def _(v):
                        return v.tensor_scalar(bT_cur[0:8, :], wstage[0:8, :], 0.125, None, ALU.mult)
                    mbt = b.sec("vector", [mw, mQK], _)
                mcv = []
                for ic in range(NCH):
                    def _(sync, ic=ic):
                        wsrc = dm["wqkvT"] if not L1 else dm["wvT"]
                        coff = 2 * D if not L1 else 0
                        b.inc(sync.dma_start(
                            out=wbf[:, ic * 1024:(ic + 1) * 1024],
                            in_=wsrc[ic * 128:(ic + 1) * 128, coff:coff + D]), "sL", 16)
                        return None
                    b.sec("sync", [mlast_mm], _)
                    mcv.append(b.mark("sL"))

                if DEBUG and seq == 0 and lyr == layers[0] and not dbg_done[0]:
                    def _(g):
                        b.inc(g.dma_start(out=dm["dbg_p3"], in_=fv_sb[:]), "sG", 16)
                        return None
                    b.sec("gpsimd", [mcv[-1] if mcv else mLN], _)

                def _(sync):
                    b.inc(sync.dma_start(out=scr2[:], in_=dm["bvrep"][lyr, :, 0:512]), "sL", 16)
                    b.inc(sync.dma_start(out=scr3[:], in_=dm["bvrep"][lyr, :, 512:1024]), "sL", 16)
                    return None
                b.sec("sync", [mLN], _)
                mvb = b.mark("sL")
                if DEBUG and seq == 0 and lyr == layers[0] and not dbg_done[0]:
                    def _(g):
                        b.inc(g.dma_start(out=dm["dbg_fv1"], in_=fv_sb[:]), "sG", 16)
                        b.inc(g.dma_start(out=dm["dbg_fvdram"], in_=dm["fvrep"]), "sG", 16)
                        return None
                    b.sec("gpsimd", [mvb], _)

                mev = []
                mfb = {}
                maug = []
                for tt in range(NKT):
                    if L1:
                        def _(sync, tt=tt):
                            b.inc(sync.dma_start(
                                out=fbb_sb[:, (tt % 2) * D:(tt % 2 + 1) * D],
                                in_=dm["fexp"][tt]), "sS", 16)
                            return None
                        w = [maug[tt - 2]] if tt >= 2 else []
                        b.sec("sync2", w, _)
                        mfl = b.mark("sS")
                    for oh in range(2):
                        bk = (tt * 2 + oh) % 2

                        def _(t, tt=tt, oh=oh, bk=bk):
                            i = None
                            for ic in range(NCH):
                                i = t.matmul(bank[bk],
                                             xn_sb[:, ic * T + tt * 128: ic * T + (tt + 1) * 128],
                                             wbf[:, ic * 1024 + oh * 512: ic * 1024 + (oh + 1) * 512],
                                             start=(ic == 0), stop=(not L1 and ic == NCH - 1))
                            if L1:
                                i = t.matmul(bank[bk], u_v[0:8, tt * 128:(tt + 1) * 128],
                                             bT_cur[0:8, oh * 512:(oh + 1) * 512],
                                             start=False, stop=True)
                            return i
                        w = [mcv[-1], mvb, mxn[tt // 4]] + ([mev[-2]] if len(mev) >= 2 else [])
                        if L1:
                            w.append(mu_v[tt // 4])
                            if tt == 0 and oh == 0:
                                w.append(mbt)
                        mpe = b.sec("tensor", w, _)

                        vdst = v_sb[:].rearrange("p (kt h e) -> p kt h e", kt=NKT, h=NH)[
                            :, tt, oh * 8:(oh + 1) * 8, 0:64]
                        psv = bank[bk].rearrange("p (h e) -> p h e", h=8)

                        if not L1:
                            def _(v, vdst=vdst, psv=psv, oh=oh):
                                vb = scr2 if oh == 0 else scr3
                                if NO_VEVAC:
                                    return v.tensor_copy(scr1[:], vb[:])
                                return v.tensor_tensor(
                                    vdst, psv, vb[:].rearrange("p (h e) -> p h e", h=8),
                                    ALU.add)
                            mev.append(b.sec("vector", [mpe], _))
                        else:
                            def _(v, vdst=vdst, psv=psv, oh=oh, tt=tt):
                                vb = scr2 if oh == 0 else scr3
                                if NO_VEVAC:
                                    return v.tensor_copy(scr1[:], vb[:])
                                v.tensor_tensor(scr1[:], bank[(tt * 2 + oh) % 2],
                                                vb[:], ALU.add)
                                return v.tensor_tensor(
                                    vdst, scr1[:].rearrange("p (h e) -> p h e", h=8),
                                    fbb_sb[:, (tt % 2) * D + oh * 512:(tt % 2) * D + (oh + 1) * 512]
                                    .rearrange("p (h e) -> p h e", h=8), ALU.mult)
                            mev.append(b.sec("vector", [mpe, mfl], _))
                    if L1:
                        def _(v, tt=tt):
                            vaug = v_sb[:].rearrange("p (kt h e) -> p kt h e", kt=NKT, h=NH)[
                                :, tt, :, 64:65]
                            srcv = fbb_sb[:, (tt % 2) * D:(tt % 2) * D + 1024].rearrange(
                                "p (h e) -> p h e", h=16)[:, :, 0:1]
                            return v.tensor_copy(vaug, srcv)
                        maug.append(b.sec("vector", [mev[-1], mfl], _))
                        mfb[tt] = maug[-1]

                if QKV_SUB < 4:
                    b.barrier()
                    continue
                if not L1:
                    def _(v):
                        vaug = v_sb[:].rearrange("p (kt h e) -> p kt h e", kt=NKT, h=NH)[:, :, :, 64:65]
                        return v.memset(vaug, 1.0)
                    mV = b.sec("vector", [mev[-1]], _)
                else:
                    mV = maug[-1]
                if DEBUG and seq == 0 and lyr == layers[0] and not dbg_done[0]:
                    def _(g):
                        b.inc(g.dma_start(out=dm["dbg_fv2"], in_=fv_sb[:]), "sG", 16)
                        return None
                    b.sec("gpsimd", [mV], _)

                b.barrier()

                b.tag = "attn"
                if PHASES < 3:
                    b.barrier()
                    continue
                halves = []
                cellct = 0
                for qc in range(NQC):
                    for pc in range(NH // 2):
                        ktmax = NKT if L1 else 4 * qc + 4
                        nhv = (ktmax + 7) // 8
                        for hi in range(nhv):
                            kts = list(range(hi * 8, min(ktmax, hi * 8 + 8)))
                            halves.append((pc, qc, hi, kts, cellct,
                                           hi == nhv - 1, ktmax))
                        cellct += 1

                mexpk = {}
                mqkk = {}
                myev = {}
                mlt = {}
                mlt2 = {}
                mys = {}
                mys2 = {}
                mrec_qc = {}
                pvms = []
                g_ = [0]
                mpv_last = [None]

                def emit_qk(hx, half):
                    pc, qc, hi, kts, cc, islast, ktmax = half
                    buf = (hx % 2) * 8192
                    ch = pc
                    for kt in kts:
                        g = g_[0]
                        s_ = g % 3

                        def _(t, kt=kt, ch=ch, qc=qc, s_=s_, first=(kt == kts[0])):
                            if first and DRAINS:
                                t.drain()
                            t.matmul(
                                ps[:, s_ * 1024: s_ * 1024 + 512],
                                k_sb[0:64, ch * T + kt * 128: ch * T + (kt + 1) * 128],
                                q_sb[0:64, ch * T + qc * QC: ch * T + (qc + 1) * QC],
                                start=True, stop=True, tile_position=(0, 0))
                            return t.matmul(
                                ps[:, s_ * 1024 + 512: (s_ + 1) * 1024],
                                k_sb[64:128, ch * T + kt * 128: ch * T + (kt + 1) * 128],
                                q_sb[64:128, ch * T + qc * QC: ch * T + (qc + 1) * QC],
                                start=True, stop=True, tile_position=(64, 0))
                        w = []
                        if hx == 0 and kt == kts[0]:
                            w += [mV, mQK]
                        if g >= 3:
                            w.append(mexpk[g - 3])
                        mqkk[g] = b.sec("tensor", w, _)

                        j = kt - hi * 8

                        def _(s, g=g, s_=s_, buf=buf, j=j):
                            return s.activation(
                                p1_sb[:, buf + j * 1024: buf + (j + 1) * 1024],
                                ps[:, s_ * 1024:(s_ + 1) * 1024],
                                AF.Exp, scale=0.125)
                        mexpk[g] = b.sec("scalar", [mqkk[g]], _)
                        g_[0] += 1

                    mmask = None
                    if not L1:
                        dk = [kt for kt in kts if 4 * qc <= kt < 4 * qc + 4]
                        if dk:
                            def _(v, dk=dk, hi=hi, qc=qc, buf=buf):
                                i = None
                                for kt in dk:
                                    j = kt - hi * 8
                                    sl = p1_sb[:, buf + j * 1024: buf + (j + 1) * 1024]
                                    i = v.tensor_tensor(
                                        sl, sl,
                                        masks_sb[:, (kt - 4 * qc) * 1024:
                                                 (kt - 4 * qc + 1) * 1024], ALU.mult)
                                return i
                            mmask = b.sec("vector", [mexpk[g_[0] - 1]], _)
                    return mmask

                def emit_pv(hx, half, mmask, mlastexp):
                    pc, qc, hi, kts, cc, islast, ktmax = half
                    buf = (hx % 2) * 8192
                    pvm = None
                    for hh in range(2):
                        yb = 6 + hh
                        H = pc * 2 + hh

                        def _(t, hh=hh, yb=yb, H=H, kts=kts, hi=hi, ktmax=ktmax, buf=buf):
                            if hh == 0 and DRAINS:
                                t.drain()
                            i = None
                            for kt in kts:
                                j = kt - hi * 8
                                i = t.matmul(
                                    ps[0:65, yb * 512:(yb + 1) * 512],
                                    v_sb[:, kt * VS + H * 65: kt * VS + (H + 1) * 65],
                                    p1_sb[:, buf + j * 1024 + hh * 512:
                                          buf + j * 1024 + (hh + 1) * 512],
                                    start=(kt == 0), stop=(kt == ktmax - 1),
                                    skip_group_check=True)
                            return i
                        w = [mlastexp]
                        if mmask is not None:
                            w.append(mmask)
                        if hi == 0 and yb in myev:
                            w.append(myev[yb])
                        pvm = b.sec("tensor", w, _)
                    mpv_last[0] = pvm
                    pvms.append(pvm)

                    if islast:
                        p_ = cc % 2
                        ystg = ystgA if p_ == 0 else ystgB
                        for hh in range(2):
                            yb = 6 + hh
                            H = pc * 2 + hh

                            def _(v, yb=yb, hh=hh, p_=p_, ystg=ystg):
                                v.tensor_copy(ltmp[0:1, p_ * QC:(p_ + 1) * QC],
                                              ps[64:65, yb * 512:(yb + 1) * 512])
                                return v.tensor_copy(ystg[:, hh * 512:(hh + 1) * 512],
                                                     ps[0:64, yb * 512:(yb + 1) * 512])
                            w = [pvm]
                            if p_ in mlt2:
                                w.append(mlt2[p_])
                            if (p_, hh) in mys2:
                                w.append(mys2[(p_, hh)])
                            myev[yb] = b.sec("vector", w, _)

                            def _(sync, hh=hh, H=H, qc=qc, p_=p_):
                                b.inc(sync.dma_start(out=dm["lscr"][qc, H].unsqueeze(0),
                                                     in_=ltmp[0:1, p_ * QC:(p_ + 1) * QC]), "sL", 16)
                                return None
                            b.sec("sync", [myev[yb]], _)
                            mlt[0] = mlt2[p_] = b.mark("sL")

                            def _(g, hh=hh, H=H, qc=qc, ystg=ystg):
                                b.inc(g.dma_start(
                                    out=dm["ybuf"][H * 64:(H + 1) * 64, qc * QC:(qc + 1) * QC],
                                    in_=ystg[:, hh * 512:(hh + 1) * 512]), "sG", 16)
                                return None
                            b.sec("gpsimd", [myev[yb]], _)
                            mys[hh] = mys2[(p_, hh)] = b.mark("sG")
                    if islast and pc == NH // 2 - 1 and qc < 3:
                        ro = (0, 32, 64)[qc]

                        def _(g, qc=qc):
                            b.inc(g.dma_start(out=scr3[0:16, :],
                                              in_=dm["lscr"][qc]), "sG", 16)
                            return None
                        w = [mlt[0]] + ([mrec_qc[qc - 1]] if qc > 0 else [])
                        b.sec("gpsimd", w, _)
                        mlld = b.mark("sG")

                        def _(v, ro=ro):
                            sl = scr3[0:16, :]
                            v.reciprocal(sl, sl)
                            return v.tensor_copy(rpAr[ro:ro + 16, :], sl)
                        mrec_qc[qc] = b.sec("vector", [mlld], _)

                pend = None
                for hx, half in enumerate(halves):
                    mmask = emit_qk(hx, half)
                    mlastexp = mexpk[g_[0] - 1]
                    if pend is not None:
                        emit_pv(*pend)
                    pend = (hx, half, mmask, mlastexp)
                emit_pv(*pend)
                mcell_pv = mpv_last[0]

                mbuf0_free = pvms[-2]
                mcv = []
                for ic in range(NCH):
                    def _(sync, ic=ic):
                        wsrc = dm["wprojT"] if not L1 else dm["woT"]
                        b.inc(sync.dma_start(
                            out=wbf[:, ic * 1024:(ic + 1) * 1024],
                            in_=wsrc[ic * 128:(ic + 1) * 128, :]), "sW", 16)
                        return None
                    b.sec("sync", [mbuf0_free], _)
                    mcv.append(b.mark("sW"))

                def _(sync):
                    b.inc(sync.dma_start(
                        out=yblk[:].rearrange("p (c t) -> p c t", c=NCH),
                        in_=dm["ybuf"].rearrange("(c p) t -> p c t", p=128)[
                            :, :, 0:QC]), "sW", 16)
                    return None
                b.sec("sync2", [mcell_pv, mys2[(0, 0)], mys2[(0, 1)],
                                mys2[(1, 0)], mys2[(1, 1)]], _)
                myld0 = b.mark("sW")

                def _(sync):
                    src = xin[seq].rearrange("(c p) t -> p c t", p=128)[:, :, 0:QC]
                    b.inc(sync.dma_start(
                        out=x_blk[:].rearrange("p (c t) -> p c t", c=NCH), in_=src), "sW", 16)
                    return None
                b.sec("sync2", [mcell_pv], _)
                mxf0 = b.mark("sW")

                if DEBUG and seq == 0 and lyr == layers[0] and not dbg_done[0]:
                    def _(g):
                        b.inc(g.dma_start(out=dm["dbg_xn"], in_=xn_sb[:]), "sG", 16)
                        b.inc(g.dma_start(out=dm["dbg_q"], in_=q_sb[:]), "sG", 16)
                        b.inc(g.dma_start(out=dm["dbg_k"], in_=k_sb[:]), "sG", 16)
                        b.inc(g.dma_start(out=dm["dbg_v"], in_=v_sb[:]), "sG", 16)
                        b.inc(g.dma_start(out=dm["dbg_fv"], in_=fv_sb[:]), "sG", 16)
                        b.inc(g.dma_start(out=dm["dbg_lp"], in_=dm["lscr"].rearrange("a b t -> (a b) t")), "sG", 16)
                        return None
                    b.sec("gpsimd", [mlt[0], mys[0], mys[1]], _)

                b.barrier()

                ust = None
                if not L1:
                    ust = unfold_start([])
                mbt = None
                if L1:
                    def _(sync):
                        b.inc(sync.dma_start(out=wstage[0:8, :],
                                             in_=dm["bTall"][:, 3 * D:4 * D]), "sL", 16)
                        return None
                    b.sec("sync", [], _)
                    mw = b.mark("sL")

                    def _(v):
                        return v.tensor_scalar(bT_cur[0:8, :], wstage[0:8, :], 0.125, None, ALU.mult)
                    mbt = b.sec("vector", [mw], _)


                msw_hold = [None]
                sgn = {"mswc": []}

                def emit_sign_chunk(j):
                    slot = wstage if j % 2 == 0 else wstage2

                    def _(sync, j=j, slot=slot):
                        b.inc(sync.dma_start(out=slot[:],
                                             in_=dm["wfT"][j * 128:(j + 1) * 128, :]), "sL", 16)
                        return None
                    w = [mbt] if j < 2 else [sgn["mswc"][j - 2]]
                    b.sec("sync", w, _)
                    mswl = b.mark("sL")

                    def _(v, j=j, slot=slot):
                        dst = sw_bf[:, j * 1024:(j + 1) * 1024]
                        v.tensor_scalar(dst, slot[:], 0.0, 2.0, ALU.is_ge, ALU.mult)
                        return v.tensor_scalar(dst, dst, 1.0, None, ALU.subtract)
                    sgn["mswc"].append(b.sec("vector", [mswl], _))

                mev = []
                myld = None
                muo_prev = [None]
                mpe_tb_last = {}
                for tb in range(NQC):
                    ybt = yblk if tb % 2 == 0 else yblk2
                    if tb == 0:
                        myld = myld0
                    else:
                        def _(sync, tb=tb, ybt=ybt):
                            b.inc(sync.dma_start(
                                out=ybt[:].rearrange("p (c t) -> p c t", c=NCH),
                                in_=dm["ybuf"].rearrange("(c p) t -> p c t", p=128)[
                                    :, :, tb * QC:(tb + 1) * QC]), "sS", 16)
                            return None
                        w = [mpe_tb_last[tb - 2]] if tb >= 2 else []
                        b.sec("sync2", w, _)
                        myld = b.mark("sS")

                    if tb == 3:
                        def _(g):
                            b.inc(g.dma_start(out=scr3[0:16, :],
                                              in_=dm["lscr"][3]), "sG", 16)
                            return None
                        b.sec("gpsimd", [mrn[-1], mrec_qc[2]], _)
                        mlld3 = b.mark("sG")

                        def _(v):
                            sl = scr3[0:16, :]
                            v.reciprocal(sl, sl)
                            return v.tensor_copy(rpAr[64:80, :], sl)
                        mrec_qc[3] = b.sec("vector", [mlld3], _)

                    mrn = []
                    for ic in range(NCH):
                        half = ic % 2

                        def _(t, ic=ic, half=half, tb=tb):
                            sb = (0, 32, 64, 64)[tb]
                            return t.matmul(
                                ps[:, (4 + half) * 512:(5 + half) * 512],
                                sel_r[sb:sb + 32, ic * 128:(ic + 1) * 128],
                                rpAr[sb:sb + 32, :],
                                start=True, stop=True)
                        w = [myld, mrec_qc[tb]] + ([mrn[-2]] if len(mrn) >= 2 else [])
                        mbc = b.sec("tensor", w, _)

                        def _(v, ic=ic, half=half, ybt=ybt):
                            sl = ybt[:, ic * 512:(ic + 1) * 512]
                            return v.scalar_tensor_tensor(
                                sl, sl, 1.0, ps[:, (4 + half) * 512:(5 + half) * 512],
                                ALU.mult, ALU.mult)
                        mrn.append(b.sec("vector", [mbc], _))

                    if DEBUG and seq == 0 and lyr == layers[0] and tb == 1 and not dbg_done[0]:
                        def _(g):
                            b.inc(g.dma_start(out=dm["dbg_yblk"], in_=yblk[:]), "sG", 16)
                            b.inc(g.dma_start(out=dm["dbg_rp"], in_=rpAr[:].bitcast(F32)), "sG", 16)
                            return None
                        b.sec("gpsimd", [mrn[-1]], _)

                    if L1:
                        emit_sign_chunk(2 * tb)
                        emit_sign_chunk(2 * tb + 1)
                    else:
                        if tb < 2:
                            if tb == 0:
                                unfold_tr(ust, 0)
                            unfold_cc(ust, 0, 2 * tb)
                            unfold_cc(ust, 0, 2 * tb + 1)
                            if tb == 1:
                                unfold_tr(ust, 1)
                        else:
                            unfold_cc(ust, 1, 2 * (tb - 2))
                            unfold_cc(ust, 1, 2 * (tb - 2) + 1)
                    muo = None
                    if L1:
                        def _(t, tb=tb, ybt=ybt):
                            i = None
                            for ic in range(NCH):
                                i = t.matmul(ps[0:8, 1024:1536], ao_bf[:, ic * 8:(ic + 1) * 8],
                                             ybt[:, ic * 512:(ic + 1) * 512],
                                             start=(ic == 0), stop=(ic == NCH - 1))
                            return i
                        mpe = b.sec("tensor", [mrn[-1]] + ([muo_prev[0]] if muo_prev[0] else []), _)

                        def _(v, tb=tb):
                            return v.tensor_copy(u_o[0:8, tb * QC:(tb + 1) * QC], ps[0:8, 1024:1536])
                        muo = b.sec("vector", [mpe], _)
                        muo_prev[0] = muo

                    for oc in range(NCH):
                        bk = (tb * NCH + oc) % 2

                        def _(t, tb=tb, oc=oc, bk=bk, ybt=ybt):
                            i = None
                            for ic in range(NCH):
                                i = t.matmul(bank[bk], wbf[:, ic * 1024 + oc * 128: ic * 1024 + (oc + 1) * 128],
                                             ybt[:, ic * 512:(ic + 1) * 512],
                                             start=(ic == 0), stop=(not L1 and ic == NCH - 1))
                            if L1:
                                i = t.matmul(bank[bk], bT_cur[0:8, oc * 128:(oc + 1) * 128],
                                             u_o[0:8, tb * QC:(tb + 1) * QC], start=False, stop=True)
                            return i
                        w = [mrn[-1], mcv[-1]] + ([muo] if muo else []) + ([mev[-2]] if len(mev) >= 2 else [])
                        mpe = b.sec("tensor", w, _)
                        mpe_tb_last[tb] = mpe

                        def _(v, tb=tb, oc=oc, bk=bk):
                            dst = t1_sb[:, oc * T + tb * QC: oc * T + (tb + 1) * QC]
                            if not L1:
                                return v.tensor_scalar(dst, bank[bk],
                                                       bias24[:, 16 + oc: 17 + oc], None, ALU.add)
                            return v.tensor_scalar(dst, bank[bk],
                                                   bias24[:, 16 + oc: 17 + oc], rz_sb[:],
                                                   ALU.add, ALU.mult)
                        mev.append(b.sec("vector", [mpe], _))
                msw_hold[0] = sgn["mswc"][-1] if L1 else ust["mprev"]
                mPROJ = mev[-1]

                if DEBUG and seq == 0 and lyr == layers[0] and not dbg_done[0]:
                    def _(g):
                        b.inc(g.dma_start(out=dm["dbg_t1"], in_=t1_sb[:]), "sG", 16)
                        return None
                    b.sec("gpsimd", [mPROJ], _)
                    def _(g):
                        b.inc(g.dma_start(out=dm["dbg_y"], in_=dm["ybuf"]), "sG", 16)
                        return None
                    b.sec("gpsimd", [mPROJ], _)

                b.barrier()

                b.tag = "ffn"
                if PHASES < 5:
                    b.barrier()
                    continue
                msw = msw_hold[0]
                mst_out = []
                mfin_tb = {}
                mfin_hist = []
                mfin_prev = [None]
                mg_prev = [None]
                mcl_hist = []
                for tb in range(NQC):
                    xb = x_blk if tb % 2 == 0 else x_blk1

                    if tb == 0:
                        mxl = mxf0
                    else:
                        def _(sync, tb=tb, xb=xb):
                            src = xin[seq].rearrange("(c p) t -> p c t", p=128)[
                                :, :, tb * QC:(tb + 1) * QC]
                            b.inc(sync.dma_start(
                                out=xb[:].rearrange("p (c t) -> p c t", c=NCH), in_=src), "sS", 16)
                            return None
                        w = [mfin_tb[tb - 2]] if tb >= 2 else []
                        b.sec("sync2", w, _)
                        mxl = b.mark("sS")

                    for po in range(4):
                        oc0 = 2 * po
                        bks = (po % 2) * 2

                        def _(t, tb=tb, oc0=oc0, bks=bks):
                            i = None
                            for j in range(2):
                                oc = oc0 + j
                                if not L1:
                                    for ic in range(NCH):
                                        if oc < 4:
                                            wt = (wr_bf if ic < 4 else win_bf)
                                            col = (ic % 4) * 512 + oc * 128
                                        else:
                                            wt = (wi_bf if ic < 4 else wr_bf)
                                            col = (ic % 4) * 512 + (oc - 4) * 128
                                        i = t.matmul(bank[bks + j], wt[:, col:col + 128],
                                                     xn_sb[:, ic * T + tb * QC: ic * T + (tb + 1) * QC],
                                                     start=(ic == 0), stop=(ic == NCH - 1))
                                else:
                                    for ic in range(NCH):
                                        i = t.matmul(bank[bks + j],
                                                     sw_bf[:, ic * 1024 + oc * 128: ic * 1024 + (oc + 1) * 128],
                                                     xn_sb[:, ic * T + tb * QC: ic * T + (tb + 1) * QC],
                                                     start=(ic == 0), stop=(ic == NCH - 1))
                            return i
                        w = [msw] + ([mcl_hist[-2]] if len(mcl_hist) >= 2 else [])
                        mpe = b.sec("tensor", w, _)

                        if not L1:
                            fB = ffB if len(mst_out) % 2 == 0 else ffA

                            def _(s, oc0=oc0, bks=bks, fB=fB):
                                s.activation(fB[:, 0:512], bank[bks], AF.Silu,
                                             bias=ffb_sb[:, oc0:oc0 + 1], scale=1.0)
                                return s.activation(fB[:, 512:1024], bank[bks + 1], AF.Silu,
                                                    bias=ffb_sb[:, oc0 + 1:oc0 + 2], scale=1.0)
                            w = [mpe] + ([mfin_hist[-2]] if len(mfin_hist) >= 2 else [])
                            mg = b.sec("scalar", w, _)
                            mcl_hist.append(mg)

                            fC = ffC if len(mst_out) % 2 == 0 else ffC2

                            def _(v, tb=tb, oc0=oc0, fC=fC, xb=xb, fB=fB):
                                t1p = t1_sb[:].rearrange("p (c t) -> p c t", c=NCH)[
                                    :, oc0:oc0 + 2, tb * QC:(tb + 1) * QC]
                                fv3 = fC[:].rearrange("p (c t) -> p c t", c=2)
                                v.tensor_tensor(fv3, fB[:].rearrange("p (c t) -> p c t", c=2),
                                                t1p, ALU.mult)
                                return v.tensor_tensor(fC[:], fC[:],
                                                       xb[:, oc0 * 512:(oc0 + 2) * 512], ALU.add)
                            w = [mg, mxl] + ([mst_out[-2]] if len(mst_out) >= 2 else [])
                            mfin = b.sec("vector", w, _)
                            mfin_hist.append(mfin)
                            mfin_prev[0] = mfin
                        else:
                            def _(v, oc0=oc0, bks=bks):
                                v.tensor_scalar(ffA[:, 0:512], bank[bks], fsc_sb[:, oc0:oc0 + 1],
                                                fbi_sb[:, oc0:oc0 + 1], ALU.mult, ALU.add)
                                v.tensor_scalar(ffA[:, 512:1024], bank[bks + 1],
                                                fsc_sb[:, oc0 + 1:oc0 + 2],
                                                fbi_sb[:, oc0 + 1:oc0 + 2], ALU.mult, ALU.add)
                                return v.tensor_scalar(ffA[:], ffA[:], -100.0, 100.0,
                                                       ALU.max, ALU.min)
                            w = [mpe] + ([mg_prev[0]] if mg_prev[0] else [])
                            mcl = b.sec("vector", w, _)
                            mcl_hist.append(mcl)

                            def _(s):
                                return s.activation(ffB[:], ffA[:], AF.Silu)
                            w = [mcl] + ([mfin_prev[0]] if mfin_prev[0] else [])
                            mg = b.sec("scalar", w, _)
                            mg_prev[0] = mg

                            fC = ffC if len(mst_out) % 2 == 0 else ffC2

                            def _(v, tb=tb, oc0=oc0, fC=fC, xb=xb):
                                fv3 = fC[:].rearrange("p (c t) -> p c t", c=2)
                                xn3 = xn_sb[:].rearrange("p (c t) -> p c t", c=NCH)[
                                    :, oc0:oc0 + 2, tb * QC:(tb + 1) * QC]
                                t1p = t1_sb[:].rearrange("p (c t) -> p c t", c=NCH)[
                                    :, oc0:oc0 + 2, tb * QC:(tb + 1) * QC]
                                v.tensor_tensor(fv3, xn3, t1p, ALU.add)
                                v.tensor_tensor(fC[:], ffB[:], fC[:], ALU.mult)
                                return v.tensor_tensor(fC[:], fC[:],
                                                       xb[:, oc0 * 512:(oc0 + 2) * 512], ALU.add)
                            w = [mg, mxl] + ([mst_out[-2]] if len(mst_out) >= 2 else [])
                            mfin = b.sec("vector", w, _)
                            mfin_prev[0] = mfin

                        def _(g, tb=tb, oc0=oc0, fC=fC):
                            b.inc(g.dma_start(
                                out=xout[seq, oc0 * 128:(oc0 + 2) * 128,
                                         tb * QC:(tb + 1) * QC].rearrange(
                                             "(c p) t -> p c t", p=128),
                                in_=fC[:].rearrange("p (c t) -> p c t", c=2)), "sG", 16)
                            return None
                        b.sec("gpsimd", [mfin], _)
                        mst_out.append(b.mark("sG"))
                    mfin_tb[tb] = mfin

                if not (seq == NSEQ - 1 and lyr == layers[-1]):
                    nseq, nlyr = (seq, layers[-1]) if lyr == layers[0] and len(layers) > 1 else (seq + 1, layers[0])

                    def _(sync, nseq=nseq, nlyr=nlyr):
                        src = xsrc[nlyr][nseq].rearrange("(c p) t -> p c t", p=128)[
                            :, :, 0:QC]
                        b.inc(sync.dma_start(
                            out=x_blk[:].rearrange("p (c t) -> p c t", c=NCH), in_=src), "sS", 16)
                        return None
                    b.sec("sync2", [mfin_tb[2], mst_out[3]], _)
                    pre["mx0"] = b.mark("sS")

                if DEBUG and seq == 0 and lyr == layers[0]:
                    dbg_done[0] = True
                b.barrier()

        def _(g):
            g.wait_ge(b.sems["sG"], b.cnt["sG"])
            return None
        b.sec("gpsimd", [], _)
        b.flush()
        nc._ledger = dict(b.cnt)
        try:
            nc._addrs = {}
            for _nm, _h in [("arenaA", arenaA), ("arenaB", arenaB), ("arenaC", arenaC),
                            ("xn_sb", xn_sb),
                            ("masks_sb", masks_sb), ("eps_sb", eps_sb), ("ones_r", ones_r),
                            ("st_tmp", st_tmp), ("r_r", r_r), ("mr_r", mr_r),
                            ("ones_rr", ones_rr),
                            ("sel_r", sel_r), ("rpAr", rpAr), ("ltmp", ltmp),
                            ("fbb_sb", fbb_sb), ("stg", stg),
                            ("bias24", bias24), ("fsc_sb", fsc_sb), ("fbi_sb", fbi_sb),
                            ("ffb_sb", ffb_sb), ("rz_sb", rz_sb), ("u_v", u_v), ("u_o", u_o),
                            ("a1_bf", a1_bf), ("av_bf", av_bf), ("ao_bf", ao_bf),
                            ("bT_cur", bT_cur), ("scr1", scr1), ("scr2", scr2), ("scr3", scr3)]:
                _m = nc.lookup_mloc(_h)
                nc._addrs[_nm] = (_m.addr, _h.shape, str(_h.dtype))
        except Exception as _e:
            nc._addrs = {"error": str(_e)}

    return nc




def _host_constants():
    slopes = 0.5 ** np.arange(NH, dtype=np.float64)
    p = np.arange(128, dtype=np.float64)
    fvals_ = np.ones((2, 256), dtype=np.float64)
    for kt in range(NKT):
        for h in range(NH):
            fvals_[1, kt * NH + h] = np.exp(slopes[h] * (128.0 * kt - 1920.0))
    fexp = np.ones((NKT, 128, D), dtype=np.float32)
    for kt in range(NKT):
        arg = slopes[None, :, None] * (128.0 * kt + p[:, None, None] - 2047.0)
        fexp[kt] = np.broadcast_to(np.exp(arg).astype(BF).astype(np.float32),
                                   (128, NH, HD)).reshape(128, D)
    masks = np.zeros((128, 8 * QC), dtype=np.float32)
    t = np.arange(QC)
    m1 = np.zeros((128, QC), dtype=np.float32)
    for j in range(4):
        m1[:] = (p[:, None] + 128 * j <= t[None, :])
        masks[:, j * 2 * QC: j * 2 * QC + QC] = m1
        masks[:, j * 2 * QC + QC: (j + 1) * 2 * QC] = m1
    repD = np.zeros((32, 4, 128), dtype=np.float32)
    for cc in range(4):
        for pp in range(128):
            repD[cc * 8 + pp // 16, cc, pp] = 1.0
    repG = np.zeros((16, 128), dtype=np.float32)
    for pp in range(128):
        repG[pp % 16, pp] = 1.0
    idn = np.eye(128, dtype=np.float32)
    sel = np.zeros((128, 8 * 128), dtype=np.float32)
    for ic in range(8):
        for p in range(128):
            for off in (0, 32, 64):
                sel[off + 2 * ic + p // 64, ic * 128 + p] = 1.0
    return {
        "idn": idn,
        "fvrep": np.ascontiguousarray(np.broadcast_to(
            fvals_.astype(BF)[:, None, :], (2, 128, 256))),
        "fexp": fexp,
        "masks": masks.astype(np.uint8), "repD": repD, "repG": repG,
        "eps_t": np.full((1, 1), 1e-5, dtype=np.float32),
        "onescol": np.ones((128, 1), dtype=np.float32),
        "onesrr": np.ones((1, 128), dtype=np.float32),
        "selr": sel,
    }


def _fm(vec):
    v = np.asarray(vec, dtype=np.float32)
    n = v.shape[0] // 128
    return np.ascontiguousarray(v.reshape(n, 128).T)


def _host_inputs(inputs, layers=(0, 1)):
    first, last = (0 in layers), (1 in layers)
    d = dict(_host_constants())
    bvrep = np.zeros((2, 128, D), dtype=np.float32)
    if first:
        Wqkv = inputs["L0_attn_W_qkv"]
        d["wqkvT"] = np.ascontiguousarray(Wqkv.T)
        bq = inputs["L0_attn_b_qkv"]
        d["bqkv_fm"] = np.concatenate([_fm(bq[0:D]), _fm(bq[D:2 * D])], axis=1)
        d["bproj_fm"] = _fm(inputs["L0_attn_b_proj"])
        d["wprojT"] = np.ascontiguousarray(inputs["L0_attn_W_proj"].T)
        g1r = inputs["L0_ffn_G1_R"][0].reshape(32, 128)
        g1i = inputs["L0_ffn_G1_I"][0].reshape(32, 128)
        d["g1"] = np.ascontiguousarray(np.concatenate([g1r, g1i], axis=1))
        g2r = inputs["L0_ffn_G2_R"][:, :, 0, :].reshape(64, 16)
        g2i = inputs["L0_ffn_G2_I"][:, :, 0, :].reshape(64, 16)
        d["g2"] = np.ascontiguousarray(np.concatenate([g2r, g2i], axis=1))
        d["fbias_fm"] = _fm(np.concatenate([inputs["L0_ffn_bias_real"],
                                            inputs["L0_ffn_bias_imag"]]))
        bvrep[0] = np.broadcast_to(bq[2 * D:3 * D], (128, D))
    if last:
        d["wqT"] = np.ascontiguousarray(inputs["L1_attn_Wq"].T)
        d["wkT"] = np.ascontiguousarray(inputs["L1_attn_Wk"].T)
        d["wvT"] = np.ascontiguousarray(inputs["L1_attn_Wv"].T)
        d["woT"] = np.ascontiguousarray(inputs["L1_attn_Wo"].T)
        d["bq_fm"] = _fm(inputs["L1_attn_bq"])
        d["bk_fm"] = _fm(inputs["L1_attn_bk"])
        d["bo_fm"] = _fm(inputs["L1_attn_bo"])
        a1 = np.zeros((D, 64), dtype=np.float32)
        a1[:, 0:8] = inputs["L1_attn_Aq"].T
        a1[:, 32:40] = inputs["L1_attn_Ak"].T
        d["a1T"] = a1
        d["avT"] = np.ascontiguousarray(inputs["L1_attn_Av"].T)
        d["aoT"] = np.ascontiguousarray(inputs["L1_attn_Ao"].T)
        d["bTall"] = np.ascontiguousarray(np.concatenate(
            [inputs["L1_attn_Bq"].T, inputs["L1_attn_Bk"].T,
             inputs["L1_attn_Bv"].T, inputs["L1_attn_Bo"].T], axis=1))
        d["rzrep"] = np.ascontiguousarray(np.broadcast_to(
            inputs["L1_attn_rezero"].reshape(1, 1), (128, 1))).astype(np.float32)
        d["wfT"] = np.ascontiguousarray(inputs["L1_ffn_weights"].T)
        d["fscale_fm"] = _fm(inputs["L1_ffn_scale"])
        d["fbias1_fm"] = _fm(inputs["L1_ffn_bias"])
        bvrep[1] = np.broadcast_to(inputs["L1_attn_bv"], (128, D))
    d["bvrep"] = bvrep
    return d


def kernel(**inputs):
    shared0 = _host_inputs(inputs, (0, 1) if MODE == "fused" else (0,))
    x = np.ascontiguousarray(inputs["L0_attn_x"], dtype=np.float32)

    if MODE == "fused":
        nc = build_program((0, 1))
        in_maps = []
        for c in range(NCORES):
            m = dict(shared0)
            m["x"] = np.ascontiguousarray(
                x[c * NSEQ:(c + 1) * NSEQ].transpose(0, 2, 1))
            in_maps.append(m)
        res = run_bass_kernel_spmd(nc, in_maps, core_ids=list(range(NCORES)))
        outs = [r["out"].transpose(0, 2, 1) for r in res.results]
        return np.ascontiguousarray(np.concatenate(outs, axis=0), dtype=np.float32)

    nc0 = build_program((0,))
    in_maps = []
    for c in range(NCORES):
        m = dict(shared0)
        m["x"] = np.ascontiguousarray(x[c * NSEQ:(c + 1) * NSEQ].transpose(0, 2, 1))
        in_maps.append(m)
    res0 = run_bass_kernel_spmd(nc0, in_maps, core_ids=list(range(NCORES)))
    xmids = [r["xmid"] for r in res0.results]

    shared1 = _host_inputs(inputs, (1,))
    nc1 = build_program((1,))
    in_maps = []
    for c in range(NCORES):
        m = dict(shared1)
        m["xmid"] = np.ascontiguousarray(xmids[c])
        in_maps.append(m)
    res1 = run_bass_kernel_spmd(nc1, in_maps, core_ids=list(range(NCORES)))
    outs = [r["out"].transpose(0, 2, 1) for r in res1.results]
    return np.ascontiguousarray(np.concatenate(outs, axis=0), dtype=np.float32)


# version 27
# speedup vs baseline: 1.2166x; 1.0082x over previous
import numpy as np
import ml_dtypes
from contextlib import ExitStack
import concourse.bass as bass
import concourse.mybir as mybir
from concourse.bass_utils import run_bass_kernel_spmd

F32 = mybir.dt.float32
F32R = mybir.dt.float32r
BF16 = mybir.dt.bfloat16
U8 = mybir.dt.uint8
AF = mybir.ActivationFunctionType
ALU = mybir.AluOpType
AX = mybir.AxisListType
BF = ml_dtypes.bfloat16

NCORES = 8
NSEQ = 2
D = 1024
T = 2048
NCH = 8
NH = 16
HD = 64
QC = 512
NQC = T // QC
NKT = T // 128
VS = NH * (HD + 1)

MODE = "fused"
PHASES = 99
QKV_SUB = 9
Q_SUB = 9
DEBUG = False
SERIALIZE = False
SER_TAGS = set()
SER_ENGS = {"tensor", "vector", "scalar", "gpsimd", "sync"}
EVAC_FULLWAIT = False
NO_AUG = False
NO_VEVAC = False
DMA_QUEUE = "gpsimd"
DRAINS = False

ENG_SEM = {"tensor": "sP", "vector": "sV", "scalar": "sA", "gpsimd": "sO", "sync": "sL"}


class _Mock:
    def __getattr__(self, name):
        return lambda *a, **kw: self
_MOCK = _Mock()


class Bld:

    def __init__(self, nc, block, sems):
        self.nc, self.block, self.sems = nc, block, sems
        self.cnt = {k: 0 for k in sems}
        self.tag = ""
        self.pend = {e: [] for e in ("tensor", "vector", "scalar", "gpsimd", "sync")}

    def inc(self, instr, s, k):
        if instr is _MOCK:
            self.cnt[s] += k
        else:
            instr.then_inc(self.sems[s], k)

    def mark(self, s):
        return (s, self.cnt[s])

    def sec(self, eng, waits, emit, sem=None):
        if eng == "sync2":
            eng = "sync"
            sem = sem or "sS"
        elif eng == "sync":
            eng = DMA_QUEUE
        """One section on `eng`: waits (markers) then emit(handle). emit returns
        the last instruction (gets +1 on the engine sem) or None (emit did its
        own incs, e.g. DMA +16s). Returns a completion marker."""
        s = sem or ENG_SEM[eng]
        me = self
        ser = []
        if SERIALIZE or (self.tag in SER_TAGS and eng in SER_ENGS):
            ser = [(ss, self.cnt[ss]) for ss in ("sL", "sG", "sP", "sV", "sA", "sO")
                   if self.cnt[ss] > 0 and ss != s]
        if emit(_MOCK) is not None:
            self.cnt[s] += 1
        wl = ser + list(waits)

        def body(h):
            seen = {}
            for (ws, wv) in wl:
                if wv > 0 and seen.get(ws, -1) < wv:
                    h.wait_ge(me.sems[ws], wv)
                    seen[ws] = wv
            last = emit(h)
            if last is not None:
                last.then_inc(me.sems[s], 1)
        self.pend[eng].append(body)
        return self.mark(s)

    def flush(self):
        for e in ("sync", "gpsimd", "tensor", "vector", "scalar"):
            fns = self.pend[e]
            if not fns:
                continue
            self.pend[e] = []

            def body(h, fns=fns):
                for f in fns:
                    f(h)
            getattr(self.block, e)(body)

    def barrier(self):
        self.flush()
        tgt = self.cnt["bar"] + 5
        nG, nL, nS = self.cnt["sG"], self.cnt["sL"], self.cnt["sS"]

        def _(h):
            h.wait_ge(self.sems["sG"], nG)
        self.block.gpsimd(_)

        def _(h):
            h.wait_ge(self.sems["sL"], nL)
            h.wait_ge(self.sems["sS"], nS)
        self.block.sync(_)
        for e in ENG_SEM:
            def _(h):
                h.sem_inc(self.sems["bar"], 1)
            getattr(self.block, e)(_)
        self.cnt["bar"] = tgt
        for e in ENG_SEM:
            def _(h, tgt=tgt):
                h.wait_ge(self.sems["bar"], tgt)
            getattr(self.block, e)(_)


def build_program(layers=(0, 1)):
    first, last = (0 in layers), (1 in layers)
    nc = bass.Bass("TRN2", target_bir_lowering=False)
    dm = {}

    def din(name, shape, dt=F32, kind="ExternalInput"):
        dm[name] = nc.dram_tensor(name, shape, dt, kind=kind).ap()

    if first:
        din("x", [NSEQ, D, T])
    if first and last:
        din("xmid", [NSEQ, D, T], kind="Internal")
    elif first:
        din("xmid", [NSEQ, D, T], kind="ExternalOutput")
    else:
        din("xmid", [NSEQ, D, T], kind="ExternalInput")
    if last:
        din("out", [NSEQ, D, T], kind="ExternalOutput")
    din("ybuf", [D, T], BF16, kind="Internal")

    if first:
        din("wqkvT", [D, 3 * D])
        din("bqkv_fm", [128, 16])
        din("bproj_fm", [128, 8])
        din("wprojT", [D, D])
        din("g1", [32, 256])
        din("g2", [64, 32])
        din("fbias_fm", [128, 8])
        din("repD", [32, 4, 128])
        din("repG", [16, 128])
    if last:
        for p in "qkvo":
            din("w%sT" % p, [D, D])
        din("bq_fm", [128, 8]); din("bk_fm", [128, 8]); din("bo_fm", [128, 8])
        din("a1T", [D, 64]); din("avT", [D, 8]); din("aoT", [D, 8])
        din("bTall", [8, 4 * D])
        din("rzrep", [128, 1])
        din("wfT", [D, D]); din("fscale_fm", [128, 8]); din("fbias1_fm", [128, 8])
    din("bvrep", [2, 128, D])
    din("idn", [128, 128])
    din("fvrep", [2, 128, 256], BF16)
    din("fexp", [NKT, 128, D])
    din("masks", [128, 8 * QC], U8)
    din("eps_t", [1, 1])
    din("onescol", [128, 1])
    din("onesrr", [1, 128])
    din("selr", [128, 8 * 128])
    dm["lscr"] = nc.dram_tensor("lscr", [4, 16, QC], F32, kind="Internal").ap()
    if DEBUG:
        din("dbg_xn", [128, NCH * T], BF16, kind="ExternalOutput")
        din("dbg_q", [128, NCH * T], BF16, kind="ExternalOutput")
        din("dbg_k", [128, NCH * T], BF16, kind="ExternalOutput")
        din("dbg_v", [128, NKT * VS], BF16, kind="ExternalOutput")
        din("dbg_lp", [64, QC], kind="ExternalOutput")
        din("dbg_t1", [128, NCH * T], BF16, kind="ExternalOutput")
        din("dbg_yblk", [128, NCH * QC], BF16, kind="ExternalOutput")
        din("dbg_rp", [64, QC], kind="ExternalOutput")
        din("dbg_y", [D, T], BF16, kind="ExternalOutput")
        din("dbg_fv", [128, 256], BF16, kind="ExternalOutput")
        din("dbg_fv1", [128, 256], BF16, kind="ExternalOutput")
        din("dbg_fv2", [128, 256], BF16, kind="ExternalOutput")
        din("dbg_p1", [128, 256], BF16, kind="ExternalOutput")
        din("dbg_p2", [128, 256], BF16, kind="ExternalOutput")
        din("dbg_p3", [128, 256], BF16, kind="ExternalOutput")
        din("dbg_fvdram", [2, 128, 256], BF16, kind="ExternalOutput")

    xsrc = {0: dm.get("x"), 1: dm["xmid"]}
    xdst = {0: dm["xmid"], 1: dm.get("out")}

    es = ExitStack()
    with es:
        def _al(f, *a):
            return es.enter_context(f(*a))
        arenaA = _al(nc.sbuf_tensor, "arenaA", [128, 65536], U8)
        arenaB = _al(nc.sbuf_tensor, "arenaB", [128, 33280], U8)
        arenaC = _al(nc.sbuf_tensor, "arenaC", [128, 32768], U8)
        xn_sb = _al(nc.sbuf_tensor, "xn_sb", [128, NCH * T], BF16)
        if DEBUG:
            fv_sb = _al(nc.sbuf_tensor, "fv_sb", [128, 256], BF16)
        masks_sb = _al(nc.sbuf_tensor, "masks_sb", [128, 8 * QC], U8)
        eps_sb = _al(nc.sbuf_tensor, "eps_sb", [1, 1], F32)
        ones_r = _al(nc.sbuf_tensor, "ones_r", [128, 1], F32)
        st_tmp = _al(nc.sbuf_tensor, "st_tmp", [1, 2 * QC], F32)
        r_r = _al(nc.sbuf_tensor, "r_r", [1, QC], F32R)
        mr_r = _al(nc.sbuf_tensor, "mr_r", [1, QC], F32R)
        ones_rr = _al(nc.sbuf_tensor, "ones_rr", [1, 128], F32R)
        sel_r = _al(nc.sbuf_tensor, "sel_r", [128, 8 * 128], F32R)
        rpAr = _al(nc.sbuf_tensor, "rpAr", [128, QC], F32R)
        ltmp = _al(nc.sbuf_tensor, "ltmp", [1, 2 * QC], F32)
        fbb_sb = _al(nc.sbuf_tensor, "fbb_sb", [128, 2 * D], F32)
        stg = _al(nc.sbuf_tensor, "stg", [128, 2048], U8)
        ystgA = stg[:, 0:2048].bitcast(BF16)[0:64, :]
        ystgB = stg[:, 0:2048].bitcast(BF16)[64:128, :]
        x2stg = stg[:, 0:2048].bitcast(F32)
        bias24 = _al(nc.sbuf_tensor, "bias24", [128, 24], F32)
        fsc_sb = _al(nc.sbuf_tensor, "fsc_sb", [128, 8], F32)
        fbi_sb = _al(nc.sbuf_tensor, "fbi_sb", [128, 8], F32)
        ffb_sb = _al(nc.sbuf_tensor, "ffb_sb", [128, 8], F32)
        rz_sb = _al(nc.sbuf_tensor, "rz_sb", [128, 1], F32)
        u_v = _al(nc.sbuf_tensor, "u_v", [8, T], BF16)
        u_o = u_v
        a1_bf = _al(nc.sbuf_tensor, "a1_bf", [128, NCH * 64], BF16)
        av_bf = _al(nc.sbuf_tensor, "av_bf", [128, NCH * 8], BF16)
        ao_bf = _al(nc.sbuf_tensor, "ao_bf", [128, NCH * 8], BF16)
        bT_cur = _al(nc.sbuf_tensor, "bT_cur", [8, D], BF16)
        scr1 = _al(nc.sbuf_tensor, "scr1", [128, 512], F32)
        scr2 = _al(nc.sbuf_tensor, "scr2", [128, 512], F32)
        scr3 = _al(nc.sbuf_tensor, "scr3", [128, 512], F32)
        ps = _al(nc.psum_tensor, "ps", [128, 4096], F32)
        _sL = _al(nc.semaphore, "sL")
        _sG = _al(nc.semaphore, "sG")
        _sP = _al(nc.semaphore, "sP")
        _sV = _al(nc.semaphore, "sV")
        _sA = _al(nc.semaphore, "sA")
        _sO = _al(nc.semaphore, "sO")
        _sS = _al(nc.semaphore, "sS")
        _sW = _al(nc.semaphore, "sW")
        _bar = _al(nc.semaphore, "bar")
        block = _al(nc.Block)
        b = Bld(nc, block, dict(sL=_sL, sG=_sG, sP=_sP, sV=_sV, sA=_sA, sO=_sO, sS=_sS, sW=_sW, bar=_bar))
        bank = [ps[:, i * 512:(i + 1) * 512] for i in range(8)]

        q_sb = arenaA[:, 0:32768].bitcast(BF16)
        k_sb = arenaA[:, 32768:65536].bitcast(BF16)
        t1_sb = arenaA[:, 0:32768].bitcast(BF16)
        sw_bf = arenaA[:, 32768:65536].bitcast(BF16)
        wr_bf = arenaA[:, 32768:36864].bitcast(BF16)
        wi_bf = arenaA[:, 36864:40960].bitcast(BF16)
        win_bf = arenaA[:, 40960:45056].bitcast(BF16)
        v_sb = arenaB[:, 0:2 * NKT * VS].bitcast(BF16)
        x_blk = arenaB[:, 0:16384].bitcast(F32)
        x_blk1 = arenaB[:, 16384:32768].bitcast(F32)
        xsq_c = arenaC[:, 28672:32768].bitcast(F32)
        p1_sb = arenaC[:, 0:32768].bitcast(BF16)
        wstage_bf = arenaC[:, 16384:20480].bitcast(BF16)
        wstage2_bf = arenaC[:, 28672:32768].bitcast(BF16)
        wbf = arenaC[:, 0:16384].bitcast(BF16)
        wstage = arenaC[:, 16384:20480].bitcast(F32)
        wstage2 = arenaC[:, 28672:32768].bitcast(F32)
        ffA = arenaC[:, 4096:8192].bitcast(F32)
        ffB = arenaC[:, 8192:12288].bitcast(F32)
        ffC = arenaC[:, 12288:16384].bitcast(F32)
        ffC2 = arenaC[:, 0:4096].bitcast(F32)
        yblk = arenaC[:, 20480:28672].bitcast(BF16)
        yblk2 = arenaB[:, 16384:24576].bitcast(BF16)
        u_q = arenaC[:, 20480:24576].bitcast(BF16)
        u_k = arenaC[:, 24576:28672].bitcast(BF16)
        idn_sb = arenaB[:, 29824:30336].bitcast(F32)
        gld = arenaB[:, 24576:25728].bitcast(F32)[0:64, :]
        g1t = arenaB[:, 25728:26240].bitcast(F32)[0:32, :]
        g2t = arenaB[:, 26240:26496].bitcast(F32)[0:16, :]
        repD_sb = arenaB[:, 26496:28544].bitcast(F32)[0:32, :]
        repG_sb = arenaB[:, 28544:29056].bitcast(F32)[0:16, :]
        g1rep = arenaB[:, 29056:29568].bitcast(F32)
        g2rep = arenaB[:, 29568:29824].bitcast(F32)

        def _(sync):
            for (dst, src) in [
                (masks_sb[:], dm["masks"]), (eps_sb[:], dm["eps_t"]),
            ]:
                b.inc(sync.dma_start(out=dst, in_=src), "sL", 16)
            b.inc(sync.dma_start(out=ones_r[:], in_=dm["onescol"]), "sL", 16)
            b.inc(sync.dma_start(out=ones_rr[:], in_=dm["onesrr"].bitcast(F32R)), "sL", 16)
            b.inc(sync.dma_start(out=sel_r[:], in_=dm["selr"].bitcast(F32R)), "sL", 16)
            return None
        b.sec("sync", [], _)

        def _(v):
            v.memset(scr3[:], 1.0)
            return v.tensor_copy(rpAr[:], scr3[:])
        b.sec("vector", [], _)

        def _(sync):
            if first:
                b.inc(sync.dma_start(out=ffb_sb[:], in_=dm["fbias_fm"]), "sL", 16)
            if last:
                b.inc(sync.dma_start(out=rz_sb[:], in_=dm["rzrep"]), "sL", 16)
                b.inc(sync.dma_start(out=fsc_sb[:], in_=dm["fscale_fm"]), "sL", 16)
                b.inc(sync.dma_start(out=fbi_sb[:], in_=dm["fbias1_fm"]), "sL", 16)
                pass
            return None
        b.sec("sync", [], _)
        mSetup = b.mark("sL")

        if last:
            mprev = mSetup
            for (dst, srcn, w_) in (
                (a1_bf, "a1T", 64), (av_bf, "avT", 8), (ao_bf, "aoT", 8),
            ):
                def _(sync, srcn=srcn, w_=w_):
                    for ic in range(NCH):
                        b.inc(sync.dma_start(out=wstage[:, ic * w_:(ic + 1) * w_],
                                             in_=dm[srcn][ic * 128:(ic + 1) * 128, :]), "sL", 16)
                    return None
                b.sec("sync", [mprev], _)
                mw = b.mark("sL")

                def _(v, dst=dst, w_=w_):
                    return v.tensor_copy(dst[:], wstage[:, 0:NCH * w_])
                mprev = b.sec("vector", [mw], _)

        def unfold_start(waits):
            def _(sync):
                b.inc(sync.dma_start(out=gld[0:32, 0:256], in_=dm["g1"]), "sL", 16)
                b.inc(sync.dma_start(out=gld[0:64, 256:288], in_=dm["g2"]), "sL", 16)
                b.inc(sync.dma_start(out=repD_sb[:].rearrange("d (c p) -> d c p", c=4),
                                     in_=dm["repD"]), "sL", 16)
                b.inc(sync.dma_start(out=repG_sb[:], in_=dm["repG"]), "sL", 16)
                b.inc(sync.dma_start(out=idn_sb[:], in_=dm["idn"]), "sL", 16)
                return None
            b.sec("sync", waits, _)
            return {"mld": b.mark("sL"), "mprev": None}

        def unfold_tr(st, mi):
            def _(t, mi=mi):
                for c in range(4):
                    t.transpose(ps[0:32, 3584 + c * 32:3584 + (c + 1) * 32],
                                gld[0:32, mi * 128 + c * 32:mi * 128 + (c + 1) * 32],
                                idn_sb[0:32, 0:32])
                return t.transpose(ps[0:16, 3712:3776],
                                   gld[0:64, 256 + mi * 16:256 + (mi + 1) * 16],
                                   idn_sb[0:64, 0:64])
            mT = b.sec("tensor", [st["mld"]] + ([st["mprev"]] if st["mprev"] else []), _)

            def _(v):
                v.tensor_copy(g1t[0:32, :], ps[0:32, 3584:3712])
                return v.tensor_copy(g2t[0:16, :], ps[0:16, 3712:3776])
            st["mprev"] = b.sec("vector", [mT], _)

        def unfold_cc(st, mi, cc):
            wdst, wneg = ((wr_bf, None) if mi == 0 else (wi_bf, win_bf))

            def _(t, cc=cc):
                i = None
                for c in range(4):
                    t.matmul(ps[:, 3776 + c * 32:3776 + (c + 1) * 32],
                             repD_sb[:, cc * 128:(cc + 1) * 128],
                             g1t[0:32, c * 32:(c + 1) * 32], start=True, stop=True)
                    i = t.matmul(ps[:, 3904 + c * 16:3904 + (c + 1) * 16],
                                 repG_sb[:], g2t[0:16, c * 16:(c + 1) * 16],
                                 start=True, stop=True)
                return i
            mrep = b.sec("tensor", [st["mprev"]], _)

            def _(v, cc=cc, wdst=wdst, wneg=wneg):
                v.tensor_copy(g1rep[:], ps[:, 3776:3904])
                v.tensor_copy(g2rep[:], ps[:, 3904:3968])
                for c in range(4):
                    av = g1rep[:, c * 32:(c + 1) * 32].unsqueeze(2).broadcast_to([128, 32, 16])
                    bv = g2rep[:, c * 16:(c + 1) * 16].unsqueeze(1).broadcast_to([128, 32, 16])
                    dst = (scr1 if c == 0 else scr2)[:, 0:512].rearrange(
                        "p (i j) -> p i j", i=32)
                    v.tensor_tensor(dst, av, bv, ALU.mult)
                    if c > 0:
                        v.tensor_tensor(scr1[:], scr1[:], scr2[:], ALU.add)
                i = v.tensor_copy(wdst[:, cc * 512:(cc + 1) * 512], scr1[:])
                if wneg is not None:
                    i = v.tensor_scalar(wneg[:, cc * 512:(cc + 1) * 512],
                                        scr1[:], -1.0, None, ALU.mult)
                return i
            st["mprev"] = b.sec("vector", [mrep], _)

        if DEBUG:
            def _(v):
                return v.memset(fv_sb[:], 7.0)
            msent = b.sec("vector", [], _)

            def _(g):
                b.inc(g.dma_start(out=dm["dbg_p1"], in_=fv_sb[:]), "sG", 16)
                return None
            b.sec("gpsimd", [msent], _)
        b.barrier()

        dbg_done = [False]
        pre = {}

        if first:
            def _(sync):
                src = dm["x"][0].rearrange("(c p) t -> p c t", p=128)[:, :, 0:QC]
                b.inc(sync.dma_start(
                    out=x_blk[:].rearrange("p (c t) -> p c t", c=NCH), in_=src), "sS", 16)
                return None
            b.sec("sync2", [], _)
            pre["mx0"] = b.mark("sS")

        for seq in range(NSEQ):
            for lyr in layers:
                L1 = (lyr == 1)
                xin = xsrc[lyr]
                xout = xdst[lyr]

                def _(sync):
                    if not L1:
                        b.inc(sync.dma_start(out=bias24[:, 0:16], in_=dm["bqkv_fm"]), "sL", 16)
                        b.inc(sync.dma_start(out=bias24[:, 16:24], in_=dm["bproj_fm"]), "sL", 16)
                    else:
                        b.inc(sync.dma_start(out=bias24[:, 0:8], in_=dm["bq_fm"]), "sL", 16)
                        b.inc(sync.dma_start(out=bias24[:, 8:16], in_=dm["bk_fm"]), "sL", 16)
                        b.inc(sync.dma_start(out=bias24[:, 16:24], in_=dm["bo_fm"]), "sL", 16)
                    return None
                b.sec("sync", [], _)
                mbias = b.mark("sL")

                mbtQ = None
                if L1:
                    def _(sync):
                        b.inc(sync.dma_start(out=bT_cur[0:8, :],
                                             in_=dm["bTall"][:, 0:D]), "sL", 16)
                        return None
                    b.sec("sync", [], _)
                    mbtQ = b.mark("sL")
                qwsrc = dm["wqkvT"] if not L1 else dm["wqT"]
                qcv = []

                def emit_qchunks(ics):
                    for ic in ics:
                        def _(sync, ic=ic):
                            b.inc(sync.dma_start(
                                out=wbf[:, ic * 1024:(ic + 1) * 1024],
                                in_=qwsrc[ic * 128:(ic + 1) * 128, 0:D]), "sL", 16)
                            return None
                        b.sec("sync", [mbias] if ic == 0 else [], _)
                        qcv.append(b.mark("sL"))

                b.tag = "ln"
                mxn = []
                mu_q = {}
                mu_v = {}
                mu_prev = [None]
                qev = []
                mlast_q = [None]

                def emit_qtb(tb):
                    if L1:
                        for (abf, wA, dsts, mud) in (
                            (a1_bf, 64, (u_q, u_k), mu_q),
                            (av_bf, 8, (u_v, None), mu_v),
                        ):
                            def _(t, abf=abf, wA=wA, tb=tb):
                                i = None
                                for ic in range(NCH):
                                    i = t.matmul(ps[0:wA, 3072:3584],
                                                 abf[:, ic * wA:(ic + 1) * wA],
                                                 xn_sb[:, ic * T + tb * QC: ic * T + (tb + 1) * QC],
                                                 start=(ic == 0), stop=(ic == NCH - 1))
                                return i
                            w = [mxn[tb]] + ([mu_prev[0]] if mu_prev[0] else [])
                            mpe = b.sec("tensor", w, _)

                            def _(v, dsts=dsts, tb=tb):
                                i = v.tensor_scalar(dsts[0][0:8, tb * QC:(tb + 1) * QC],
                                                    ps[0:8, 3072:3584], 0.125, None, ALU.mult)
                                if dsts[1] is not None:
                                    i = v.tensor_scalar(dsts[1][0:8, tb * QC:(tb + 1) * QC],
                                                        ps[32:40, 3072:3584], 0.125, None, ALU.mult)
                                return i
                            mud[tb] = mu_prev[0] = b.sec("vector", [mpe], _)
                    for oc in range(NCH):
                        bk = (tb * NCH + oc) % 2

                        def _(t, tb=tb, oc=oc, bk=bk):
                            i = None
                            for ic in range(NCH):
                                i = t.matmul(bank[bk], wbf[:, ic * 1024 + oc * 128: ic * 1024 + (oc + 1) * 128],
                                             xn_sb[:, ic * T + tb * QC: ic * T + (tb + 1) * QC],
                                             start=(ic == 0), stop=(not L1 and ic == NCH - 1))
                            if L1:
                                i = t.matmul(bank[bk], bT_cur[0:8, oc * 128:(oc + 1) * 128],
                                             u_q[0:8, tb * QC:(tb + 1) * QC], start=False, stop=True)
                            return i
                        w = [qcv[-1], mxn[tb]] + ([qev[-2]] if len(qev) >= 2 else [])
                        if L1:
                            w.append(mu_q[tb])
                            if tb == 0 and oc == 0 and mbtQ is not None:
                                w.append(mbtQ)
                        mpe = b.sec("tensor", w, _)
                        mlast_q[0] = mpe

                        def _(v, tb=tb, oc=oc, bk=bk):
                            return v.tensor_scalar(
                                q_sb[:, oc * T + tb * QC: oc * T + (tb + 1) * QC],
                                bank[bk], bias24[:, oc: oc + 1], None, ALU.add)
                        qev.append(b.sec("vector", [mpe], _))
                mx_used = None
                mxu = {}
                mst_prev = [None]
                mbc_prev = [None]
                mrb_prev = [None]
                for tb in range(NQC):
                    xb = x_blk if tb % 2 == 0 else x_blk1

                    if tb == 0 and "mx0" in pre:
                        mxl = pre.pop("mx0")
                    else:
                        def _(sync, tb=tb, xb=xb):
                            src = xin[seq].rearrange("(c p) t -> p c t", p=128)[
                                :, :, tb * QC:(tb + 1) * QC]
                            b.inc(sync.dma_start(
                                out=xb[:].rearrange("p (c t) -> p c t", c=NCH), in_=src), "sS", 16)
                            return None
                        w = [mxu[tb - 2]] if tb >= 2 else []
                        b.sec("sync2", w, _)
                        mxl = b.mark("sS")

                    if tb == 0:
                        emit_qchunks((0, 1, 2, 3))
                    elif tb == 1:
                        emit_qchunks((4, 5, 6, 7))

                    msq = []
                    mpe_list = []
                    for ic in range(NCH):
                        def _(s, ic=ic, xb=xb):
                            return s.activation(
                                xsq_c[:, (ic % 2) * 512:(ic % 2 + 1) * 512],
                                xb[:, ic * 512:(ic + 1) * 512], AF.Square)
                        w = [mxl] + ([mpe_list[ic - 2]] if ic >= 2 else [])
                        msq.append(b.sec("scalar", w, _))

                        def _(t, ic=ic, tb=tb, xb=xb, mstv=(mst_prev[0] if mst_prev[0] is not None else None)):
                            if ic == 0 and mstv is not None:
                                t.wait_ge(b.sems["sV"], mstv[1])
                            t.matmul(ps[0:1, 0:512], ones_r[:], xb[:, ic * 512:(ic + 1) * 512],
                                     start=(ic == 0), stop=(ic == NCH - 1))
                            return t.matmul(ps[0:1, 512:1024], ones_r[:],
                                            xsq_c[:, (ic % 2) * 512:(ic % 2 + 1) * 512],
                                            start=(ic == 0), stop=(ic == NCH - 1))
                        mpe_list.append(b.sec("tensor", [mxl, msq[ic]], _))
                    mpe_stats = mpe_list[-1]

                    def _(v, tb=tb):
                        ms = st_tmp[0:1, 0:512]
                        v.tensor_scalar(ms, ps[0:1, 0:512], 1.0 / D, None, ALU.mult)
                        v.tensor_scalar(st_tmp[0:1, 512:1024], ps[0:1, 512:1024], 1.0 / D, None, ALU.mult)
                        v.tensor_tensor(scr2[0:1, 0:512], ms, ms, ALU.mult)
                        return v.tensor_tensor(st_tmp[0:1, 512:1024], st_tmp[0:1, 512:1024],
                                               scr2[0:1, 0:512], ALU.subtract)
                    mst = b.sec("vector", [mpe_stats] + ([mbc_prev[0]] if mbc_prev[0] else []), _)
                    mst_prev[0] = mst

                    def _(s, tb=tb):
                        s.activation(scr1[0:1, :], st_tmp[0:1, 512:1024], AF.Ln,
                                     bias=eps_sb[:], scale=1.0)
                        return s.activation(r_r[0:1, :],
                                            scr1[0:1, :], AF.Exp, scale=-0.5)
                    mr_ = b.sec("scalar", [mst] + ([mrb_prev[0]] if mrb_prev[0] else []), _)

                    def _(v, tb=tb):
                        return v.tensor_tensor(mr_r[0:1, :],
                                               st_tmp[0:1, 0:512],
                                               r_r[0:1, :].bitcast(F32), ALU.mult)
                    mmr = b.sec("vector", [mr_], _)
                    mbc_prev[0] = mmr

                    def _(t, tb=tb):
                        t.matmul(ps[:, 1024:1536], ones_rr[:], r_r[0:1, :], start=True, stop=True)
                        return t.matmul(ps[:, 1536:2048], ones_rr[:], mr_r[0:1, :],
                                        start=True, stop=True)
                    mrb = b.sec("tensor", [mmr] + ([mx_used] if mx_used else []), _)
                    mrb_prev[0] = mrb

                    def _(v, tb=tb, xb=xb):
                        i = None
                        for ic in range(NCH):
                            v.tensor_tensor(scr1[:], xb[:, ic * 512:(ic + 1) * 512],
                                            ps[:, 1024:1536], ALU.mult)
                            i = v.tensor_tensor(
                                xn_sb[:, ic * T + tb * QC: ic * T + (tb + 1) * QC],
                                scr1[:], ps[:, 1536:2048], ALU.subtract)
                        return i
                    mx_used = b.sec("vector", [mrb], _)
                    mxu[tb] = mx_used
                    mxn.append(mx_used)
                    if tb >= 1:
                        emit_qtb(tb - 1)

                emit_qtb(NQC - 1)
                mLN = mx_used

                kwsrc = dm["wqkvT"] if not L1 else dm["wkT"]
                kcoff = D if not L1 else 0
                mkpre = []
                for ic in (0, 1):
                    def _(sync, ic=ic, kwsrc=kwsrc, kcoff=kcoff):
                        b.inc(sync.dma_start(
                            out=wstage2_bf[:, ic * 1024:(ic + 1) * 1024],
                            in_=kwsrc[ic * 128:(ic + 1) * 128, kcoff:kcoff + D]), "sW", 16)
                        return None
                    b.sec("sync", [mpe_stats], _)
                    mkpre.append(b.mark("sW"))
                if DEBUG and seq == 0 and lyr == layers[0] and not dbg_done[0]:
                    def _(g):
                        b.inc(g.dma_start(out=dm["dbg_p2"], in_=fv_sb[:]), "sG", 16)
                        return None
                    b.sec("gpsimd", [mLN], _)

                b.tag = "qkv"
                if PHASES < 2:
                    b.barrier()
                    continue
                mQK = qev[-1]
                mlast_mm = mlast_q[0]
                mbt = None
                if L1:
                    def _(sync):
                        b.inc(sync.dma_start(out=bT_cur[0:8, :],
                                             in_=dm["bTall"][:, D:2 * D]), "sL", 16)
                        return None
                    b.sec("sync", [mQK], _)
                    mbt = b.mark("sL")
                wsrc = dm["wqkvT"] if not L1 else dm["wkT"]
                coff = D if not L1 else 0
                mcv = []
                for ic in range(NCH):
                    def _(sync, ic=ic, wsrc=wsrc, coff=coff):
                        b.inc(sync.dma_start(
                            out=wbf[:, ic * 1024:(ic + 1) * 1024],
                            in_=wsrc[ic * 128:(ic + 1) * 128, coff:coff + D]), "sL", 16)
                        return None
                    b.sec("sync", [mlast_mm], _)
                    mcv.append(b.mark("sL"))

                mev = []
                for tb in range(NQC):
                    for oc in range(NCH):
                        bk = (tb * NCH + oc) % 2

                        def _(t, tb=tb, oc=oc, bk=bk):
                            i = None
                            for ic in range(NCH):
                                wv = wbf
                                i = t.matmul(bank[bk], wv[:, ic * 1024 + oc * 128: ic * 1024 + (oc + 1) * 128],
                                             xn_sb[:, ic * T + tb * QC: ic * T + (tb + 1) * QC],
                                             start=(ic == 0), stop=(not L1 and ic == NCH - 1))
                            if L1:
                                i = t.matmul(bank[bk], bT_cur[0:8, oc * 128:(oc + 1) * 128],
                                             u_k[0:8, tb * QC:(tb + 1) * QC], start=False, stop=True)
                            return i
                        w = [mcv[-1], mkpre[-1], mxn[tb]] + ([mev[-2]] if len(mev) >= 2 else [])
                        if L1:
                            w.append(mu_q[tb])
                            if tb == 0 and oc == 0 and mbt is not None:
                                w.append(mbt)
                        mpe = b.sec("tensor", w, _)
                        mlast_mm = mpe

                        def _(v, tb=tb, oc=oc, bk=bk):
                            return v.tensor_scalar(
                                k_sb[:, oc * T + tb * QC: oc * T + (tb + 1) * QC],
                                bank[bk], bias24[:, 8 + oc: 9 + oc], None, ALU.add)
                        mev.append(b.sec("vector", [mpe], _))
                mQK = mev[-1]

                mbt = None
                if QKV_SUB < 3:
                    b.barrier()
                    continue
                mcvK = mcv
                vwsrc = dm["wqkvT"] if not L1 else dm["wvT"]
                vcoff = 2 * D if not L1 else 0
                mvpre = []
                for ic in (0, 1):
                    def _(sync, ic=ic, vwsrc=vwsrc, vcoff=vcoff):
                        b.inc(sync.dma_start(
                            out=wstage_bf[:, ic * 1024:(ic + 1) * 1024],
                            in_=vwsrc[ic * 128:(ic + 1) * 128, vcoff:vcoff + D]), "sW", 16)
                        return None
                    b.sec("sync", [], _)
                    mvpre.append(b.mark("sW"))
                if L1:
                    def _(sync):
                        b.inc(sync.dma_start(out=bT_cur[0:8, :],
                                             in_=dm["bTall"][:, 2 * D:3 * D]), "sL", 16)
                        return None
                    b.sec("sync", [mQK], _)
                    mbt = b.mark("sL")
                mcv = []
                for ic in range(NCH):
                    def _(sync, ic=ic):
                        wsrc = dm["wqkvT"] if not L1 else dm["wvT"]
                        coff = 2 * D if not L1 else 0
                        b.inc(sync.dma_start(
                            out=wbf[:, ic * 1024:(ic + 1) * 1024],
                            in_=wsrc[ic * 128:(ic + 1) * 128, coff:coff + D]), "sL", 16)
                        return None
                    b.sec("sync", [mlast_mm], _)
                    mcv.append(b.mark("sL"))

                if DEBUG and seq == 0 and lyr == layers[0] and not dbg_done[0]:
                    def _(g):
                        b.inc(g.dma_start(out=dm["dbg_p3"], in_=fv_sb[:]), "sG", 16)
                        return None
                    b.sec("gpsimd", [mcv[-1] if mcv else mLN], _)

                def _(sync):
                    b.inc(sync.dma_start(out=scr2[:], in_=dm["bvrep"][lyr, :, 0:512]), "sL", 16)
                    b.inc(sync.dma_start(out=scr3[:], in_=dm["bvrep"][lyr, :, 512:1024]), "sL", 16)
                    return None
                b.sec("sync", [mLN], _)
                mvb = b.mark("sL")
                if DEBUG and seq == 0 and lyr == layers[0] and not dbg_done[0]:
                    def _(g):
                        b.inc(g.dma_start(out=dm["dbg_fv1"], in_=fv_sb[:]), "sG", 16)
                        b.inc(g.dma_start(out=dm["dbg_fvdram"], in_=dm["fvrep"]), "sG", 16)
                        return None
                    b.sec("gpsimd", [mvb], _)

                mev = []
                mfb = {}
                maug = []
                for tt in range(NKT):
                    if L1:
                        def _(sync, tt=tt):
                            b.inc(sync.dma_start(
                                out=fbb_sb[:, (tt % 2) * D:(tt % 2 + 1) * D],
                                in_=dm["fexp"][tt]), "sS", 16)
                            return None
                        w = [maug[tt - 2]] if tt >= 2 else []
                        b.sec("sync2", w, _)
                        mfl = b.mark("sS")
                    for oh in range(2):
                        bk = (tt * 2 + oh) % 2

                        def _(t, tt=tt, oh=oh, bk=bk):
                            i = None
                            for ic in range(NCH):
                                wv = wbf
                                i = t.matmul(bank[bk],
                                             xn_sb[:, ic * T + tt * 128: ic * T + (tt + 1) * 128],
                                             wv[:, ic * 1024 + oh * 512: ic * 1024 + (oh + 1) * 512],
                                             start=(ic == 0), stop=(not L1 and ic == NCH - 1))
                            if L1:
                                i = t.matmul(bank[bk], u_v[0:8, tt * 128:(tt + 1) * 128],
                                             bT_cur[0:8, oh * 512:(oh + 1) * 512],
                                             start=False, stop=True)
                            return i
                        w = [mcv[-1], mvpre[-1], mvb, mxn[tt // 4]] + ([mev[-2]] if len(mev) >= 2 else [])
                        if L1:
                            w.append(mu_v[tt // 4])
                            if tt == 0 and oh == 0:
                                w.append(mbt)
                        mpe = b.sec("tensor", w, _)

                        vdst = v_sb[:].rearrange("p (kt h e) -> p kt h e", kt=NKT, h=NH)[
                            :, tt, oh * 8:(oh + 1) * 8, 0:64]
                        psv = bank[bk].rearrange("p (h e) -> p h e", h=8)

                        if not L1:
                            def _(v, vdst=vdst, psv=psv, oh=oh):
                                vb = scr2 if oh == 0 else scr3
                                if NO_VEVAC:
                                    return v.tensor_copy(scr1[:], vb[:])
                                return v.tensor_tensor(
                                    vdst, psv, vb[:].rearrange("p (h e) -> p h e", h=8),
                                    ALU.add)
                            mev.append(b.sec("vector", [mpe], _))
                        else:
                            def _(v, vdst=vdst, psv=psv, oh=oh, tt=tt):
                                vb = scr2 if oh == 0 else scr3
                                if NO_VEVAC:
                                    return v.tensor_copy(scr1[:], vb[:])
                                v.tensor_tensor(scr1[:], bank[(tt * 2 + oh) % 2],
                                                vb[:], ALU.add)
                                return v.tensor_tensor(
                                    vdst, scr1[:].rearrange("p (h e) -> p h e", h=8),
                                    fbb_sb[:, (tt % 2) * D + oh * 512:(tt % 2) * D + (oh + 1) * 512]
                                    .rearrange("p (h e) -> p h e", h=8), ALU.mult)
                            mev.append(b.sec("vector", [mpe, mfl], _))
                    if L1:
                        def _(v, tt=tt):
                            vaug = v_sb[:].rearrange("p (kt h e) -> p kt h e", kt=NKT, h=NH)[
                                :, tt, :, 64:65]
                            srcv = fbb_sb[:, (tt % 2) * D:(tt % 2) * D + 1024].rearrange(
                                "p (h e) -> p h e", h=16)[:, :, 0:1]
                            return v.tensor_copy(vaug, srcv)
                        maug.append(b.sec("vector", [mev[-1], mfl], _))
                        mfb[tt] = maug[-1]

                if QKV_SUB < 4:
                    b.barrier()
                    continue
                if not L1:
                    def _(v):
                        vaug = v_sb[:].rearrange("p (kt h e) -> p kt h e", kt=NKT, h=NH)[:, :, :, 64:65]
                        return v.memset(vaug, 1.0)
                    mV = b.sec("vector", [mev[-1]], _)
                else:
                    mV = maug[-1]
                if DEBUG and seq == 0 and lyr == layers[0] and not dbg_done[0]:
                    def _(g):
                        b.inc(g.dma_start(out=dm["dbg_fv2"], in_=fv_sb[:]), "sG", 16)
                        return None
                    b.sec("gpsimd", [mV], _)

                b.barrier()

                b.tag = "attn"
                if PHASES < 3:
                    b.barrier()
                    continue
                halves = []
                cellct = 0
                for qc in range(NQC):
                    for pc in range(NH // 2):
                        ktmax = NKT if L1 else 4 * qc + 4
                        nhv = (ktmax + 7) // 8
                        for hi in range(nhv):
                            kts = list(range(hi * 8, min(ktmax, hi * 8 + 8)))
                            halves.append((pc, qc, hi, kts, cellct,
                                           hi == nhv - 1, ktmax))
                        cellct += 1

                mexpk = {}
                mqkk = {}
                myev = {}
                mlt = {}
                mlt2 = {}
                mys = {}
                mys2 = {}
                mrec_qc = {}
                pvms = []
                g_ = [0]
                mpv_last = [None]

                def emit_qk(hx, half):
                    pc, qc, hi, kts, cc, islast, ktmax = half
                    buf = (hx % 2) * 8192
                    ch = pc
                    for kt in kts:
                        g = g_[0]
                        s_ = g % 3

                        def _(t, kt=kt, ch=ch, qc=qc, s_=s_, first=(kt == kts[0])):
                            if first and DRAINS:
                                t.drain()
                            t.matmul(
                                ps[:, s_ * 1024: s_ * 1024 + 512],
                                k_sb[0:64, ch * T + kt * 128: ch * T + (kt + 1) * 128],
                                q_sb[0:64, ch * T + qc * QC: ch * T + (qc + 1) * QC],
                                start=True, stop=True, tile_position=(0, 0))
                            return t.matmul(
                                ps[:, s_ * 1024 + 512: (s_ + 1) * 1024],
                                k_sb[64:128, ch * T + kt * 128: ch * T + (kt + 1) * 128],
                                q_sb[64:128, ch * T + qc * QC: ch * T + (qc + 1) * QC],
                                start=True, stop=True, tile_position=(64, 0))
                        w = []
                        if hx == 0 and kt == kts[0]:
                            w += [mV, mQK]
                        if g >= 3:
                            w.append(mexpk[g - 3])
                        mqkk[g] = b.sec("tensor", w, _)

                        j = kt - hi * 8

                        def _(s, g=g, s_=s_, buf=buf, j=j):
                            return s.activation(
                                p1_sb[:, buf + j * 1024: buf + (j + 1) * 1024],
                                ps[:, s_ * 1024:(s_ + 1) * 1024],
                                AF.Exp, scale=0.125)
                        mexpk[g] = b.sec("scalar", [mqkk[g]], _)
                        g_[0] += 1

                    mmask = None
                    if not L1:
                        dk = [kt for kt in kts if 4 * qc <= kt < 4 * qc + 4]
                        if dk:
                            def _(v, dk=dk, hi=hi, qc=qc, buf=buf):
                                i = None
                                for kt in dk:
                                    j = kt - hi * 8
                                    sl = p1_sb[:, buf + j * 1024: buf + (j + 1) * 1024]
                                    i = v.tensor_tensor(
                                        sl, sl,
                                        masks_sb[:, (kt - 4 * qc) * 1024:
                                                 (kt - 4 * qc + 1) * 1024], ALU.mult)
                                return i
                            mmask = b.sec("vector", [mexpk[g_[0] - 1]], _)
                    return mmask

                def emit_pv(hx, half, mmask, mlastexp):
                    pc, qc, hi, kts, cc, islast, ktmax = half
                    buf = (hx % 2) * 8192
                    pvm = None
                    for hh in range(2):
                        yb = 6 + hh
                        H = pc * 2 + hh

                        def _(t, hh=hh, yb=yb, H=H, kts=kts, hi=hi, ktmax=ktmax, buf=buf):
                            if hh == 0 and DRAINS:
                                t.drain()
                            i = None
                            for kt in kts:
                                j = kt - hi * 8
                                i = t.matmul(
                                    ps[0:65, yb * 512:(yb + 1) * 512],
                                    v_sb[:, kt * VS + H * 65: kt * VS + (H + 1) * 65],
                                    p1_sb[:, buf + j * 1024 + hh * 512:
                                          buf + j * 1024 + (hh + 1) * 512],
                                    start=(kt == 0), stop=(kt == ktmax - 1),
                                    skip_group_check=True)
                            return i
                        w = [mlastexp]
                        if mmask is not None:
                            w.append(mmask)
                        if hi == 0 and yb in myev:
                            w.append(myev[yb])
                        pvm = b.sec("tensor", w, _)
                    mpv_last[0] = pvm
                    pvms.append(pvm)

                    if islast:
                        p_ = cc % 2
                        ystg = ystgA if p_ == 0 else ystgB
                        for hh in range(2):
                            yb = 6 + hh
                            H = pc * 2 + hh

                            def _(v, yb=yb, hh=hh, p_=p_, ystg=ystg):
                                v.tensor_copy(ltmp[0:1, p_ * QC:(p_ + 1) * QC],
                                              ps[64:65, yb * 512:(yb + 1) * 512])
                                return v.tensor_copy(ystg[:, hh * 512:(hh + 1) * 512],
                                                     ps[0:64, yb * 512:(yb + 1) * 512])
                            w = [pvm]
                            if p_ in mlt2:
                                w.append(mlt2[p_])
                            if (p_, hh) in mys2:
                                w.append(mys2[(p_, hh)])
                            myev[yb] = b.sec("vector", w, _)

                            def _(sync, hh=hh, H=H, qc=qc, p_=p_):
                                b.inc(sync.dma_start(out=dm["lscr"][qc, H].unsqueeze(0),
                                                     in_=ltmp[0:1, p_ * QC:(p_ + 1) * QC]), "sL", 16)
                                return None
                            b.sec("sync", [myev[yb]], _)
                            mlt[0] = mlt2[p_] = b.mark("sL")

                            def _(g, hh=hh, H=H, qc=qc, ystg=ystg):
                                b.inc(g.dma_start(
                                    out=dm["ybuf"][H * 64:(H + 1) * 64, qc * QC:(qc + 1) * QC],
                                    in_=ystg[:, hh * 512:(hh + 1) * 512]), "sG", 16)
                                return None
                            b.sec("gpsimd", [myev[yb]], _)
                            mys[hh] = mys2[(p_, hh)] = b.mark("sG")
                    if islast and pc == NH // 2 - 1 and qc < 3:
                        ro = (0, 32, 64)[qc]

                        def _(g, qc=qc):
                            b.inc(g.dma_start(out=scr3[0:16, :],
                                              in_=dm["lscr"][qc]), "sG", 16)
                            return None
                        w = [mlt[0]] + ([mrec_qc[qc - 1]] if qc > 0 else [])
                        b.sec("gpsimd", w, _)
                        mlld = b.mark("sG")

                        def _(v, ro=ro):
                            sl = scr3[0:16, :]
                            v.reciprocal(sl, sl)
                            return v.tensor_copy(rpAr[ro:ro + 16, :], sl)
                        mrec_qc[qc] = b.sec("vector", [mlld], _)

                pend = None
                for hx, half in enumerate(halves):
                    mmask = emit_qk(hx, half)
                    mlastexp = mexpk[g_[0] - 1]
                    if pend is not None:
                        emit_pv(*pend)
                    pend = (hx, half, mmask, mlastexp)
                emit_pv(*pend)
                mcell_pv = mpv_last[0]

                mbuf0_free = pvms[-2]
                mcv = []
                for ic in range(NCH):
                    def _(sync, ic=ic):
                        wsrc = dm["wprojT"] if not L1 else dm["woT"]
                        b.inc(sync.dma_start(
                            out=wbf[:, ic * 1024:(ic + 1) * 1024],
                            in_=wsrc[ic * 128:(ic + 1) * 128, :]), "sW", 16)
                        return None
                    b.sec("sync", [mbuf0_free], _)
                    mcv.append(b.mark("sW"))

                def _(sync):
                    b.inc(sync.dma_start(
                        out=yblk[:].rearrange("p (c t) -> p c t", c=NCH),
                        in_=dm["ybuf"].rearrange("(c p) t -> p c t", p=128)[
                            :, :, 0:QC]), "sW", 16)
                    return None
                b.sec("sync2", [mcell_pv, mys2[(0, 0)], mys2[(0, 1)],
                                mys2[(1, 0)], mys2[(1, 1)]], _)
                myld0 = b.mark("sW")

                def _(sync):
                    src = xin[seq].rearrange("(c p) t -> p c t", p=128)[:, :, 0:QC]
                    b.inc(sync.dma_start(
                        out=x_blk[:].rearrange("p (c t) -> p c t", c=NCH), in_=src), "sW", 16)
                    return None
                b.sec("sync2", [mcell_pv], _)
                mxf0 = b.mark("sW")

                if DEBUG and seq == 0 and lyr == layers[0] and not dbg_done[0]:
                    def _(g):
                        b.inc(g.dma_start(out=dm["dbg_xn"], in_=xn_sb[:]), "sG", 16)
                        b.inc(g.dma_start(out=dm["dbg_q"], in_=q_sb[:]), "sG", 16)
                        b.inc(g.dma_start(out=dm["dbg_k"], in_=k_sb[:]), "sG", 16)
                        b.inc(g.dma_start(out=dm["dbg_v"], in_=v_sb[:]), "sG", 16)
                        b.inc(g.dma_start(out=dm["dbg_fv"], in_=fv_sb[:]), "sG", 16)
                        b.inc(g.dma_start(out=dm["dbg_lp"], in_=dm["lscr"].rearrange("a b t -> (a b) t")), "sG", 16)
                        return None
                    b.sec("gpsimd", [mlt[0], mys[0], mys[1]], _)

                b.barrier()

                ust = None
                if not L1:
                    ust = unfold_start([])
                mbt = None
                if L1:
                    def _(sync):
                        b.inc(sync.dma_start(out=bT_cur[0:8, :],
                                             in_=dm["bTall"][:, 3 * D:4 * D]), "sL", 16)
                        return None
                    b.sec("sync", [], _)
                    mbt = b.mark("sL")


                msw_hold = [None]
                sgn = {"mswc": []}

                def emit_sign_chunk(j):
                    slot = wstage if j % 2 == 0 else wstage2

                    def _(sync, j=j, slot=slot):
                        b.inc(sync.dma_start(out=slot[:],
                                             in_=dm["wfT"][j * 128:(j + 1) * 128, :]), "sL", 16)
                        return None
                    w = [mbt] if j < 2 else [sgn["mswc"][j - 2]]
                    b.sec("sync", w, _)
                    mswl = b.mark("sL")

                    def _(v, j=j, slot=slot):
                        dst = sw_bf[:, j * 1024:(j + 1) * 1024]
                        v.tensor_scalar(dst, slot[:], 0.0, 2.0, ALU.is_ge, ALU.mult)
                        return v.tensor_scalar(dst, dst, 1.0, None, ALU.subtract)
                    sgn["mswc"].append(b.sec("vector", [mswl], _))

                mev = []
                myld = None
                muo_prev = [None]
                mpe_tb_last = {}
                for tb in range(NQC):
                    ybt = yblk if tb % 2 == 0 else yblk2
                    if tb == 0:
                        myld = myld0
                    else:
                        def _(sync, tb=tb, ybt=ybt):
                            b.inc(sync.dma_start(
                                out=ybt[:].rearrange("p (c t) -> p c t", c=NCH),
                                in_=dm["ybuf"].rearrange("(c p) t -> p c t", p=128)[
                                    :, :, tb * QC:(tb + 1) * QC]), "sS", 16)
                            return None
                        w = [mpe_tb_last[tb - 2]] if tb >= 2 else []
                        b.sec("sync2", w, _)
                        myld = b.mark("sS")

                    if tb == 3:
                        def _(g):
                            b.inc(g.dma_start(out=scr3[0:16, :],
                                              in_=dm["lscr"][3]), "sG", 16)
                            return None
                        b.sec("gpsimd", [mrn[-1], mrec_qc[2]], _)
                        mlld3 = b.mark("sG")

                        def _(v):
                            sl = scr3[0:16, :]
                            v.reciprocal(sl, sl)
                            return v.tensor_copy(rpAr[64:80, :], sl)
                        mrec_qc[3] = b.sec("vector", [mlld3], _)

                    mrn = []
                    for ic in range(NCH):
                        half = ic % 2

                        def _(t, ic=ic, half=half, tb=tb):
                            sb = (0, 32, 64, 64)[tb]
                            return t.matmul(
                                ps[:, (4 + half) * 512:(5 + half) * 512],
                                sel_r[sb:sb + 32, ic * 128:(ic + 1) * 128],
                                rpAr[sb:sb + 32, :],
                                start=True, stop=True)
                        w = [myld, mrec_qc[tb]] + ([mrn[-2]] if len(mrn) >= 2 else [])
                        mbc = b.sec("tensor", w, _)

                        def _(v, ic=ic, half=half, ybt=ybt):
                            sl = ybt[:, ic * 512:(ic + 1) * 512]
                            return v.scalar_tensor_tensor(
                                sl, sl, 1.0, ps[:, (4 + half) * 512:(5 + half) * 512],
                                ALU.mult, ALU.mult)
                        mrn.append(b.sec("vector", [mbc], _))

                    if DEBUG and seq == 0 and lyr == layers[0] and tb == 1 and not dbg_done[0]:
                        def _(g):
                            b.inc(g.dma_start(out=dm["dbg_yblk"], in_=yblk[:]), "sG", 16)
                            b.inc(g.dma_start(out=dm["dbg_rp"], in_=rpAr[:].bitcast(F32)), "sG", 16)
                            return None
                        b.sec("gpsimd", [mrn[-1]], _)

                    if L1:
                        emit_sign_chunk(2 * tb)
                        emit_sign_chunk(2 * tb + 1)
                    else:
                        if tb < 2:
                            if tb == 0:
                                unfold_tr(ust, 0)
                            unfold_cc(ust, 0, 2 * tb)
                            unfold_cc(ust, 0, 2 * tb + 1)
                            if tb == 1:
                                unfold_tr(ust, 1)
                        else:
                            unfold_cc(ust, 1, 2 * (tb - 2))
                            unfold_cc(ust, 1, 2 * (tb - 2) + 1)
                    muo = None
                    if L1:
                        def _(t, tb=tb, ybt=ybt):
                            i = None
                            for ic in range(NCH):
                                i = t.matmul(ps[0:8, 1024:1536], ao_bf[:, ic * 8:(ic + 1) * 8],
                                             ybt[:, ic * 512:(ic + 1) * 512],
                                             start=(ic == 0), stop=(ic == NCH - 1))
                            return i
                        mpe = b.sec("tensor", [mrn[-1]] + ([muo_prev[0]] if muo_prev[0] else []), _)

                        def _(v, tb=tb):
                            return v.tensor_scalar(u_o[0:8, tb * QC:(tb + 1) * QC],
                                                   ps[0:8, 1024:1536], 0.125, None, ALU.mult)
                        muo = b.sec("vector", [mpe], _)
                        muo_prev[0] = muo

                    for oc in range(NCH):
                        bk = (tb * NCH + oc) % 2

                        def _(t, tb=tb, oc=oc, bk=bk, ybt=ybt):
                            i = None
                            for ic in range(NCH):
                                i = t.matmul(bank[bk], wbf[:, ic * 1024 + oc * 128: ic * 1024 + (oc + 1) * 128],
                                             ybt[:, ic * 512:(ic + 1) * 512],
                                             start=(ic == 0), stop=(not L1 and ic == NCH - 1))
                            if L1:
                                i = t.matmul(bank[bk], bT_cur[0:8, oc * 128:(oc + 1) * 128],
                                             u_o[0:8, tb * QC:(tb + 1) * QC], start=False, stop=True)
                            return i
                        w = [mrn[-1], mcv[-1]] + ([muo] if muo else []) + ([mev[-2]] if len(mev) >= 2 else [])
                        mpe = b.sec("tensor", w, _)
                        mpe_tb_last[tb] = mpe

                        def _(v, tb=tb, oc=oc, bk=bk):
                            dst = t1_sb[:, oc * T + tb * QC: oc * T + (tb + 1) * QC]
                            if not L1:
                                return v.tensor_scalar(dst, bank[bk],
                                                       bias24[:, 16 + oc: 17 + oc], None, ALU.add)
                            return v.tensor_scalar(dst, bank[bk],
                                                   bias24[:, 16 + oc: 17 + oc], rz_sb[:],
                                                   ALU.add, ALU.mult)
                        mev.append(b.sec("vector", [mpe], _))
                msw_hold[0] = sgn["mswc"][-1] if L1 else ust["mprev"]
                mPROJ = mev[-1]

                if DEBUG and seq == 0 and lyr == layers[0] and not dbg_done[0]:
                    def _(g):
                        b.inc(g.dma_start(out=dm["dbg_t1"], in_=t1_sb[:]), "sG", 16)
                        return None
                    b.sec("gpsimd", [mPROJ], _)
                    def _(g):
                        b.inc(g.dma_start(out=dm["dbg_y"], in_=dm["ybuf"]), "sG", 16)
                        return None
                    b.sec("gpsimd", [mPROJ], _)

                b.barrier()

                b.tag = "ffn"
                if PHASES < 5:
                    b.barrier()
                    continue
                msw = msw_hold[0]
                mst_out = []
                mfin_tb = {}
                mfin_hist = []
                mfin_prev = [None]
                mg_prev = [None]
                mcl_hist = []
                for tb in range(NQC):
                    xb = x_blk if tb % 2 == 0 else x_blk1

                    if tb == 0:
                        mxl = mxf0
                    else:
                        def _(sync, tb=tb, xb=xb):
                            src = xin[seq].rearrange("(c p) t -> p c t", p=128)[
                                :, :, tb * QC:(tb + 1) * QC]
                            b.inc(sync.dma_start(
                                out=xb[:].rearrange("p (c t) -> p c t", c=NCH), in_=src), "sS", 16)
                            return None
                        w = [mfin_tb[tb - 2]] if tb >= 2 else []
                        b.sec("sync2", w, _)
                        mxl = b.mark("sS")

                    for po in range(4):
                        oc0 = 2 * po
                        bks = (po % 2) * 2

                        def _(t, tb=tb, oc0=oc0, bks=bks):
                            i = None
                            for j in range(2):
                                oc = oc0 + j
                                if not L1:
                                    for ic in range(NCH):
                                        if oc < 4:
                                            wt = (wr_bf if ic < 4 else win_bf)
                                            col = (ic % 4) * 512 + oc * 128
                                        else:
                                            wt = (wi_bf if ic < 4 else wr_bf)
                                            col = (ic % 4) * 512 + (oc - 4) * 128
                                        i = t.matmul(bank[bks + j], wt[:, col:col + 128],
                                                     xn_sb[:, ic * T + tb * QC: ic * T + (tb + 1) * QC],
                                                     start=(ic == 0), stop=(ic == NCH - 1))
                                else:
                                    for ic in range(NCH):
                                        i = t.matmul(bank[bks + j],
                                                     sw_bf[:, ic * 1024 + oc * 128: ic * 1024 + (oc + 1) * 128],
                                                     xn_sb[:, ic * T + tb * QC: ic * T + (tb + 1) * QC],
                                                     start=(ic == 0), stop=(ic == NCH - 1))
                            return i
                        w = [msw] + ([mcl_hist[-2]] if len(mcl_hist) >= 2 else [])
                        mpe = b.sec("tensor", w, _)

                        if not L1:
                            fB = ffB if len(mst_out) % 2 == 0 else ffA

                            def _(s, oc0=oc0, bks=bks, fB=fB):
                                s.activation(fB[:, 0:512], bank[bks], AF.Silu,
                                             bias=ffb_sb[:, oc0:oc0 + 1], scale=1.0)
                                return s.activation(fB[:, 512:1024], bank[bks + 1], AF.Silu,
                                                    bias=ffb_sb[:, oc0 + 1:oc0 + 2], scale=1.0)
                            w = [mpe] + ([mfin_hist[-2]] if len(mfin_hist) >= 2 else [])
                            mg = b.sec("scalar", w, _)
                            mcl_hist.append(mg)

                            fC = ffC if len(mst_out) % 2 == 0 else ffC2

                            def _(v, tb=tb, oc0=oc0, fC=fC, xb=xb, fB=fB):
                                t1p = t1_sb[:].rearrange("p (c t) -> p c t", c=NCH)[
                                    :, oc0:oc0 + 2, tb * QC:(tb + 1) * QC]
                                fv3 = fC[:].rearrange("p (c t) -> p c t", c=2)
                                v.tensor_tensor(fv3, fB[:].rearrange("p (c t) -> p c t", c=2),
                                                t1p, ALU.mult)
                                return v.tensor_tensor(fC[:], fC[:],
                                                       xb[:, oc0 * 512:(oc0 + 2) * 512], ALU.add)
                            w = [mg, mxl] + ([mst_out[-2]] if len(mst_out) >= 2 else [])
                            mfin = b.sec("vector", w, _)
                            mfin_hist.append(mfin)
                            mfin_prev[0] = mfin
                        else:
                            def _(v, oc0=oc0, bks=bks):
                                v.tensor_scalar(ffA[:, 0:512], bank[bks], fsc_sb[:, oc0:oc0 + 1],
                                                fbi_sb[:, oc0:oc0 + 1], ALU.mult, ALU.add)
                                v.tensor_scalar(ffA[:, 512:1024], bank[bks + 1],
                                                fsc_sb[:, oc0 + 1:oc0 + 2],
                                                fbi_sb[:, oc0 + 1:oc0 + 2], ALU.mult, ALU.add)
                                return v.tensor_scalar(ffA[:], ffA[:], -100.0, 100.0,
                                                       ALU.max, ALU.min)
                            w = [mpe] + ([mg_prev[0]] if mg_prev[0] else [])
                            mcl = b.sec("vector", w, _)
                            mcl_hist.append(mcl)

                            def _(s):
                                return s.activation(ffB[:], ffA[:], AF.Silu)
                            w = [mcl] + ([mfin_prev[0]] if mfin_prev[0] else [])
                            mg = b.sec("scalar", w, _)
                            mg_prev[0] = mg

                            fC = ffC if len(mst_out) % 2 == 0 else ffC2

                            def _(v, tb=tb, oc0=oc0, fC=fC, xb=xb):
                                fv3 = fC[:].rearrange("p (c t) -> p c t", c=2)
                                xn3 = xn_sb[:].rearrange("p (c t) -> p c t", c=NCH)[
                                    :, oc0:oc0 + 2, tb * QC:(tb + 1) * QC]
                                t1p = t1_sb[:].rearrange("p (c t) -> p c t", c=NCH)[
                                    :, oc0:oc0 + 2, tb * QC:(tb + 1) * QC]
                                v.tensor_tensor(fv3, xn3, t1p, ALU.add)
                                v.tensor_tensor(fC[:], ffB[:], fC[:], ALU.mult)
                                return v.tensor_tensor(fC[:], fC[:],
                                                       xb[:, oc0 * 512:(oc0 + 2) * 512], ALU.add)
                            w = [mg, mxl] + ([mst_out[-2]] if len(mst_out) >= 2 else [])
                            mfin = b.sec("vector", w, _)
                            mfin_prev[0] = mfin

                        def _(g, tb=tb, oc0=oc0, fC=fC):
                            b.inc(g.dma_start(
                                out=xout[seq, oc0 * 128:(oc0 + 2) * 128,
                                         tb * QC:(tb + 1) * QC].rearrange(
                                             "(c p) t -> p c t", p=128),
                                in_=fC[:].rearrange("p (c t) -> p c t", c=2)), "sG", 16)
                            return None
                        b.sec("gpsimd", [mfin], _)
                        mst_out.append(b.mark("sG"))
                    mfin_tb[tb] = mfin

                if not (seq == NSEQ - 1 and lyr == layers[-1]):
                    nseq, nlyr = (seq, layers[-1]) if lyr == layers[0] and len(layers) > 1 else (seq + 1, layers[0])

                    def _(sync, nseq=nseq, nlyr=nlyr):
                        src = xsrc[nlyr][nseq].rearrange("(c p) t -> p c t", p=128)[
                            :, :, 0:QC]
                        b.inc(sync.dma_start(
                            out=x_blk[:].rearrange("p (c t) -> p c t", c=NCH), in_=src), "sS", 16)
                        return None
                    b.sec("sync2", [mfin_tb[2], mst_out[3]], _)
                    pre["mx0"] = b.mark("sS")

                if DEBUG and seq == 0 and lyr == layers[0]:
                    dbg_done[0] = True
                b.barrier()

        def _(g):
            g.wait_ge(b.sems["sG"], b.cnt["sG"])
            return None
        b.sec("gpsimd", [], _)
        b.flush()
        nc._ledger = dict(b.cnt)
        try:
            nc._addrs = {}
            for _nm, _h in [("arenaA", arenaA), ("arenaB", arenaB), ("arenaC", arenaC),
                            ("xn_sb", xn_sb),
                            ("masks_sb", masks_sb), ("eps_sb", eps_sb), ("ones_r", ones_r),
                            ("st_tmp", st_tmp), ("r_r", r_r), ("mr_r", mr_r),
                            ("ones_rr", ones_rr),
                            ("sel_r", sel_r), ("rpAr", rpAr), ("ltmp", ltmp),
                            ("fbb_sb", fbb_sb), ("stg", stg),
                            ("bias24", bias24), ("fsc_sb", fsc_sb), ("fbi_sb", fbi_sb),
                            ("ffb_sb", ffb_sb), ("rz_sb", rz_sb), ("u_v", u_v), ("u_o", u_o),
                            ("a1_bf", a1_bf), ("av_bf", av_bf), ("ao_bf", ao_bf),
                            ("bT_cur", bT_cur), ("scr1", scr1), ("scr2", scr2), ("scr3", scr3)]:
                _m = nc.lookup_mloc(_h)
                nc._addrs[_nm] = (_m.addr, _h.shape, str(_h.dtype))
        except Exception as _e:
            nc._addrs = {"error": str(_e)}

    return nc




def _host_constants():
    slopes = 0.5 ** np.arange(NH, dtype=np.float64)
    p = np.arange(128, dtype=np.float64)
    fvals_ = np.ones((2, 256), dtype=np.float64)
    for kt in range(NKT):
        for h in range(NH):
            fvals_[1, kt * NH + h] = np.exp(slopes[h] * (128.0 * kt - 1920.0))
    fexp = np.ones((NKT, 128, D), dtype=np.float32)
    for kt in range(NKT):
        arg = slopes[None, :, None] * (128.0 * kt + p[:, None, None] - 2047.0)
        fexp[kt] = np.broadcast_to(np.exp(arg).astype(BF).astype(np.float32),
                                   (128, NH, HD)).reshape(128, D)
    masks = np.zeros((128, 8 * QC), dtype=np.float32)
    t = np.arange(QC)
    m1 = np.zeros((128, QC), dtype=np.float32)
    for j in range(4):
        m1[:] = (p[:, None] + 128 * j <= t[None, :])
        masks[:, j * 2 * QC: j * 2 * QC + QC] = m1
        masks[:, j * 2 * QC + QC: (j + 1) * 2 * QC] = m1
    repD = np.zeros((32, 4, 128), dtype=np.float32)
    for cc in range(4):
        for pp in range(128):
            repD[cc * 8 + pp // 16, cc, pp] = 1.0
    repG = np.zeros((16, 128), dtype=np.float32)
    for pp in range(128):
        repG[pp % 16, pp] = 1.0
    idn = np.eye(128, dtype=np.float32)
    sel = np.zeros((128, 8 * 128), dtype=np.float32)
    for ic in range(8):
        for p in range(128):
            for off in (0, 32, 64):
                sel[off + 2 * ic + p // 64, ic * 128 + p] = 1.0
    return {
        "idn": idn,
        "fvrep": np.ascontiguousarray(np.broadcast_to(
            fvals_.astype(BF)[:, None, :], (2, 128, 256))),
        "fexp": fexp,
        "masks": masks.astype(np.uint8), "repD": repD, "repG": repG,
        "eps_t": np.full((1, 1), 1e-5, dtype=np.float32),
        "onescol": np.ones((128, 1), dtype=np.float32),
        "onesrr": np.ones((1, 128), dtype=np.float32),
        "selr": sel,
    }


def _fm(vec):
    v = np.asarray(vec, dtype=np.float32)
    n = v.shape[0] // 128
    return np.ascontiguousarray(v.reshape(n, 128).T)


def _host_inputs(inputs, layers=(0, 1)):
    first, last = (0 in layers), (1 in layers)
    d = dict(_host_constants())
    bvrep = np.zeros((2, 128, D), dtype=np.float32)
    if first:
        Wqkv = inputs["L0_attn_W_qkv"]
        d["wqkvT"] = np.ascontiguousarray(Wqkv.T)
        bq = inputs["L0_attn_b_qkv"]
        d["bqkv_fm"] = np.concatenate([_fm(bq[0:D]), _fm(bq[D:2 * D])], axis=1)
        d["bproj_fm"] = _fm(inputs["L0_attn_b_proj"])
        d["wprojT"] = np.ascontiguousarray(inputs["L0_attn_W_proj"].T)
        g1r = inputs["L0_ffn_G1_R"][0].reshape(32, 128)
        g1i = inputs["L0_ffn_G1_I"][0].reshape(32, 128)
        d["g1"] = np.ascontiguousarray(np.concatenate([g1r, g1i], axis=1))
        g2r = inputs["L0_ffn_G2_R"][:, :, 0, :].reshape(64, 16)
        g2i = inputs["L0_ffn_G2_I"][:, :, 0, :].reshape(64, 16)
        d["g2"] = np.ascontiguousarray(np.concatenate([g2r, g2i], axis=1))
        d["fbias_fm"] = _fm(np.concatenate([inputs["L0_ffn_bias_real"],
                                            inputs["L0_ffn_bias_imag"]]))
        bvrep[0] = np.broadcast_to(bq[2 * D:3 * D], (128, D))
    if last:
        d["wqT"] = np.ascontiguousarray(inputs["L1_attn_Wq"].T)
        d["wkT"] = np.ascontiguousarray(inputs["L1_attn_Wk"].T)
        d["wvT"] = np.ascontiguousarray(inputs["L1_attn_Wv"].T)
        d["woT"] = np.ascontiguousarray(inputs["L1_attn_Wo"].T)
        d["bq_fm"] = _fm(inputs["L1_attn_bq"])
        d["bk_fm"] = _fm(inputs["L1_attn_bk"])
        d["bo_fm"] = _fm(inputs["L1_attn_bo"])
        a1 = np.zeros((D, 64), dtype=np.float32)
        a1[:, 0:8] = inputs["L1_attn_Aq"].T
        a1[:, 32:40] = inputs["L1_attn_Ak"].T
        d["a1T"] = a1
        d["avT"] = np.ascontiguousarray(inputs["L1_attn_Av"].T)
        d["aoT"] = np.ascontiguousarray(inputs["L1_attn_Ao"].T)
        d["bTall"] = np.ascontiguousarray(np.concatenate(
            [inputs["L1_attn_Bq"].T, inputs["L1_attn_Bk"].T,
             inputs["L1_attn_Bv"].T, inputs["L1_attn_Bo"].T], axis=1))
        d["rzrep"] = np.ascontiguousarray(np.broadcast_to(
            inputs["L1_attn_rezero"].reshape(1, 1), (128, 1))).astype(np.float32)
        d["wfT"] = np.ascontiguousarray(inputs["L1_ffn_weights"].T)
        d["fscale_fm"] = _fm(inputs["L1_ffn_scale"])
        d["fbias1_fm"] = _fm(inputs["L1_ffn_bias"])
        bvrep[1] = np.broadcast_to(inputs["L1_attn_bv"], (128, D))
    d["bvrep"] = bvrep
    return d


def kernel(**inputs):
    shared0 = _host_inputs(inputs, (0, 1) if MODE == "fused" else (0,))
    x = np.ascontiguousarray(inputs["L0_attn_x"], dtype=np.float32)

    if MODE == "fused":
        nc = build_program((0, 1))
        in_maps = []
        for c in range(NCORES):
            m = dict(shared0)
            m["x"] = np.ascontiguousarray(
                x[c * NSEQ:(c + 1) * NSEQ].transpose(0, 2, 1))
            in_maps.append(m)
        res = run_bass_kernel_spmd(nc, in_maps, core_ids=list(range(NCORES)))
        outs = [r["out"].transpose(0, 2, 1) for r in res.results]
        return np.ascontiguousarray(np.concatenate(outs, axis=0), dtype=np.float32)

    nc0 = build_program((0,))
    in_maps = []
    for c in range(NCORES):
        m = dict(shared0)
        m["x"] = np.ascontiguousarray(x[c * NSEQ:(c + 1) * NSEQ].transpose(0, 2, 1))
        in_maps.append(m)
    res0 = run_bass_kernel_spmd(nc0, in_maps, core_ids=list(range(NCORES)))
    xmids = [r["xmid"] for r in res0.results]

    shared1 = _host_inputs(inputs, (1,))
    nc1 = build_program((1,))
    in_maps = []
    for c in range(NCORES):
        m = dict(shared1)
        m["xmid"] = np.ascontiguousarray(xmids[c])
        in_maps.append(m)
    res1 = run_bass_kernel_spmd(nc1, in_maps, core_ids=list(range(NCORES)))
    outs = [r["out"].transpose(0, 2, 1) for r in res1.results]
    return np.ascontiguousarray(np.concatenate(outs, axis=0), dtype=np.float32)
